# version 05
# baseline (speedup 1.0000x reference)
import math
from contextlib import ExitStack

import numpy as np

import concourse.bass as bass
import concourse.mybir as mybir
from concourse.bass_utils import run_bass_kernel_spmd

F32 = mybir.dt.float32
BF16 = mybir.dt.bfloat16
AF = mybir.ActivationFunctionType
ALU = mybir.AluOpType
AX = mybir.AxisListType

P = 128
DM = 1024

_TWO_PI = 2.0 * math.pi
CW_C1 = 6.28125
CW_C2 = float(int((_TWO_PI - CW_C1) * 2 ** 21) / 2 ** 21)
CW_C3 = float(np.float32(_TWO_PI - CW_C1 - CW_C2))
INV_2PI = float(np.float32(1.0 / _TWO_PI))
EPS = 1e-5


def _stair(vec, t, g, nbits):
    for k in range(nbits - 1, -1, -1):
        h = float(2 ** k)
        vec.tensor_single_scalar(g, t, h - 0.5, ALU.is_ge)
        vec.scalar_tensor_tensor(t, g, -h, t, ALU.mult, ALU.add)


def _ev(ap2d):
    return ap2d.rearrange("p (c two) -> p c two", two=2)[:, :, 0:1]


def _od(ap2d):
    return ap2d.rearrange("p (c two) -> p c two", two=2)[:, :, 1:2]


def build_nc(n_blk=32, phases=("A", "B", "C", "D"), dbg=False):
    nc = bass.Bass("TRN2", target_bir_lowering=False, detect_race_conditions=False)
    NT = n_blk * P
    NB = n_blk

    x_in = nc.dram_tensor("x", [NT, DM], F32, kind="ExternalInput").ap()
    y_out = nc.dram_tensor("y", [NT, DM], F32, kind="ExternalOutput").ap()
    wq_d = nc.dram_tensor("wq", [DM, DM], F32, kind="ExternalInput").ap()
    wk_d = nc.dram_tensor("wk", [DM, DM], F32, kind="ExternalInput").ap()
    wv_d = nc.dram_tensor("wv", [DM, 2048], F32, kind="ExternalInput").ap()
    wg_d = nc.dram_tensor("wg", [DM, 2048], F32, kind="ExternalInput").ap()
    wo_d = nc.dram_tensor("wo", [2048, DM], F32, kind="ExternalInput").ap()
    wgate_d = nc.dram_tensor("wgate", [DM, 4096], F32, kind="ExternalInput").ap()
    wup_d = nc.dram_tensor("wup", [DM, 4096], F32, kind="ExternalInput").ap()
    wdown_d = nc.dram_tensor("wdown", [4096, DM], F32, kind="ExternalInput").ap()
    wqkv_d = nc.dram_tensor("wqkv", [DM, 3072], F32, kind="ExternalInput").ap()
    bqkv_d = nc.dram_tensor("bqkv", [1, 3072], F32, kind="ExternalInput").ap()
    wg1_d = nc.dram_tensor("wg1", [DM, 64], F32, kind="ExternalInput").ap()
    bg1_d = nc.dram_tensor("bg1", [1, 64], F32, kind="ExternalInput").ap()
    wg2_d = nc.dram_tensor("wg2", [64, DM], F32, kind="ExternalInput").ap()
    bg2_d = nc.dram_tensor("bg2", [1, DM], F32, kind="ExternalInput").ap()
    wout_d = nc.dram_tensor("wout", [DM, DM], F32, kind="ExternalInput").ap()
    bout_d = nc.dram_tensor("bout", [1, DM], F32, kind="ExternalInput").ap()
    lng_d = nc.dram_tensor("lng", [1, DM], F32, kind="ExternalInput").ap()
    lnb_d = nc.dram_tensor("lnb", [1, DM], F32, kind="ExternalInput").ap()
    ipos1_d = nc.dram_tensor("ipos1", [P, NB], F32, kind="ExternalInput").ap()
    posidx_d = nc.dram_tensor("posidx", [P, NB], F32, kind="ExternalInput").ap()
    jrev_d = nc.dram_tensor("jrev", [P, 1], F32, kind="ExternalInput").ap()
    ip1_d = nc.dram_tensor("ip1", [P, 1], F32, kind="ExternalInput").ap()
    pairidx_d = nc.dram_tensor("pairidx", [1, P], F32, kind="ExternalInput").ap()
    diffT_d = nc.dram_tensor("diffT", [P, P], F32, kind="ExternalInput").ap()
    causT_d = nc.dram_tensor("causT", [P, P], F32, kind="ExternalInput").ap()
    idf_d = nc.dram_tensor("idf", [P, P], F32, kind="ExternalInput").ap()
    xA_d = nc.dram_tensor("xA", [NT, DM], F32, kind="Internal").ap()
    xB_d = nc.dram_tensor("xB", [NT, DM], F32, kind="Internal").ap()
    xC_d = nc.dram_tensor("xC", [NT, DM], F32, kind="Internal").ap()
    sin_d = nc.dram_tensor("sin_t", [NT, P], BF16, kind="Internal").ap()
    cos_d = nc.dram_tensor("cos_t", [NT, P], BF16, kind="Internal").ap()
    hq_d = nc.dram_tensor("hq", [NB * 4 * P, 1024], BF16, kind="Internal").ap()
    tgate_d = nc.dram_tensor("tgate", [DM, 4096], BF16, kind="Internal").ap()
    tup_d = nc.dram_tensor("tup", [DM, 4096], BF16, kind="Internal").ap()
    tdown_d = nc.dram_tensor("tdown", [4096, DM], BF16, kind="Internal").ap()

    io = {}
    src = x_in
    scratch_for = {"A": xA_d, "B": xB_d, "C": xC_d, "D": None}
    for ph in phases:
        dst = y_out if ph == phases[-1] else scratch_for[ph]
        io[ph] = (src, dst)
        src = dst

    has_ffn = ("B" in phases) or ("D" in phases)
    hasA = "A" in phases

    top = ExitStack()
    with top:
        sbuf = lambda name, shape, dt=F32: top.enter_context(nc.sbuf_tensor(name, shape, dt))
        id_f = sbuf("id_f", [P, P])
        id_b = sbuf("id_b", [P, P], BF16)
        onesb = sbuf("onesb", [1, P], BF16)
        onesf = sbuf("onesf", [1, P])
        onescol = sbuf("onescol", [P, 1])
        dcol8 = sbuf("dcol8", [P, 8])
        ccol8 = sbuf("ccol8", [P, 8])
        l128c8 = sbuf("l128c8", [P, 8])
        rsR = sbuf("rsR", [P, 8 * NB])
        D8 = sbuf("D8", [P, 8 * P])
        dhc = sbuf("dhc", [P, 8])
        rs128c = sbuf("rs128c", [P, 1])
        wscol = sbuf("wscol", [P, 8])
        nmu_c = sbuf("nmu_c", [P, NB])
        rstdc = sbuf("rstdc", [P, NB])
        shc = sbuf("shc", [P, NB])
        colp = sbuf("colp", [P, 20])
        epscol = sbuf("epscol", [P, 1])
        monecol = sbuf("monecol", [P, 1])
        negepscol = sbuf("negepscol", [P, 1])
        nthrcol = sbuf("nthrcol", [P, 1])
        prow = sbuf("prow", [1, 336])
        psb = [top.enter_context(nc.psum_tensor(f"pb{i}", [P, 512], F32)) for i in range(8)]

        hIN = top.enter_context(nc.semaphore("sIN"))
        hOUT = top.enter_context(nc.semaphore("sOUT"))
        hT = top.enter_context(nc.semaphore("sT"))
        hV = top.enter_context(nc.semaphore("sV"))
        hS = top.enter_context(nc.semaphore("sS"))
        hQs = top.enter_context(nc.semaphore("sQs"))
        hQv = top.enter_context(nc.semaphore("sQv"))
        SEMO = {"T": hT, "V": hV, "S": hS}

        block = top.enter_context(nc.Block())

        aw = awo = Ssb = Sbf = None
        actx = ExitStack()
        asb = lambda name, shape, dt=F32: actx.enter_context(nc.sbuf_tensor(name, shape, dt))
        if hasA:
            aw = asb("aw", [P, 8 * 6144], BF16)
            awo = asb("awo", [P, 16 * DM], BF16)
            Ssb = asb("Ssb", [P, 8 * 257])
            Sbf = asb("Sbf", [P, 8 * 257], BF16)
        stg_ctx = ExitStack()
        psbuf = lambda name, shape, dt=F32: stg_ctx.enter_context(nc.sbuf_tensor(name, shape, dt))
        ipos1 = psbuf("ipos1_s", [P, NB])
        posidx = psbuf("posidx_s", [P, NB])
        jrev = psbuf("jrev_s", [P, 1])
        ip1 = psbuf("ip1_s", [P, 1])
        pairidx = psbuf("pairidx_s", [1, P])
        diffT = psbuf("diffT_s", [P, P])
        causT = psbuf("causT_s", [P, P])
        scg = [psbuf(f"scg{i}", [P, P], BF16) for i in range(2)]
        tga = psbuf("tga", [P, P])
        tgb = psbuf("tgb", [P, P])
        tgc = psbuf("tgc", [P, P])
        targ = psbuf("targ", [P, P])
        tsc1 = psbuf("tsc1", [P, P])
        tsc2 = psbuf("tsc2", [P, max(P, NB)])
        tsc3 = psbuf("tsc3", [P, max(P, NB)])
        angb = psbuf("angb", [P, P])
        lamr = psbuf("lamr", [1, 8])
        wred = psbuf("wred", [P, 4])
        stgA = psbuf("stgA", [P, 6144])
        stgB = psbuf("stgB", [P, 6144])

        nWS = 48 if has_ffn else 0
        nQK = 8 if hasA else 0
        nWO = 16 if hasA else 0

        steps = []

        def st(eng):
            def deco(fn):
                steps.append((eng, fn))
                return None
            return deco

        @st("V")
        def _(v):
            v.memset(onesb[:], 1.0)
            v.memset(onesf[:], 1.0)
            v.memset(onescol[:], 1.0)
            v.memset(prow[0:1, 300:301], 63.0)
            v.memset(prow[0:1, 302:303], 10000.0)
            v.memset(prow[0:1, 305:306], 128.0)
            v.memset(epscol[:], EPS)
            v.memset(monecol[:], -1.0)
            v.memset(negepscol[:], -EPS)
            v.memset(nthrcol[:], -127.5)
            v.tensor_copy(out=id_b[:], in_=id_f[:])
            v.reciprocal(prow[0:1, 301:302], prow[0:1, 300:301])
            v.tensor_scalar_mul(colp[:, 0:1], onescol[:], 128.0)
            for h in range(8):
                v.memset(lamr[0:1, h:h + 1], 1.0 - 2.0 ** (-(5 + h)))
            return v.tensor_scalar_mul(prow[0:1, 0:128], pairidx[:], prow[0:1, 301:302])

        @st("S")
        def _(s):
            s.activation(out=prow[0:1, 303:304], in_=prow[0:1, 302:303], func=AF.Ln)
            s.activation(out=prow[0:1, 304:305], in_=prow[0:1, 303:304], func=AF.Identity, scale=-1.0)
            s.activation(out=prow[0:1, 128:256], in_=prow[0:1, 0:128], func=AF.Exp, scale=prow[0:1, 304:305])
            s.activation(out=prow[0:1, 256:264], in_=lamr[0:1, :], func=AF.Ln)
            return s.activation(out=prow[0:1, 306:307], in_=prow[0:1, 305:306], func=AF.Sqrt)

        @st("V")
        def _(v):
            return v.reciprocal(prow[0:1, 307:308], prow[0:1, 306:307])

        @st("T")
        def _(t):
            t.matmul(psb[0][:, 0:P], onesf[:], prow[0:1, 128:256], start=True, stop=True)
            t.matmul(psb[1][:, 0:8], onesf[:], prow[0:1, 256:264], start=True, stop=True)
            return t.matmul(psb[2][:, 0:1], onesf[:], prow[0:1, 307:308], start=True, stop=True)

        @st("V")
        def _(v):
            v.tensor_copy(out=angb[:], in_=psb[0][:, 0:P])
            v.tensor_copy(out=dhc[:], in_=psb[1][:, 0:8])
            return v.tensor_copy(out=rs128c[:], in_=psb[2][:, 0:1])

        if has_ffn:
            @st("T")
            def _(t):
                return t.matmul(psb[3][0:1, 0:4], onescol[:], wred[:, 0:4], start=True, stop=True)

            @st("V")
            def _(v):
                v.tensor_copy(out=prow[0:1, 264:268], in_=psb[3][0:1, 0:4])
                v.tensor_copy(out=prow[0:1, 310:311], in_=onescol[0:1, 0:1])
                v.tensor_scalar_mul(prow[0:1, 272:276], prow[0:1, 264:268], 1.0 / 4194304.0)
                v.tensor_copy(out=prow[0:1, 311:312], in_=onescol[0:1, 0:1])
                v.reciprocal(prow[0:1, 276:280], prow[0:1, 272:276])
                v.tensor_copy(out=prow[0:1, 312:313], in_=onescol[0:1, 0:1])
                return v.tensor_copy(out=prow[0:1, 280:284], in_=prow[0:1, 276:280])

            @st("T")
            def _(t):
                return t.matmul(psb[4][:, 0:8], onesf[:], prow[0:1, 272:280], start=True, stop=True)

            @st("V")
            def _(v):
                return v.tensor_copy(out=wscol[:], in_=psb[4][:, 0:8])

        for h in range(8):
            @st("S")
            def _(s, h=h):
                s.activation(out=tsc1[:, 0:P], in_=diffT[:], func=AF.Exp, scale=dhc[:, h:h + 1])
                s.activation(out=dcol8[:, h:h + 1], in_=jrev[:], func=AF.Exp, scale=dhc[:, h:h + 1])
                s.activation(out=ccol8[:, h:h + 1], in_=ip1[:], func=AF.Exp, scale=dhc[:, h:h + 1])
                s.activation(out=l128c8[:, h:h + 1], in_=colp[:, 0:1], func=AF.Exp, scale=dhc[:, h:h + 1])
                s.activation(out=colp[:, 1:2], in_=onescol[:], func=AF.Exp, scale=dhc[:, h:h + 1])
                return s.activation(out=tsc2[:, 0:NB], in_=ipos1[:], func=AF.Exp, scale=dhc[:, h:h + 1])

            @st("V")
            def _(v, h=h):
                v.tensor_tensor(D8[:, h * P:(h + 1) * P], tsc1[:, 0:P], causT[:], ALU.mult)
                v.tensor_scalar(colp[:, 2:3], colp[:, 1:2], -1.0, 1.0, ALU.mult, ALU.add)
                v.tensor_scalar(tsc3[:, 0:NB], tsc2[:, 0:NB], -1.0, 1.0, ALU.mult, ALU.add)
                v.reciprocal(colp[:, 3:4], colp[:, 2:3])
                v.tensor_copy(out=colp[:, 4:5], in_=onescol[:])
                return v.tensor_scalar_mul(tsc3[:, 0:NB], tsc3[:, 0:NB], colp[:, 3:4])

            @st("S")
            def _(s, h=h):
                return s.activation(out=tsc2[:, 0:NB], in_=tsc3[:, 0:NB], func=AF.Sqrt)

            @st("V")
            def _(v, h=h):
                return v.reciprocal(rsR[:, h * NB:(h + 1) * NB], tsc2[:, 0:NB])

        if hasA:
            @st("V")
            def _(v):
                v.memset(Ssb[:], 0.0)
                return v.memset(Sbf[:], 0.0)

        cT = sum(1 for e, _ in steps if e == "T")
        cV = sum(1 for e, _ in steps if e == "V")
        cS = sum(1 for e, _ in steps if e == "S")

        IN_TAB = 8 * 16
        IN_WS = lambda i: IN_TAB + 16 * (i + 1)
        IN_QK = lambda f: IN_TAB + 16 * nWS + 64 * (f + 1)
        IN_WO = lambda v: IN_TAB + 16 * nWS + 64 * nQK + 16 * (v + 1)
        IN_PRE_END = IN_TAB + 16 * nWS + 64 * nQK + 16 * nWO

        V_WS = lambda i: i + 1
        V_STEP_BASE = nWS
        V_QKV = lambda f: nWS + cV + f + 1
        V_WO = lambda v2: nWS + cV + nQK + v2 + 1
        V_SC = lambda b, k: nWS + cV + nQK + nWO + 3 * b + k
        V_PRE_END = nWS + cV + nQK + nWO + 3 * NB + 1
        S_SC = lambda b, k: cS + 2 * b + k
        S_PRE_END = cS + 2 * NB
        T_PRE_END = cT
        OUT_SC = lambda b: 32 * (b + 1)
        OUT_PRE_END = 32 * NB

        def emit_steps(eng_name, eng, base):
            cnt_local = {"T": 0, "V": 0, "S": 0}
            for i, (en, fn) in enumerate(steps):
                if en == eng_name:
                    if i > 0:
                        pe = steps[i - 1][0]
                        if pe != eng_name:
                            eng.wait_ge(SEMO[pe], base[pe] + cnt_local[pe])
                    last = fn(eng)
                    assert last is not None, f"step {i} ({en}) returned no instruction"
                    last.then_inc(SEMO[eng_name], 1)
                cnt_local[en] += 1

        STEP_BASE = {"T": 0, "V": V_STEP_BASE, "S": 0}

        @block.sync
        def _(sync):
            for d, s in ((idf_d, id_f), (ipos1_d, ipos1), (posidx_d, posidx),
                         (jrev_d, jrev), (ip1_d, ip1), (pairidx_d, pairidx),
                         (diffT_d, diffT), (causT_d, causT)):
                sync.dma_start(out=s[:], in_=d[:]).then_inc(hIN, 16)
            for i in range(nWS):
                buf = (stgA, stgB)[i % 2]
                if i >= 2:
                    sync.wait_ge(hV, V_WS(i - 2))
                if i < 8:
                    srcap, w = wgate_d[i * P:(i + 1) * P, :], 4096
                elif i < 16:
                    srcap, w = wup_d[(i - 8) * P:(i - 7) * P, :], 4096
                else:
                    srcap, w = wdown_d[(i - 16) * P:(i - 15) * P, :], 1024
                sync.dma_start(out=buf[:, 0:w], in_=srcap).then_inc(hIN, 16)
            for f in range(nQK):
                buf = (stgA, stgB)[f % 2]
                if f >= 2:
                    sync.wait_ge(hV, V_QKV(f - 2))
                elif nWS:
                    sync.wait_ge(hV, V_WS(nWS - 2 + f))
                sync.dma_start(out=buf[:, 0:1024], in_=wq_d[f * P:(f + 1) * P, :]).then_inc(hIN, 16)
                sync.dma_start(out=buf[:, 1024:2048], in_=wk_d[f * P:(f + 1) * P, :]).then_inc(hIN, 16)
                sync.dma_start(out=buf[:, 2048:4096], in_=wv_d[f * P:(f + 1) * P, :]).then_inc(hIN, 16)
                sync.dma_start(out=buf[:, 4096:6144], in_=wg_d[f * P:(f + 1) * P, :]).then_inc(hIN, 16)
            for v2 in range(nWO):
                buf = (stgA, stgB)[v2 % 2]
                if v2 >= 2:
                    sync.wait_ge(hV, V_WO(v2 - 2))
                else:
                    sync.wait_ge(hV, V_QKV(nQK - 2 + v2))
                sync.dma_start(out=buf[:, 0:1024], in_=wo_d[v2 * P:(v2 + 1) * P, :]).then_inc(hIN, 16)

        @block.vector
        def _(vector):
            vector.wait_ge(hIN, IN_TAB + (32 if nWS >= 2 else (128 if nQK else 0)))
            vector.memset(wred[:], 0.0)
            vector.memset(wred[:, 3:4], 1.0)
            for i in range(nWS):
                buf = (stgA, stgB)[i % 2]
                w = 4096 if i < 16 else 1024
                wi = 0 if i < 8 else (1 if i < 16 else 2)
                vector.wait_ge(hIN, IN_WS(i) + ((64 if nQK else 0) if i == nWS - 1 else 16))
                vector.tensor_reduce(colp[:, 5:6], buf[:, 0:w], AX.X, ALU.add, apply_absolute_value=True)
                vector.tensor_copy(out=colp[:, 6:7], in_=wred[:, 3:4])
                nc.vector.tensor_add(wred[:, wi:wi + 1], wred[:, wi:wi + 1], colp[:, 5:6]).then_inc(hV, 1)
            emit_steps("V", vector, STEP_BASE)
            for f in range(nQK):
                buf = (stgA, stgB)[f % 2]
                vector.wait_ge(hIN, IN_QK(f) + ((16 if nWO else 0) if f == nQK - 1 else 64))
                o = f * 6144
                vector.tensor_copy(out=aw[:, o:o + 1024], in_=buf[:, 0:1024])
                vector.tensor_scalar_mul(aw[:, o + 1024:o + 2048], buf[:, 1024:2048], rs128c[:, 0:1])
                vector.tensor_copy(out=aw[:, o + 2048:o + 4096], in_=buf[:, 2048:4096])
                nc.vector.tensor_copy(out=aw[:, o + 4096:o + 6144], in_=buf[:, 4096:6144]).then_inc(hV, 1)
            for v2 in range(nWO):
                buf = (stgA, stgB)[v2 % 2]
                vector.wait_ge(hIN, IN_WO(v2) + (0 if v2 == nWO - 1 else 16))
                nc.vector.tensor_copy(out=awo[:, v2 * 1024:(v2 + 1) * 1024], in_=buf[:, 0:1024]).then_inc(hV, 1)
            for b in range(NB):
                sb_s, sb_c = scg[0], scg[1]
                if b >= 1:
                    vector.wait_ge(hOUT, OUT_SC(b - 1))
                vector.tensor_scalar_mul(targ[:], angb[:], posidx[:, b:b + 1])
                vector.tensor_scalar_mul(tsc1[:, 0:P], targ[:], INV_2PI)
                vector.tensor_scalar_mul(tsc3[:, 0:P], targ[:], INV_2PI)
                _stair(vector, tsc1[:, 0:P], tsc2[:, 0:P], 10)
                vector.tensor_sub(tsc3[:, 0:P], tsc3[:, 0:P], tsc1[:, 0:P])
                vector.scalar_tensor_tensor(tsc2[:, 0:P], tsc3[:, 0:P], -CW_C1, targ[:], ALU.mult, ALU.add)
                vector.scalar_tensor_tensor(tsc2[:, 0:P], tsc3[:, 0:P], -CW_C2, tsc2[:, 0:P], ALU.mult, ALU.add)
                nc.vector.scalar_tensor_tensor(targ[:], tsc3[:, 0:P], -CW_C3, tsc2[:, 0:P], ALU.mult, ALU.add).then_inc(hV, 1)
                vector.wait_ge(hS, S_SC(b, 1))
                vector.tensor_tensor(tgb[:], tga[:], tga[:], ALU.mult)
                nc.vector.tensor_scalar(tgb[:], tgb[:], -1.0, 1.0, ALU.mult, ALU.add).then_inc(hV, 1)
                vector.wait_ge(hS, S_SC(b, 2))
                vector.scalar_tensor_tensor(tga[:], tga[:], 2.0, tgc[:], ALU.mult, ALU.mult)
                vector.tensor_scalar(tgb[:], tgb[:], 2.0, -1.0, ALU.mult, ALU.add)
                vector.scalar_tensor_tensor(sb_s[:], tga[:], 2.0, tgb[:], ALU.mult, ALU.mult)
                vector.tensor_tensor(tgc[:], tga[:], tga[:], ALU.mult)
                nc.vector.tensor_scalar(sb_c[:], tgc[:], -2.0, 1.0, ALU.mult, ALU.add).then_inc(hV, 1)
            nc.vector.engine_nop().then_inc(hV, 1)

        @block.scalar
        def _(scalar):
            emit_steps("S", scalar, STEP_BASE)
            for b in range(NB):
                scalar.wait_ge(hV, V_SC(b, 1))
                nc.scalar.activation(out=tga[:], in_=targ[:], func=AF.Sin, scale=0.25).then_inc(hS, 1)
                scalar.wait_ge(hV, V_SC(b, 2))
                nc.scalar.activation(out=tgc[:], in_=tgb[:], func=AF.Sqrt).then_inc(hS, 1)

        @block.tensor
        def _(tensor):
            emit_steps("T", tensor, STEP_BASE)

        @block.gpsimd
        def _(g):
            for b in range(NB):
                g.wait_ge(hV, V_SC(b, 3))
                sb_s, sb_c = scg[0], scg[1]
                g.dma_start(out=sin_d[b * P:(b + 1) * P, :], in_=sb_s[:]).then_inc(hOUT, 16)
                g.dma_start(out=cos_d[b * P:(b + 1) * P, :], in_=sb_c[:]).then_inc(hOUT, 16)

        BIN = IN_PRE_END
        BV = V_PRE_END
        BS = S_PRE_END
        BT = T_PRE_END
        BOUT = OUT_PRE_END

        V_CASTS_DONE = nWS + cV + nQK + nWO
        stg_ctx.close()
        if hasA:
            xsrc, xdst = io["A"]
            xblk = [asb("xblk0", [P, DM]), asb("xblk1", [P, DM])]
            sinrep = asb("sinrep", [P, DM], BF16)
            cosrep = asb("cosrep", [P, DM], BF16)
            xlnb = asb("xlnb", [P, DM])
            xlnT = asb("xlnT", [P, DM], BF16)
            qr_t = asb("qr_t", [P, DM])
            kr_t = asb("kr_t", [P, DM])
            krd = asb("krd", [P, DM], BF16)
            qrT = asb("qrT", [P, DM], BF16)
            krT = asb("krT", [P, DM], BF16)
            vr1 = asb("vr1", [P, 8 * 257], BF16)
            gsilu = asb("gsilu", [P, 2048], BF16)
            gated = asb("gated", [P, 2048])
            gatedT = asb("gatedT", [P, 2048], BF16)
            Mbf = asb("Mbf", [P, P], BF16)
            ocomb = asb("ocomb", [P, 257])
            osct = asb("osct", [P, 256])
            crossS = asb("crossS", [P, 257])
            dumpb = asb("dumpb", [P, 512], BF16)
            tmpe = asb("tmpe", [P, 256])
            tmpo = asb("tmpo", [P, 256])

            KIN, KOUT, KT, KV, KS = 48, 16, 22, 49, 28
            DBG_OUT = 22 * 16 if dbg else 0
            A_in = lambda b, k: BIN + KIN * b + 16 * k
            A_out = lambda b: BOUT + DBG_OUT + KOUT * (b + 1)
            A_t = lambda b, m: BT + KT * b + m
            A_v = lambda b, m: BV + KV * b + m
            V_oc = lambda h: 9 + 5 * h
            V_Su = lambda h: 10 + 5 * h
            V_inv = lambda h: 11 + 5 * h
            V_M = lambda h: 8 if h == 0 else 7 + 5 * h
            V_gate = lambda h: (13 + 5 * h) if h < 7 else 47
            A_s = lambda b, m: BS + KS * b + m
            S_cS = lambda h: 5 + 3 * h
            S_den = lambda h: 6 + 3 * h
            S_g2 = lambda h: 7 + 3 * h

            @block.sync
            def _(sync):
                sync.wait_ge(hV, V_PRE_END)
                sync.wait_ge(hOUT, OUT_PRE_END)
                for b in range(NB):
                    if b >= 2:
                        sync.wait_ge(hOUT, A_out(b - 2) + 16)
                    sync.dma_start(out=xblk[b % 2][:], in_=xsrc[b * P:(b + 1) * P, :]).then_inc(hIN, 16)
                    if b >= 1:
                        sync.wait_ge(hV, A_v(b - 1, 5))
                    sync.wait_ge(hOUT, OUT_SC(b))
                    sin_ap = bass.AP(sin_d.tensor, b * P * P, [[P, P], [0, 8], [1, P]])
                    cos_ap = bass.AP(cos_d.tensor, b * P * P, [[P, P], [0, 8], [1, P]])
                    srep = sinrep[:].rearrange("p (r c) -> p r c", r=8)
                    crep = cosrep[:].rearrange("p (r c) -> p r c", r=8)
                    sync.dma_start(out=srep, in_=sin_ap).then_inc(hIN, 16)
                    sync.dma_start(out=crep, in_=cos_ap).then_inc(hIN, 16)

            if dbg:
                dbg_specs = [
                    ("d_xlnb", xlnb, ("S", 3)), ("d_xlnT", xlnT, ("V", 4)),
                    ("d_qr", qr_t, ("V", 5)), ("d_kr", kr_t, ("V", 5)), ("d_krd", krd, ("V", 5)),
                    ("d_qrT", qrT, ("V", 6)), ("d_krT", krT, ("V", 6)),
                    ("d_vr1", vr1, ("V", 7)), ("d_gsilu", gsilu, ("S", 4)),
                    ("d_Mbf", Mbf, ("V", 8)), ("d_gated", gated, ("V", 48)),
                    ("d_Ssb", Ssb, ("V", 48)), ("d_colp", colp, ("V", 48)),
                    ("d_sinrep", sinrep, ("IN", 3)), ("d_cosrep", cosrep, ("IN", 3)),
                    ("d_rsR", rsR, ("V", 3)), ("d_ccol", ccol8, ("V", 3)),
                    ("d_dcol", dcol8, ("V", 3)), ("d_l128", l128c8, ("V", 3)),
                    ("d_D8", D8, ("V", 3)), ("d_ocomb", ocomb, ("V", 48)),
                    ("d_osct", osct, ("V", 48)), ("d_crossS", crossS, ("V", 48)),
                ]
                dbg_outs = {}
                for nm, tile, _w in dbg_specs:
                    shape = list(tile[:].shape)
                    dt = F32 if nm not in ("d_xlnb", "d_xlnT", "d_qr2", "d_krd", "d_qrT", "d_krT",
                                           "d_vr1", "d_gsilu", "d_Mbf", "d_sinrep", "d_cosrep") else F32
                    dtt = tile[:].dtype
                    dbg_outs[nm] = nc.dram_tensor(nm, shape, dtt, kind="ExternalOutput").ap()

            @block.gpsimd
            def _(g):
                if dbg:
                    for nm, tile, (sem, val) in dbg_specs:
                        hh = {"S": hS, "V": hV, "T": hT, "IN": hIN}[sem]
                        base = {"S": A_s, "V": A_v, "T": A_t, "IN": A_in}[sem]
                        g.wait_ge(hh, base(0, val))
                        g.dma_start(out=dbg_outs[nm], in_=tile[:]).then_inc(hOUT, 16)
                for b in range(NB):
                    g.wait_ge(hV, A_v(b, 49))
                    g.dma_start(out=xdst[b * P:(b + 1) * P, :], in_=xblk[b % 2][:]).then_inc(hOUT, 16)

            @block.tensor
            def _(tensor):
                for b in range(NB):
                    tensor.wait_ge(hS, A_s(b, 3))
                    if b >= 1:
                        tensor.wait_ge(hS, A_s(b - 1, 4))
                    for f in range(8):
                        mm = tensor.transpose(psb[f // 4][:, (f % 4) * P:(f % 4 + 1) * P],
                                              xlnb[:, f * P:(f + 1) * P], id_f[:])
                    mm.then_inc(hT, 1)
                    tensor.wait_ge(hV, A_v(b, 4))
                    for f in range(8):
                        st_, sp = (f == 0), (f == 7)
                        o = f * 6144
                        tensor.matmul(psb[2][:, :], xlnT[:, f * P:(f + 1) * P], aw[:, o:o + 512], start=st_, stop=sp)
                        tensor.matmul(psb[3][:, :], xlnT[:, f * P:(f + 1) * P], aw[:, o + 512:o + 1024], start=st_, stop=sp)
                        tensor.matmul(psb[4][:, :], xlnT[:, f * P:(f + 1) * P], aw[:, o + 1024:o + 1536], start=st_, stop=sp)
                        mm = tensor.matmul(psb[5][:, :], xlnT[:, f * P:(f + 1) * P], aw[:, o + 1536:o + 2048], start=st_, stop=sp)
                    mm.then_inc(hT, 1)
                    tensor.wait_ge(hV, A_v(b, 5))
                    for f in range(8):
                        tensor.transpose(psb[f // 4][:, (f % 4) * P:(f % 4 + 1) * P],
                                         qr_t[:, f * P:(f + 1) * P], id_f[:])
                    for f in range(8):
                        mm = tensor.transpose(psb[6 + f // 4][:, (f % 4) * P:(f % 4 + 1) * P],
                                              kr_t[:, f * P:(f + 1) * P], id_f[:])
                    mm.then_inc(hT, 1)
                    tensor.wait_ge(hV, A_v(b, 6))
                    for f in range(8):
                        st_, sp = (f == 0), (f == 7)
                        o = f * 6144 + 2048
                        for j in range(4):
                            tensor.matmul(psb[2 + j][:, :], xlnT[:, f * P:(f + 1) * P],
                                          aw[:, o + j * 512:o + (j + 1) * 512], start=st_, stop=sp)
                        o = f * 6144 + 4096
                        for j, pbix in enumerate((0, 1, 6, 7)):
                            mm = tensor.matmul(psb[pbix][:, :], xlnT[:, f * P:(f + 1) * P],
                                               aw[:, o + j * 512:o + (j + 1) * 512], start=st_, stop=sp)
                    mm.then_inc(hT, 1)
                    tensor.wait_ge(hV, A_v(b, 7))
                    def hbanks(h):
                        return (psb[2], psb[3], psb[4], psb[5]) if h % 2 == 0 else (psb[0], psb[1], psb[6], psb[7])
                    pA, pC, pAV, pU = hbanks(0)
                    tensor.matmul(pA[:, 0:P], krT[:, 0:P], qrT[:, 0:P], start=True, stop=True).then_inc(hT, 1)
                    for h in range(8):
                        pA, pC, pAV, pU = hbanks(h)
                        tensor.wait_ge(hV, A_v(b, V_M(h)))
                        sl = slice(h * 257, (h + 1) * 257)
                        tensor.matmul(pC[:, 0:257], qrT[:, h * P:(h + 1) * P], Sbf[:, sl], start=True, stop=True)
                        tensor.matmul(pAV[:, 0:257], Mbf[:], vr1[:, sl], start=True, stop=True)
                        mm = tensor.matmul(pU[:, 0:257], krd[:, h * P:(h + 1) * P], vr1[:, sl], start=True, stop=True)
                        mm.then_inc(hT, 1)
                        if h < 7:
                            if h == 0:
                                tensor.wait_ge(hS, A_s(b, 4))
                            else:
                                tensor.wait_ge(hV, A_v(b, V_Su(h - 1)))
                                tensor.wait_ge(hS, A_s(b, S_cS(h - 1)))
                            pA2 = hbanks(h + 1)[0]
                            tensor.matmul(pA2[:, 0:P], krT[:, (h + 1) * P:(h + 2) * P],
                                          qrT[:, (h + 1) * P:(h + 2) * P], start=True, stop=True).then_inc(hT, 1)
                    tensor.wait_ge(hV, A_v(b, V_gate(7)))
                    tensor.wait_ge(hS, A_s(b, S_cS(6)))
                    for j in range(16):
                        mm = tensor.transpose(psb[2 + j // 4][:, (j % 4) * P:(j % 4 + 1) * P],
                                              gated[:, j * P:(j + 1) * P], id_f[:])
                    mm.then_inc(hT, 1)
                    tensor.wait_ge(hV, A_v(b, 48))
                    tensor.wait_ge(hS, A_s(b, S_cS(7)))
                    for c in range(16):
                        st_, sp = (c == 0), (c == 15)
                        tensor.matmul(psb[6][:, :], gatedT[:, c * P:(c + 1) * P],
                                      awo[:, c * 1024:c * 1024 + 512], start=st_, stop=sp)
                        mm = tensor.matmul(psb[7][:, :], gatedT[:, c * P:(c + 1) * P],
                                           awo[:, c * 1024 + 512:(c + 1) * 1024], start=st_, stop=sp)
                    mm.then_inc(hT, 1)

            @block.vector
            def _(vector):
                for h in range(8):
                    vector.memset(vr1[:, h * 257 + 256:h * 257 + 257], 1.0)
                for b in range(NB):
                    xb = xblk[b % 2]
                    vector.wait_ge(hIN, A_in(b, 1) + (48 if b == 0 and NB > 1 else 32))
                    nc.vector.tensor_reduce(colp[:, 4:5], xb[:], AX.X, ALU.add).then_inc(hV, 1)
                    vector.wait_ge(hS, A_s(b, 1))
                    vector.scalar_tensor_tensor(colp[:, 7:8], colp[:, 8:9], 1.0 / 1024.0,
                                                colp[:, 6:7], ALU.mult, ALU.subtract)
                    if b >= 1:
                        vector.wait_ge(hT, A_t(b - 1, 1))
                    nc.vector.tensor_scalar_sub(xlnb[:], xb[:], colp[:, 5:6]).then_inc(hV, 1)
                    vector.wait_ge(hS, A_s(b, 2))
                    nc.vector.reciprocal(colp[:, 10:11], colp[:, 9:10]).then_inc(hV, 1)
                    vector.wait_ge(hT, A_t(b, 1))
                    vector.tensor_copy(out=xlnT[:, 0:512], in_=psb[0][:, :])
                    nc.vector.tensor_copy(out=xlnT[:, 512:1024], in_=psb[1][:, :]).then_inc(hV, 1)
                    vector.wait_ge(hT, A_t(b, 2))
                    vector.wait_ge(hIN, A_in(b, 3) + (16 if b == 0 and NB > 1 else 0))
                    for (pq, dst) in ((2, qr_t), (4, kr_t)):
                        for j in range(2):
                            pb = psb[pq + j]
                            c0 = j * 512
                            pe, po = _ev(pb[:, :]), _od(pb[:, :])
                            ce, co = _ev(cosrep[:, c0:c0 + 512]), _od(cosrep[:, c0:c0 + 512])
                            se, so = _ev(sinrep[:, c0:c0 + 512]), _od(sinrep[:, c0:c0 + 512])
                            vector.tensor_tensor(tmpe[:], pe, ce, ALU.mult)
                            vector.tensor_tensor(tmpo[:], po, se, ALU.mult)
                            vector.tensor_tensor(osct[:, 0:256], po, co, ALU.mult)
                            vector.tensor_tensor(crossS[:, 0:256], pe, so, ALU.mult)
                            vector.tensor_tensor(_ev(dst[:, c0:c0 + 512]), tmpe[:], tmpo[:], ALU.subtract)
                            vector.tensor_tensor(_od(dst[:, c0:c0 + 512]), osct[:, 0:256], crossS[:, 0:256], ALU.add)
                    for h in range(8):
                        mm = vector.tensor_scalar_mul(krd[:, h * P:(h + 1) * P],
                                                      kr_t[:, h * P:(h + 1) * P], dcol8[:, h:h + 1])
                    mm.then_inc(hV, 1)
                    vector.wait_ge(hT, A_t(b, 3))
                    vector.tensor_copy(out=qrT[:, 0:512], in_=psb[0][:, :])
                    vector.tensor_copy(out=qrT[:, 512:1024], in_=psb[1][:, :])
                    vector.tensor_copy(out=krT[:, 0:512], in_=psb[6][:, :])
                    nc.vector.tensor_copy(out=krT[:, 512:1024], in_=psb[7][:, :]).then_inc(hV, 1)
                    vector.wait_ge(hT, A_t(b, 4))
                    for j in range(4):
                        dst = vr1[:].rearrange("p (h c) -> p h c", c=257)[:, 2 * j:2 * j + 2, 0:256]
                        srcp = psb[2 + j][:, :].rearrange("p (h c) -> p h c", c=256)
                        mm = vector.tensor_copy(out=dst, in_=srcp)
                    mm.then_inc(hV, 1)
                    vector.wait_ge(hT, A_t(b, 5))
                    nc.vector.tensor_tensor(Mbf[:], psb[2][:, 0:P], D8[:, 0:P], ALU.mult).then_inc(hV, 1)
                    for h in range(8):
                        pA, pC, pAV, pU = ((psb[2], psb[3], psb[4], psb[5]) if h % 2 == 0
                                           else (psb[0], psb[1], psb[6], psb[7]))
                        sl = slice(h * 257, (h + 1) * 257)
                        vector.wait_ge(hT, A_t(b, 6 + 2 * h))
                        vector.wait_ge(hS, A_s(b, S_cS(h)))
                        nc.vector.tensor_add(ocomb[:], pAV[:, 0:257], crossS[:]).then_inc(hV, 1)
                        vector.wait_ge(hS, A_s(b, S_den(h)))
                        vector.reciprocal(colp[:, 18:19], colp[:, 17:18])
                        vector.scalar_tensor_tensor(Ssb[:, sl], Ssb[:, sl], l128c8[:, h:h + 1],
                                                    pU[:, 0:257], ALU.mult, ALU.add)
                        nc.vector.tensor_copy(out=Sbf[:, sl], in_=Ssb[:, sl]).then_inc(hV, 1)
                        nc.vector.tensor_tensor(colp[:, 12:13], colp[:, 18:19],
                                                rsR[:, h * NB + b:h * NB + b + 1], ALU.mult).then_inc(hV, 1)
                        vector.wait_ge(hS, A_s(b, S_g2(h)))
                        vector.reciprocal(colp[:, 15:16], colp[:, 14:15])
                        if h < 7:
                            vector.wait_ge(hT, A_t(b, 7 + 2 * h))
                            pA2 = (psb[2] if (h + 1) % 2 == 0 else psb[0])
                            nc.vector.tensor_tensor(Mbf[:], pA2[:, 0:P],
                                                    D8[:, (h + 1) * P:(h + 2) * P], ALU.mult).then_inc(hV, 1)
                        else:
                            vector.tensor_copy(out=colp[:, 19:20], in_=onescol[:])
                        nc.vector.scalar_tensor_tensor(gated[:, h * 256:(h + 1) * 256], osct[:],
                                                       colp[:, 15:16], gsilu[:, h * 256:(h + 1) * 256],
                                                       ALU.mult, ALU.mult).then_inc(hV, 1)
                    vector.wait_ge(hT, A_t(b, 21))
                    for j in range(4):
                        mm = vector.tensor_copy(out=gatedT[:, j * 512:(j + 1) * 512], in_=psb[2 + j][:, :])
                    mm.then_inc(hV, 1)
                    vector.wait_ge(hT, A_t(b, 22))
                    vector.tensor_add(xb[:, 0:512], psb[6][:, :], xb[:, 0:512])
                    nc.vector.tensor_add(xb[:, 512:1024], psb[7][:, :], xb[:, 512:1024]).then_inc(hV, 1)

            @block.scalar
            def _(scalar):
                for b in range(NB):
                    xb = xblk[b % 2]
                    scalar.wait_ge(hIN, A_in(b, 1) + (48 if b == 0 and NB > 1 else 32))
                    if b >= 1:
                        scalar.wait_ge(hV, A_v(b - 1, 2))
                    nc.scalar.activation(out=dumpb[:], in_=xb[:, 0:512], func=AF.Square,
                                         accum_out=colp[:, 2:3])
                    nc.scalar.activation(out=dumpb[:], in_=xb[:, 512:1024], func=AF.Square,
                                         accum_out=colp[:, 8:9])
                    nc.scalar.activation(out=colp[:, 1:2], in_=onescol[:], func=AF.Identity)
                    nc.scalar.activation(out=colp[:, 8:9], in_=colp[:, 8:9], func=AF.Identity,
                                         bias=colp[:, 2:3])
                    scalar.wait_ge(hV, A_v(b, 1))
                    nc.scalar.activation(out=colp[:, 5:6], in_=colp[:, 4:5], func=AF.Identity,
                                         scale=1.0 / 1024.0)
                    nc.scalar.activation(out=colp[:, 6:7], in_=colp[:, 5:6],
                                         func=AF.Square).then_inc(hS, 1)
                    scalar.wait_ge(hV, A_v(b, 2))
                    nc.scalar.activation(out=colp[:, 9:10], in_=colp[:, 7:8], func=AF.Sqrt,
                                         bias=epscol[:, 0:1]).then_inc(hS, 1)
                    scalar.wait_ge(hV, A_v(b, 3))
                    nc.scalar.activation(out=xlnb[:], in_=xlnb[:], func=AF.Identity,
                                         scale=colp[:, 10:11]).then_inc(hS, 1)
                    scalar.wait_ge(hT, A_t(b, 4))
                    for j, pbix in enumerate((0, 1, 6, 7)):
                        mm = nc.scalar.activation(out=gsilu[:, j * 512:(j + 1) * 512],
                                                  in_=psb[pbix][:, :], func=AF.Silu)
                    mm.then_inc(hS, 1)
                    for h in range(8):
                        pC = psb[3] if h % 2 == 0 else psb[1]
                        scalar.wait_ge(hT, A_t(b, 6 + 2 * h))
                        nc.scalar.activation(out=crossS[:], in_=pC[:, 0:257], func=AF.Identity,
                                             scale=ccol8[:, h:h + 1]).then_inc(hS, 1)
                        scalar.wait_ge(hV, A_v(b, V_oc(h)))
                        nc.scalar.activation(out=colp[:, 16:17], in_=ocomb[:, 256:257], func=AF.Identity,
                                             scale=rsR[:, h * NB + b:h * NB + b + 1])
                        nc.scalar.activation(out=colp[:, 16:17], in_=colp[:, 16:17], func=AF.Abs)
                        nc.scalar.activation(out=colp[:, 17:18], in_=colp[:, 16:17], func=AF.Relu,
                                             bias=monecol[:, 0:1])
                        nc.scalar.activation(out=colp[:, 17:18], in_=colp[:, 17:18], func=AF.Identity,
                                             bias=onescol[:, 0:1]).then_inc(hS, 1)
                        scalar.wait_ge(hV, A_v(b, V_inv(h)))
                        nc.scalar.activation(out=osct[:], in_=ocomb[:, 0:256], func=AF.Identity,
                                             scale=colp[:, 12:13])
                        nc.scalar.activation(out=dumpb[:, 0:256], in_=osct[:], func=AF.Square,
                                             accum_out=colp[:, 13:14])
                        nc.scalar.activation(out=colp[:, 14:15], in_=colp[:, 13:14], func=AF.Sqrt,
                                             scale=1.0 / 256.0, bias=epscol[:, 0:1]).then_inc(hS, 1)

            BIN += KIN * NB
            BOUT += DBG_OUT + KOUT * NB
            BT += KT * NB
            BV += KV * NB
            BS += KS * NB
        actx.close()

        qstair = [0]

        def emit_ffn(tag, xsrc, xdst, use_stash, dbg_ffn=False):
            nonlocal BIN, BOUT, BT, BV, BS
            fctx = ExitStack()
            fsb = lambda n_, s_, dt=F32: fctx.enter_context(nc.sbuf_tensor(f"{tag}{n_}", s_, dt))
            guw = fsb("guw", [P, 8 * 8192], BF16)
            stgc = [fsb("stg0", [P, 2048])]
            qa = fsb("qa", [P, 2048], BF16)
            qb = fsb("qb", [P, 2048], BF16)
            xblk = [fsb("xb0", [P, DM]), fsb("xb1", [P, DM])]
            xlnf = fsb("xlnf", [P, DM])
            xqs = fsb("xqs", [P, DM])
            xqf = fsb("xqf", [P, DM])
            xqT = fsb("xqT", [P, DM], BF16)
            gq2 = fsb("gq2", [P, 1024])
            hsv = fsb("hsv", [P, 1024])
            ht = fsb("ht", [P, 4096])
            rgt = fsb("rgt", [P, 512])
            sqt = fsb("sqt", [P, 512])
            sdump = fsb("sdump", [P, 512])
            hqb = fsb("hqb", [P, 1024], BF16)

            b_in, b_out, b_t, b_v, b_s = BIN, BOUT, BT, BV, BS
            nGU = 32
            PRE_IN = lambda i: b_in + 16 * (i + 1)
            PRE_V = lambda i: b_v + (i + 1)
            pV = b_v + (0 if use_stash else nGU)
            pOUT = b_out + (0 if use_stash else nGU * 16)
            pIN = b_in + nGU * 16
            KIN, KOUT, KT, KV, KS = 16, 64, 13, 22, 15
            F_in = lambda b: pIN + KIN * b + 16
            F_t = lambda b, m: b_t + KT * b + m
            F_v = lambda b, m: pV + KV * b + m
            F_s = lambda b, m: b_s + KS * b + m
            F_out = lambda b, t: pOUT + KOUT * b + 16 * (t + 1)
            QB = qstair[0]
            qstair[0] += 40 * NB

            @block.sync
            def _(sync):
                sync.wait_ge(hV, b_v)
                sync.wait_ge(hS, b_s)
                sync.wait_ge(hT, b_t)
                sync.wait_ge(hOUT, b_out)
                for i in range(nGU):
                    buf = stgc[0]
                    wd_ = wgate_d if i < 16 else wup_d
                    f, half = (i % 16) // 2, i % 2
                    row0 = ((i % 16) // 2) * P
                    col0 = (i % 2) * 2048
                    if use_stash:
                        td_ = tgate_d if i < 16 else tup_d
                        dst = guw[:, f * 8192 + (0 if i < 16 else 4096) + col0:
                                  f * 8192 + (0 if i < 16 else 4096) + col0 + 2048]
                        sync.dma_start(out=dst, in_=td_[row0:row0 + P, col0:col0 + 2048]).then_inc(hIN, 16)
                    else:
                        if i >= 1:
                            sync.wait_ge(hV, PRE_V(i - 1))
                        sync.dma_start(out=buf[:], in_=wd_[row0:row0 + P, col0:col0 + 2048]).then_inc(hIN, 16)
                for b in range(NB):
                    if b >= 2:
                        sync.wait_ge(hV, F_v(b - 2, 2))
                        sync.wait_ge(hS, F_s(b - 2, 5))
                    sync.dma_start(out=xblk[b % 2][:], in_=xsrc[b * P:(b + 1) * P, :]).then_inc(hIN, 16)

            @block.vector
            def _(vector):
                vector.wait_ge(hS, b_s)
                vector.wait_ge(hT, b_t)
                if not use_stash:
                    for i in range(nGU):
                        buf = stgc[0]
                        wi = 0 if i < 16 else 1
                        f = (i % 16) // 2
                        col0 = (i % 2) * 2048
                        dst = guw[:, f * 8192 + (0 if i < 16 else 4096) + col0:
                                  f * 8192 + (0 if i < 16 else 4096) + col0 + 2048]
                        vector.wait_ge(hIN, PRE_IN(i) + (16 * min(NB, 2) if i == nGU - 1 else 0))
                        vector.tensor_scalar(qb[:], buf[:], wscol[:, 4 + wi:5 + wi], -0.5, ALU.mult, ALU.is_le)
                        vector.tensor_scalar(qa[:], buf[:], wscol[:, 4 + wi:5 + wi], 0.5, ALU.mult, ALU.is_ge)
                        vector.tensor_copy(out=colp[:, 3:4], in_=onescol[:])
                        nc.vector.tensor_sub(dst, qa[:], qb[:]).then_inc(hV, 1)
                for b in range(NB):
                    xb = xblk[b % 2]
                    vector.wait_ge(hIN, F_in(b) + (0 if b == NB - 1 else 16))
                    nc.vector.tensor_reduce(colp[:, 4:5], xb[:], AX.X, ALU.add).then_inc(hV, 1)
                    vector.wait_ge(hS, F_s(b, 1))
                    vector.scalar_tensor_tensor(colp[:, 7:8], colp[:, 8:9], 1.0 / 1024.0,
                                                colp[:, 6:7], ALU.mult, ALU.subtract)
                    nc.vector.tensor_scalar_sub(xlnf[:], xb[:], colp[:, 5:6]).then_inc(hV, 1)
                    vector.wait_ge(hS, F_s(b, 2))
                    nc.vector.reciprocal(colp[:, 10:11], colp[:, 9:10]).then_inc(hV, 1)
                    vector.wait_ge(hS, F_s(b, 3))
                    nc.vector.tensor_reduce(colp[:, 4:5], xlnf[:], AX.X, ALU.max,
                                            apply_absolute_value=True).then_inc(hV, 1)
                    vector.wait_ge(hS, F_s(b, 4))
                    vector.tensor_scalar_mul(colp[:, 11:12], colp[:, 5:6], 1.0 / 127.0)
                    vector.reciprocal(colp[:, 6:7], colp[:, 5:6])
                    vector.tensor_tensor(colp[:, 12:13], colp[:, 11:12], wscol[:, 0:1], ALU.mult)
                    nc.vector.tensor_tensor(colp[:, 13:14], colp[:, 11:12], wscol[:, 1:2], ALU.mult).then_inc(hV, 1)
                    vector.wait_ge(hS, F_s(b, 5))
                    xbase = QB + b * 40
                    for p in range(8):
                        vector.wait_ge(hQs, xbase + p + 1)
                        src = xlnf[:] if p == 0 else xqs[:]
                        nc.vector.scalar_tensor_tensor(xqs[:], gq2[:], -float(2.0 ** (6 - p)),
                                                       src, ALU.mult, ALU.add).then_inc(hQv, 1)
                    vector.tensor_sub(xqf[:], xlnf[:], xqs[:])
                    nc.vector.tensor_scalar_add(xqf[:], xqf[:], -0.5).then_inc(hV, 1)
                    vector.wait_ge(hT, F_t(b, 1))
                    vector.tensor_copy(out=xqT[:, 0:512], in_=psb[0][:, :])
                    nc.vector.tensor_copy(out=xqT[:, 512:1024], in_=psb[1][:, :]).then_inc(hV, 1)
                    for j in range(8):
                        bg, bu = 2 + 2 * (j % 3), 3 + 2 * (j % 3)
                        vector.wait_ge(hS, F_s(b, 6 + j))
                        vector.tensor_tensor(sqt[:], rgt[:], rgt[:], ALU.mult)
                        nc.vector.scalar_tensor_tensor(ht[:, j * 512:(j + 1) * 512], psb[bu][:, :],
                                                       colp[:, 13:14], sqt[:], ALU.mult,
                                                       ALU.mult).then_inc(hV, 1)
                    nc.vector.tensor_reduce(colp[:, 4:5], ht[:], AX.X, ALU.max,
                                            apply_absolute_value=True).then_inc(hV, 1)
                    vector.wait_ge(hS, F_s(b, 14))
                    vector.tensor_scalar_mul(colp[:, 11:12], colp[:, 5:6], 1.0 / 127.0)
                    vector.reciprocal(colp[:, 6:7], colp[:, 5:6])
                    nc.vector.tensor_tensor(shc[:, b:b + 1], colp[:, 11:12], wscol[:, 2:3],
                                            ALU.mult).then_inc(hV, 1)
                    vector.wait_ge(hS, F_s(b, 15))
                    for pi, pair in enumerate(((0, 1), (2, 3))):
                        pbq = QB + b * 40 + 8 + pi * 16
                        for p in range(8):
                            for qi, q in enumerate(pair):
                                sT = (gq2, xqf)[qi]
                                wT = (hsv, xqs)[qi]
                                vector.wait_ge(hQs, pbq + p * 2 + qi + 1)
                                src = ht[:, q * 1024:(q + 1) * 1024] if p == 0 else wT[:]
                                nc.vector.scalar_tensor_tensor(wT[:], sT[:], -float(2.0 ** (6 - p)),
                                                               src, ALU.mult, ALU.add).then_inc(hQv, 1)
                        for qi, q in enumerate(pair):
                            wT = (hsv, xqs)[qi]
                            hsl = ht[:, q * 1024:(q + 1) * 1024]
                            vector.tensor_sub(hsl, hsl, wT[:])
                            vector.tensor_scalar_add(hsl, hsl, -0.5)
                    nc.vector.tensor_copy(out=colp[:, 3:4], in_=onescol[:]).then_inc(hV, 1)
                    for t in range(4):
                        vector.wait_ge(hT, F_t(b, 10 + t))
                        if t >= 1:
                            vector.wait_ge(hOUT, F_out(b, t - 1))
                        elif b >= 1:
                            vector.wait_ge(hOUT, F_out(b - 1, 3))
                        vector.tensor_copy(out=hqb[:, 0:512], in_=psb[0][:, :])
                        nc.vector.tensor_copy(out=hqb[:, 512:1024], in_=psb[1][:, :]).then_inc(hV, 1)

            @block.scalar
            def _(scalar):
                scalar.wait_ge(hV, pV)
                for b in range(NB):
                    xb = xblk[b % 2]
                    scalar.wait_ge(hIN, F_in(b) + (0 if b == NB - 1 else 16))
                    if b >= 1:
                        scalar.wait_ge(hV, F_v(b - 1, 2))
                    nc.scalar.activation(out=sdump[:], in_=xb[:, 0:512], func=AF.Square,
                                         accum_out=colp[:, 2:3])
                    nc.scalar.activation(out=sdump[:], in_=xb[:, 512:1024], func=AF.Square,
                                         accum_out=colp[:, 8:9])
                    nc.scalar.activation(out=colp[:, 1:2], in_=onescol[:], func=AF.Identity)
                    nc.scalar.activation(out=colp[:, 8:9], in_=colp[:, 8:9], func=AF.Identity,
                                         bias=colp[:, 2:3])
                    scalar.wait_ge(hV, F_v(b, 1))
                    nc.scalar.activation(out=colp[:, 5:6], in_=colp[:, 4:5], func=AF.Identity,
                                         scale=1.0 / 1024.0)
                    nc.scalar.activation(out=colp[:, 6:7], in_=colp[:, 5:6],
                                         func=AF.Square).then_inc(hS, 1)
                    scalar.wait_ge(hV, F_v(b, 2))
                    nc.scalar.activation(out=colp[:, 9:10], in_=colp[:, 7:8], func=AF.Sqrt,
                                         bias=epscol[:, 0:1]).then_inc(hS, 1)
                    scalar.wait_ge(hV, F_v(b, 3))
                    nc.scalar.activation(out=xlnf[:], in_=xlnf[:], func=AF.Identity,
                                         scale=colp[:, 10:11]).then_inc(hS, 1)
                    scalar.wait_ge(hV, F_v(b, 4))
                    nc.scalar.activation(out=colp[:, 5:6], in_=colp[:, 4:5], func=AF.Relu,
                                         bias=negepscol[:, 0:1])
                    nc.scalar.activation(out=colp[:, 5:6], in_=colp[:, 5:6], func=AF.Identity,
                                         bias=epscol[:, 0:1]).then_inc(hS, 1)
                    scalar.wait_ge(hV, F_v(b, 5))
                    nc.scalar.activation(out=xlnf[:], in_=xlnf[:], func=AF.Identity,
                                         scale=colp[:, 6:7])
                    nc.scalar.activation(out=xlnf[:], in_=xlnf[:], func=AF.Identity,
                                         scale=127.0, bias=colp[:, 0:1]).then_inc(hS, 1)
                    xbase = QB + b * 40
                    for p in range(8):
                        if p == 0:
                            scalar.wait_ge(hQv, xbase)
                        else:
                            scalar.wait_ge(hQv, xbase + p)
                        src = xlnf[:] if p == 0 else xqs[:]
                        nc.scalar.activation(out=gq2[:], in_=src, func=AF.Sign,
                                             bias=nthrcol[:, 0:1]).then_inc(hQs, 1)
                    for j in range(8):
                        bg = 2 + 2 * (j % 3)
                        scalar.wait_ge(hT, F_t(b, 2 + j))
                        if j >= 1:
                            scalar.wait_ge(hV, F_v(b, 8 + j - 1))
                        elif b >= 1:
                            scalar.wait_ge(hV, F_v(b - 1, 15))
                        nc.scalar.activation(out=rgt[:], in_=psb[bg][:, :], func=AF.Relu,
                                             scale=colp[:, 12:13]).then_inc(hS, 1)
                    scalar.wait_ge(hV, F_v(b, 16))
                    nc.scalar.activation(out=colp[:, 5:6], in_=colp[:, 4:5], func=AF.Relu,
                                         bias=negepscol[:, 0:1])
                    nc.scalar.activation(out=colp[:, 5:6], in_=colp[:, 5:6], func=AF.Identity,
                                         bias=epscol[:, 0:1]).then_inc(hS, 1)
                    scalar.wait_ge(hV, F_v(b, 17))
                    nc.scalar.activation(out=ht[:], in_=ht[:], func=AF.Identity, scale=colp[:, 6:7])
                    nc.scalar.activation(out=ht[:], in_=ht[:], func=AF.Identity,
                                         scale=127.0, bias=colp[:, 0:1]).then_inc(hS, 1)
                    for pi, pair in enumerate(((0, 1), (2, 3))):
                        pbq = QB + b * 40 + 8 + pi * 16
                        for p in range(8):
                            for qi, q in enumerate(pair):
                                sT = (gq2, xqf)[qi]
                                wT = (hsv, xqs)[qi]
                                if p == 0:
                                    scalar.wait_ge(hQv, pbq)
                                else:
                                    scalar.wait_ge(hQv, pbq + (p - 1) * 2 + qi + 1)
                                src = ht[:, q * 1024:(q + 1) * 1024] if p == 0 else wT[:]
                                nc.scalar.activation(out=sT[:], in_=src, func=AF.Sign,
                                                     bias=nthrcol[:, 0:1]).then_inc(hQs, 1)

            @block.tensor
            def _(tensor):
                tensor.wait_ge(hV, pV)
                for b in range(NB):
                    tensor.wait_ge(hV, F_v(b, 6))
                    if b >= 1:
                        tensor.wait_ge(hV, F_v(b - 1, 22))
                    for f in range(8):
                        mm = tensor.transpose(psb[f // 4][:, (f % 4) * P:(f % 4 + 1) * P],
                                              xqf[:, f * P:(f + 1) * P], id_f[:])
                    mm.then_inc(hT, 1)
                    tensor.wait_ge(hV, F_v(b, 7))
                    for j in range(8):
                        bg, bu = 2 + 2 * (j % 3), 3 + 2 * (j % 3)
                        if j >= 3:
                            tensor.wait_ge(hV, F_v(b, 8 + j - 3))
                        c0 = j * 512
                        for f in range(8):
                            st_, sp = (f == 0), (f == 7)
                            tensor.matmul(psb[bg][:, :], xqT[:, f * P:(f + 1) * P],
                                          guw[:, f * 8192 + c0:f * 8192 + c0 + 512], start=st_, stop=sp)
                            mm = tensor.matmul(psb[bu][:, :], xqT[:, f * P:(f + 1) * P],
                                               guw[:, f * 8192 + 4096 + c0:f * 8192 + 4096 + c0 + 512],
                                               start=st_, stop=sp)
                        mm.then_inc(hT, 1)
                    tensor.wait_ge(hV, F_v(b, 18))
                    for t in range(4):
                        if t >= 1:
                            tensor.wait_ge(hV, F_v(b, 19 + t - 1))
                        for c in range(8):
                            ch = t * 8 + c
                            mm = tensor.transpose(psb[c // 4][:, (c % 4) * P:(c % 4 + 1) * P],
                                                  ht[:, ch * P:(ch + 1) * P], id_f[:])
                        mm.then_inc(hT, 1)

            @block.gpsimd
            def _(g):
                if not use_stash:
                    for i in range(nGU):
                        wi = 0 if i < 16 else 1
                        f = (i % 16) // 2
                        col0 = (i % 2) * 2048
                        td_ = tgate_d if i < 16 else tup_d
                        src = guw[:, f * 8192 + (0 if i < 16 else 4096) + col0:
                                  f * 8192 + (0 if i < 16 else 4096) + col0 + 2048]
                        g.wait_ge(hV, PRE_V(i))
                        g.dma_start(out=td_[f * P:(f + 1) * P, col0:col0 + 2048],
                                    in_=src).then_inc(hOUT, 16)
                for b in range(NB):
                    for t in range(4):
                        g.wait_ge(hV, F_v(b, 19 + t))
                        row0 = (b * 4 + t) * P
                        g.dma_start(out=hq_d[row0:row0 + P, :], in_=hqb[:]).then_inc(hOUT, 16)

            if dbg_ffn:
                vend, send = pV + KV * NB, b_s + KS * NB
                specs = [("xqf", xqf), ("xqT", xqT), ("ht", ht), ("shc", shc),
                         ("colpF", colp), ("xlnf", xlnf), ("guw0", None)]
                douts = {}
                for nm, tile in specs:
                    if tile is None:
                        douts[nm] = nc.dram_tensor(f"d_{tag}{nm}", [P, 8192], BF16, kind="ExternalOutput").ap()
                    else:
                        douts[nm] = nc.dram_tensor(f"d_{tag}{nm}", list(tile[:].shape), tile[:].dtype,
                                                   kind="ExternalOutput").ap()

                @block.gpsimd
                def _(g):
                    g.wait_ge(hV, vend)
                    g.wait_ge(hS, send)
                    for nm, tile in specs:
                        src = guw[:, 0:8192] if tile is None else tile[:]
                        g.dma_start(out=douts[nm], in_=src).then_inc(hOUT, 16)
            BIN = pIN + KIN * NB
            BOUT = pOUT + KOUT * NB + (len(specs) * 16 if dbg_ffn else 0)
            BT = b_t + KT * NB
            BV = pV + KV * NB
            BS = b_s + KS * NB
            fctx.close()

            f2 = ExitStack()
            f2sb = lambda n_, s_, dt=F32: f2.enter_context(nc.sbuf_tensor(f"{tag}2{n_}", s_, dt))
            wd = f2sb("wd", [P, 32 * DM], BF16)
            st2 = [f2sb(f"st{i}", [P, DM]) for i in range(2)]
            qa2 = f2sb("qa2", [P, DM], BF16)
            qb2 = f2sb("qb2", [P, DM], BF16)
            x2 = [f2sb("x20", [P, DM]), f2sb("x21", [P, DM])]
            hqt = [f2sb(f"hqt{i}", [P, 1024], BF16) for i in range(8)]

            b_in, b_out, b_t, b_v, b_s = BIN, BOUT, BT, BV, BS
            nWD = 32
            P2_IN = lambda i: b_in + 16 * (i + 1)
            P2_V = lambda i: b_v + (i + 1)
            pV2 = b_v + (0 if use_stash else nWD)
            pOUT2 = b_out + (0 if use_stash else nWD * 16)
            pIN2 = b_in + nWD * 16
            K2IN, K2OUT, K2T, K2V = 80, 16, 1, 1
            G_in = lambda b: pIN2 + K2IN * b + 80
            G_t = lambda b: b_t + K2T * b + 1
            G_v = lambda b: pV2 + K2V * b + 1
            G_out = lambda b: pOUT2 + K2OUT * (b + 1)

            @block.sync
            def _(sync):
                sync.wait_ge(hV, pV2 - (0 if use_stash else nWD))
                sync.wait_ge(hS, b_s)
                sync.wait_ge(hT, b_t)
                sync.wait_ge(hOUT, b_out)
                for i in range(nWD):
                    if use_stash:
                        sync.dma_start(out=wd[:, i * DM:(i + 1) * DM],
                                       in_=tdown_d[i * P:(i + 1) * P, :]).then_inc(hIN, 16)
                    else:
                        if i >= 2:
                            sync.wait_ge(hV, P2_V(i - 2))
                        sync.dma_start(out=st2[i % 2][:], in_=wdown_d[i * P:(i + 1) * P, :]).then_inc(hIN, 16)
                for b in range(NB):
                    if b == 0 and not use_stash:
                        sync.wait_ge(hV, P2_V(nWD - 1))
                    if b >= 2:
                        sync.wait_ge(hV, G_v(b - 2))
                        sync.wait_ge(hOUT, G_out(b - 2) + 16)
                    sync.dma_start(out=x2[b % 2][:], in_=xsrc[b * P:(b + 1) * P, :]).then_inc(hIN, 16)
                    for t in range(4):
                        row0 = (b * 4 + t) * P
                        sync.dma_start(out=hqt[4 * (b % 2) + t][:], in_=hq_d[row0:row0 + P, :]).then_inc(hIN, 16)

            @block.vector
            def _(vector):
                if not use_stash:
                    vector.wait_ge(hT, b_t)
                    for i in range(nWD):
                        buf = st2[i % 2]
                        vector.wait_ge(hIN, P2_IN(i) + (0 if i == nWD - 1 else 16))
                        vector.tensor_scalar(qb2[:], buf[:], wscol[:, 6:7], -0.5, ALU.mult, ALU.is_le)
                        vector.tensor_scalar(qa2[:], buf[:], wscol[:, 6:7], 0.5, ALU.mult, ALU.is_ge)
                        vector.tensor_copy(out=colp[:, 3:4], in_=onescol[:])
                        nc.vector.tensor_sub(wd[:, i * DM:(i + 1) * DM], qa2[:], qb2[:]).then_inc(hV, 1)
                for b in range(NB):
                    vector.wait_ge(hT, G_t(b))
                    vector.scalar_tensor_tensor(x2[b % 2][:, 0:512], psb[0 + 2 * (b % 2)][:, :],
                                                shc[:, b:b + 1], x2[b % 2][:, 0:512], ALU.mult, ALU.add)
                    nc.vector.scalar_tensor_tensor(x2[b % 2][:, 512:1024], psb[1 + 2 * (b % 2)][:, :],
                                                   shc[:, b:b + 1], x2[b % 2][:, 512:1024],
                                                   ALU.mult, ALU.add).then_inc(hV, 1)

            @block.tensor
            def _(tensor):
                tensor.wait_ge(hV, pV2)
                for b in range(NB):
                    tensor.wait_ge(hIN, G_in(b) + (80 if b == 0 and NB > 1 else 0))
                    if b >= 2:
                        tensor.wait_ge(hV, G_v(b - 1))
                    o0, o1 = 0 + 2 * (b % 2), 1 + 2 * (b % 2)
                    for c in range(32):
                        st_, sp = (c == 0), (c == 31)
                        lhs = hqt[4 * (b % 2) + c // 8][:, (c % 8) * P:(c % 8 + 1) * P]
                        tensor.matmul(psb[o0][:, :], lhs, wd[:, c * DM:c * DM + 512], start=st_, stop=sp)
                        mm = tensor.matmul(psb[o1][:, :], lhs, wd[:, c * DM + 512:(c + 1) * DM],
                                           start=st_, stop=sp)
                    mm.then_inc(hT, 1)

            @block.gpsimd
            def _(g):
                if not use_stash:
                    for i in range(nWD):
                        g.wait_ge(hV, P2_V(i))
                        g.dma_start(out=tdown_d[i * P:(i + 1) * P, :],
                                    in_=wd[:, i * DM:(i + 1) * DM]).then_inc(hOUT, 16)
                for b in range(NB):
                    g.wait_ge(hV, G_v(b))
                    g.dma_start(out=xdst[b * P:(b + 1) * P, :], in_=x2[b % 2][:]).then_inc(hOUT, 16)

            BIN = pIN2 + K2IN * NB
            BOUT = pOUT2 + K2OUT * NB
            BT = b_t + K2T * NB
            BV = pV2 + K2V * NB
            BS = b_s
            f2.close()

        if "B" in phases:
            emit_ffn("fb", io["B"][0], io["B"][1], use_stash=False, dbg_ffn=dbg)
        def emit_attnb(xsrc, xdst):
            nonlocal BIN, BOUT, BT, BV, BS
            cctx = ExitStack()
            csb = lambda n_, s_, dt=F32: cctx.enter_context(nc.sbuf_tensor(f"c{n_}", s_, dt))
            qkvw = csb("qkvw", [P, 8 * 3072], BF16)
            woutw = csb("woutw", [P, 8 * 1024], BF16)
            wg1w = csb("wg1w", [P, 8 * 64], BF16)
            wg2w = csb("wg2w", [64, 1024], BF16)
            brow = csb("brow", [1, 3072], BF16)
            bg1r = csb("bg1r", [1, 64], BF16)
            bg2r = csb("bg2r", [1, 1024], BF16)
            botr = csb("botr", [1, 1024], BF16)
            lngb = csb("lngb", [P, 1024])
            lnbb = csb("lnbb", [P, 1024])
            kv_sb = csb("kv_sb", [64, 16 * 65])
            kvnb = csb("kvnb", [64, 16 * 64], BF16)
            negM8 = csb("negM8", [64, 16])
            z8 = csb("z8", [64, 16])
            mtmp = csb("mtmp", [64, 16])
            m2tmp = csb("m2tmp", [64, 16])
            d8 = csb("d8", [64, 16])
            rr = csb("rr", [64, 16])
            ztmp = csb("ztmp", [64, 16])
            rz8 = csb("rz8", [64, 16])
            stg3 = csb("stg3", [P, 3072])
            xb2 = [csb("xb0", [P, DM]), csb("xb1", [P, DM])]
            xlnc = csb("xlnc", [P, DM])
            xlnT = csb("xlnT", [P, DM], BF16)
            ktf = csb("ktf", [P, DM])
            kfm = csb("kfm", [64, 2048])
            ktil = csb("ktil", [64, 2048])
            ktil_tm = csb("ktil_tm", [P, DM], BF16)
            v1 = csb("v1", [P, 16 * 65], BF16)
            qsil = csb("qsil", [P, DM])
            qfmT = csb("qfmT", [64, 2048], BF16)
            olnt = csb("olnt", [P, DM])
            ght = csb("ght", [P, 64])
            ghTb = csb("ghTb", [64, P], BF16)
            gsig = csb("gsig", [P, DM])
            gout = csb("gout", [P, DM])
            goutT = csb("goutT", [P, DM], BF16)
            sdu = csb("sdu", [P, 512])

            b_in, b_out, b_t, b_v, b_s = BIN, BOUT, BT, BV, BS
            nL = 8 + 8 + 8 + 1 + 4 + 2
            C_PIN = lambda i: b_in + 16 * (i + 1)
            C_PV = lambda i: b_v + (i + 1)

            @block.sync
            def _(sync):
                sync.wait_ge(hV, b_v)
                sync.wait_ge(hS, b_s)
                sync.wait_ge(hT, b_t)
                sync.wait_ge(hOUT, b_out)
                i = 0
                for f in range(8):
                    if i >= 1:
                        sync.wait_ge(hV, C_PV(i - 1))
                    sync.dma_start(out=stg3[:, 0:3072], in_=wqkv_d[f * P:(f + 1) * P, :]).then_inc(hIN, 16)
                    i += 1
                for f in range(8):
                    sync.wait_ge(hV, C_PV(i - 1))
                    sync.dma_start(out=stg3[:, 0:1024], in_=wout_d[f * P:(f + 1) * P, :]).then_inc(hIN, 16)
                    i += 1
                for f in range(8):
                    sync.wait_ge(hV, C_PV(i - 1))
                    sync.dma_start(out=stg3[:, 0:64], in_=wg1_d[f * P:(f + 1) * P, :]).then_inc(hIN, 16)
                    i += 1
                sync.wait_ge(hV, C_PV(i - 1))
                sync.dma_start(out=stg3[0:64, 0:1024], in_=wg2_d[:, :]).then_inc(hIN, 16)
                i += 1
                for bi, bd in enumerate((bqkv_d, bg1_d, bg2_d, bout_d)):
                    sync.wait_ge(hV, C_PV(i - 1))
                    w_ = (3072, 64, 1024, 1024)[bi]
                    sync.dma_start(out=stg3[0:1, 0:w_], in_=bd[:, :]).then_inc(hIN, 16)
                    i += 1
                for li, (bd, _dst) in enumerate(((lng_d, lngb), (lnb_d, lnbb))):
                    sync.wait_ge(hV, C_PV(i - 1))
                    if li == 1:
                        sync.wait_ge(hT, b_t + 1)
                    sync.dma_start(out=stg3[0:1, 1024:2048], in_=bd[:, :]).then_inc(hIN, 16)
                    i += 1

            @block.vector
            def _(vector):
                vector.wait_ge(hS, b_s)
                vector.wait_ge(hT, b_t)
                vector.memset(kv_sb[:], 0.0)
                vector.memset(negM8[:], 1e4)
                vector.memset(z8[:], 0.0)
                for h in range(16):
                    vector.memset(v1[:, h * 65 + 64:h * 65 + 65], 1.0)
                i = 0
                for f in range(8):
                    vector.wait_ge(hIN, C_PIN(i))
                    nc.vector.tensor_copy(out=qkvw[:, f * 3072:(f + 1) * 3072],
                                          in_=stg3[:, 0:3072]).then_inc(hV, 1)
                    i += 1
                for f in range(8):
                    vector.wait_ge(hIN, C_PIN(i))
                    nc.vector.tensor_copy(out=woutw[:, f * 1024:(f + 1) * 1024],
                                          in_=stg3[:, 0:1024]).then_inc(hV, 1)
                    i += 1
                for f in range(8):
                    vector.wait_ge(hIN, C_PIN(i))
                    nc.vector.tensor_copy(out=wg1w[:, f * 64:(f + 1) * 64],
                                          in_=stg3[:, 0:64]).then_inc(hV, 1)
                    i += 1
                vector.wait_ge(hIN, C_PIN(i))
                nc.vector.tensor_copy(out=wg2w[:], in_=stg3[0:64, 0:1024]).then_inc(hV, 1)
                i += 1
                for bi, dst in enumerate((brow, bg1r, bg2r, botr)):
                    vector.wait_ge(hIN, C_PIN(i))
                    w_ = (3072, 64, 1024, 1024)[bi]
                    nc.vector.tensor_copy(out=dst[0:1, 0:w_], in_=stg3[0:1, 0:w_]).then_inc(hV, 1)
                    i += 1
                for li, dst in enumerate((lngb, lnbb)):
                    vector.wait_ge(hIN, C_PIN(i) + (16 * min(NB, 2) if li == 1 else 0))
                    vector.wait_ge(hT, b_t + li + 1)
                    vector.tensor_copy(out=dst[:, 0:512], in_=psb[4][:, :])
                    nc.vector.tensor_copy(out=dst[:, 512:1024], in_=psb[5][:, :]).then_inc(hV, 1)
                    i += 1

            @block.tensor
            def _(tensor):
                tensor.wait_ge(hV, b_v)
                for li in range(2):
                    tensor.wait_ge(hIN, C_PIN(nL - 2 + li) + (16 * min(NB, 2) if li == 1 else 0))
                    if li == 1:
                        tensor.wait_ge(hV, C_PV(nL - 2))
                    tensor.matmul(psb[4][:, :], onesf[:], stg3[0:1, 1024:1536], start=True, stop=True)
                    mm = tensor.matmul(psb[5][:, :], onesf[:], stg3[0:1, 1536:2048], start=True, stop=True)
                    mm.then_inc(hT, 1)

            BV = b_v + nL
            BT = b_t + 2
            BIN = b_in + nL * 16

            b_in, b_t, b_v, b_s = BIN, BT, BV, BS
            KIN1, KT1, KV1, KS1 = 16, 5, 10, 4
            C_in = lambda b: b_in + KIN1 * b + 16
            C_t = lambda b, m: b_t + KT1 * b + m
            C_v = lambda b, m: b_v + KV1 * b + m
            C_s = lambda b, m: b_s + KS1 * b + m

            @block.sync
            def _(sync):
                for b in range(NB):
                    if b >= 2:
                        sync.wait_ge(hV, C_v(b - 2, 2))
                        sync.wait_ge(hS, C_s(b - 2, 1))
                    sync.dma_start(out=xb2[b % 2][:], in_=xsrc[b * P:(b + 1) * P, :]).then_inc(hIN, 16)

            @block.vector
            def _(vector):
                for b in range(NB):
                    xb = xb2[b % 2]
                    vector.wait_ge(hIN, C_in(b) + (0 if b == NB - 1 else 16))
                    nc.vector.tensor_reduce(colp[:, 4:5], xb[:], AX.X, ALU.add).then_inc(hV, 1)
                    vector.wait_ge(hS, C_s(b, 1))
                    vector.scalar_tensor_tensor(colp[:, 7:8], colp[:, 8:9], 1.0 / 1024.0,
                                                colp[:, 6:7], ALU.mult, ALU.subtract)
                    vector.tensor_copy(out=nmu_c[:, b:b + 1], in_=colp[:, 5:6])
                    if b >= 1:
                        vector.wait_ge(hT, C_t(b - 1, 1))
                    nc.vector.tensor_scalar_sub(xlnc[:], xb[:], colp[:, 5:6]).then_inc(hV, 1)
                    vector.wait_ge(hS, C_s(b, 2))
                    nc.vector.reciprocal(colp[:, 10:11], colp[:, 9:10]).then_inc(hV, 1)
                    vector.wait_ge(hT, C_t(b, 1))
                    vector.tensor_copy(out=xlnT[:, 0:512], in_=psb[4][:, :])
                    vector.tensor_copy(out=xlnT[:, 512:1024], in_=psb[5][:, :])
                    nc.vector.tensor_copy(out=rstdc[:, b:b + 1], in_=colp[:, 10:11]).then_inc(hV, 1)
                    vector.wait_ge(hT, C_t(b, 2))
                    vector.tensor_copy(out=ktf[:, 0:512], in_=psb[0][:, :])
                    vector.tensor_copy(out=ktf[:, 512:1024], in_=psb[1][:, :])
                    for j in range(2):
                        dst = v1[:].rearrange("p (h c) -> p h c", c=65)[:, 8 * j:8 * j + 8, 0:64]
                        srcp = psb[2 + j][:, :].rearrange("p (h c) -> p h c", c=64)
                        mm = vector.tensor_copy(out=dst, in_=srcp)
                    mm.then_inc(hV, 1)
                    vector.wait_ge(hT, C_t(b, 3))
                    for j in range(4):
                        mm = vector.tensor_copy(out=kfm[:, j * 512:(j + 1) * 512], in_=psb[j][0:64, :])
                    mm.then_inc(hV, 1)
                    for c in range(16):
                        vector.tensor_reduce(mtmp[:, c:c + 1], kfm[:, c * P:(c + 1) * P],
                                             AX.X, ALU.max, negate=True)
                    for c in range(16):
                        vector.tensor_tensor(m2tmp[:, c:c + 1], negM8[:, c:c + 1],
                                             mtmp[:, c:c + 1], ALU.min)
                    for c in range(16):
                        mm = vector.tensor_sub(d8[:, c:c + 1], m2tmp[:, c:c + 1], negM8[:, c:c + 1])
                    mm.then_inc(hV, 1)
                    vector.wait_ge(hS, C_s(b, 4))
                    for c in range(16):
                        vector.tensor_reduce(ztmp[:, c:c + 1], ktil[:, c * P:(c + 1) * P], AX.X, ALU.add)
                    for c in range(16):
                        vector.scalar_tensor_tensor(z8[:, c:c + 1], z8[:, c:c + 1], rr[:, c:c + 1],
                                                    ztmp[:, c:c + 1], ALU.mult, ALU.add)
                    for c in range(16):
                        mm = vector.tensor_copy(out=negM8[:, c:c + 1], in_=m2tmp[:, c:c + 1])
                    mm.then_inc(hV, 1)
                    vector.wait_ge(hT, C_t(b, 4))
                    vector.tensor_copy(out=ktil_tm[:, 0:512], in_=psb[4][:, :])
                    nc.vector.tensor_copy(out=ktil_tm[:, 512:1024], in_=psb[5][:, :]).then_inc(hV, 1)
                    vector.wait_ge(hT, C_t(b, 5))
                    for h in range(16):
                        pb_ = (psb[6], psb[7], psb[0])[h // 7]
                        c0 = (h % 7) * 65
                        sl = slice(h * 65, (h + 1) * 65)
                        mm = vector.scalar_tensor_tensor(kv_sb[:, sl], kv_sb[:, sl], rr[:, h:h + 1],
                                                         pb_[0:64, c0:c0 + 65], ALU.mult, ALU.add)
                    mm.then_inc(hV, 1)

            @block.scalar
            def _(scalar):
                scalar.wait_ge(hV, b_v)
                for b in range(NB):
                    xb = xb2[b % 2]
                    scalar.wait_ge(hIN, C_in(b) + (0 if b == NB - 1 else 16))
                    if b >= 1:
                        scalar.wait_ge(hV, C_v(b - 1, 2))
                    nc.scalar.activation(out=sdu[:], in_=xb[:, 0:512], func=AF.Square,
                                         accum_out=colp[:, 2:3])
                    nc.scalar.activation(out=sdu[:], in_=xb[:, 512:1024], func=AF.Square,
                                         accum_out=colp[:, 8:9])
                    nc.scalar.activation(out=colp[:, 1:2], in_=onescol[:], func=AF.Identity)
                    nc.scalar.activation(out=colp[:, 8:9], in_=colp[:, 8:9], func=AF.Identity,
                                         bias=colp[:, 2:3])
                    scalar.wait_ge(hV, C_v(b, 1))
                    nc.scalar.activation(out=colp[:, 5:6], in_=colp[:, 4:5], func=AF.Identity,
                                         scale=1.0 / 1024.0)
                    nc.scalar.activation(out=colp[:, 6:7], in_=colp[:, 5:6],
                                         func=AF.Square).then_inc(hS, 1)
                    scalar.wait_ge(hV, C_v(b, 2))
                    nc.scalar.activation(out=colp[:, 9:10], in_=colp[:, 7:8], func=AF.Sqrt,
                                         bias=epscol[:, 0:1]).then_inc(hS, 1)
                    scalar.wait_ge(hV, C_v(b, 3))
                    nc.scalar.activation(out=xlnc[:], in_=xlnc[:], func=AF.Identity,
                                         scale=colp[:, 10:11]).then_inc(hS, 1)
                    scalar.wait_ge(hV, C_v(b, 7))
                    for c in range(16):
                        nc.scalar.activation(out=rr[:, c:c + 1], in_=d8[:, c:c + 1], func=AF.Exp)
                    for c in range(16):
                        mm = nc.scalar.activation(out=ktil[:, c * P:(c + 1) * P],
                                                  in_=kfm[:, c * P:(c + 1) * P], func=AF.Exp,
                                                  bias=m2tmp[:, c:c + 1])
                    mm.then_inc(hS, 1)

            @block.tensor
            def _(tensor):
                tensor.wait_ge(hV, b_v)
                for b in range(NB):
                    tensor.wait_ge(hS, C_s(b, 3))
                    if b >= 1:
                        tensor.wait_ge(hV, C_v(b - 1, 9))
                    for f in range(8):
                        mm = tensor.transpose(psb[4 + f // 4][:, (f % 4) * P:(f % 4 + 1) * P],
                                              xlnc[:, f * P:(f + 1) * P], id_f[:])
                    mm.then_inc(hT, 1)
                    tensor.wait_ge(hV, C_v(b, 4))
                    if b >= 1:
                        tensor.wait_ge(hV, C_v(b - 1, 10))
                    for f in range(8):
                        st_ = (f == 0)
                        o = f * 3072 + 1024
                        tensor.matmul(psb[0][:, :], xlnT[:, f * P:(f + 1) * P], qkvw[:, o:o + 512], start=st_, stop=False)
                        tensor.matmul(psb[1][:, :], xlnT[:, f * P:(f + 1) * P], qkvw[:, o + 512:o + 1024], start=st_, stop=False)
                        tensor.matmul(psb[2][:, :], xlnT[:, f * P:(f + 1) * P], qkvw[:, o + 1024:o + 1536], start=st_, stop=False)
                        tensor.matmul(psb[3][:, :], xlnT[:, f * P:(f + 1) * P], qkvw[:, o + 1536:o + 2048], start=st_, stop=False)
                    tensor.matmul(psb[0][:, :], onesb[:], brow[0:1, 1024:1536], start=False, stop=True)
                    tensor.matmul(psb[1][:, :], onesb[:], brow[0:1, 1536:2048], start=False, stop=True)
                    tensor.matmul(psb[2][:, :], onesb[:], brow[0:1, 2048:2560], start=False, stop=True)
                    mm = tensor.matmul(psb[3][:, :], onesb[:], brow[0:1, 2560:3072], start=False, stop=True)
                    mm.then_inc(hT, 1)
                    tensor.wait_ge(hV, C_v(b, 5))
                    for h in range(16):
                        mm = tensor.transpose(psb[h // 4][0:64, (h % 4) * P:(h % 4 + 1) * P],
                                              ktf[:, h * 64:(h + 1) * 64], id_f[:])
                    mm.then_inc(hT, 1)
                    tensor.wait_ge(hS, C_s(b, 4))
                    for h in range(16):
                        mm = tensor.transpose(psb[4 + h // 8][:, (h % 8) * 64:(h % 8 + 1) * 64],
                                              ktil[:, h * P:(h + 1) * P], id_f[0:64, 0:64])
                    mm.then_inc(hT, 1)
                    tensor.wait_ge(hV, C_v(b, 9))
                    for h in range(16):
                        pb_ = (psb[6], psb[7], psb[0])[h // 7]
                        c0 = (h % 7) * 65
                        mm = tensor.matmul(pb_[0:64, c0:c0 + 65],
                                           ktil_tm[:, h * 64:(h + 1) * 64],
                                           v1[:, h * 65:(h + 1) * 65], start=True, stop=True)
                    mm.then_inc(hT, 1)

            BIN = b_in + KIN1 * NB
            BT = b_t + KT1 * NB
            BV = b_v + KV1 * NB
            BS = b_s + KS1 * NB

            b_v = BV
            @block.vector
            def _(vector):
                vector.reciprocal(rz8[:], z8[:])
                vector.tensor_copy(out=colp[:, 3:4], in_=onescol[:])
                for h in range(16):
                    mm = vector.tensor_scalar_mul(kvnb[:, h * 64:(h + 1) * 64],
                                                  kv_sb[:, h * 65:h * 65 + 64], rz8[:, h:h + 1])
                mm.then_inc(hV, 1)
            BV = b_v + 1

            b_in, b_out, b_t, b_v, b_s = BIN, BOUT, BT, BV, BS
            KIN2, KOUT2, KT2, KV2, KS2 = 16, 16, 8, 10, 6
            G_in = lambda b: b_in + KIN2 * b + 16
            G_out = lambda b: b_out + KOUT2 * (b + 1)
            G_t = lambda b, m: b_t + KT2 * b + m
            G_v = lambda b, m: b_v + KV2 * b + m
            G_s = lambda b, m: b_s + KS2 * b + m

            @block.sync
            def _(sync):
                sync.wait_ge(hV, b_v)
                sync.wait_ge(hS, b_s)
                for b in range(NB):
                    if b >= 2:
                        sync.wait_ge(hV, G_v(b - 2, 10))
                        sync.wait_ge(hOUT, G_out(b - 2) + 16)
                    sync.dma_start(out=xb2[b % 2][:], in_=xsrc[b * P:(b + 1) * P, :]).then_inc(hIN, 16)

            @block.gpsimd
            def _(g):
                for b in range(NB):
                    g.wait_ge(hV, G_v(b, 10))
                    g.dma_start(out=xdst[b * P:(b + 1) * P, :], in_=xb2[b % 2][:]).then_inc(hOUT, 16)

            @block.vector
            def _(vector):
                for b in range(NB):
                    xb = xb2[b % 2]
                    vector.wait_ge(hIN, G_in(b) + (16 if b == 0 and NB > 1 else 0))
                    if b >= 1:
                        vector.wait_ge(hT, G_t(b - 1, 1))
                    nc.vector.tensor_scalar_sub(xlnc[:], xb[:], nmu_c[:, b:b + 1]).then_inc(hV, 1)
                    vector.wait_ge(hT, G_t(b, 1))
                    vector.tensor_copy(out=xlnT[:, 0:512], in_=psb[4][:, :])
                    nc.vector.tensor_copy(out=xlnT[:, 512:1024], in_=psb[5][:, :]).then_inc(hV, 1)
                    vector.wait_ge(hT, G_t(b, 3))
                    for j in range(4):
                        mm = vector.tensor_copy(out=qfmT[:, j * 512:(j + 1) * 512], in_=psb[2 + j][0:64, :])
                    mm.then_inc(hV, 1)
                    vector.wait_ge(hT, G_t(b, 4))
                    vector.tensor_reduce(colp[:, 2:3], psb[2][:, :], AX.X, ALU.add)
                    vector.tensor_reduce(colp[:, 3:4], psb[3][:, :], AX.X, ALU.add)
                    vector.tensor_copy(out=ght[:], in_=psb[6][:, 0:64])
                    nc.vector.tensor_add(colp[:, 4:5], colp[:, 2:3], colp[:, 3:4]).then_inc(hV, 1)
                    vector.wait_ge(hS, G_s(b, 3))
                    nc.vector.scalar_tensor_tensor(colp[:, 7:8], colp[:, 8:9], 1.0 / 1024.0,
                                                   colp[:, 6:7], ALU.mult, ALU.subtract).then_inc(hV, 1)
                    vector.wait_ge(hS, G_s(b, 4))
                    nc.vector.reciprocal(colp[:, 10:11], colp[:, 9:10]).then_inc(hV, 1)
                    vector.wait_ge(hT, G_t(b, 5))
                    nc.vector.tensor_copy(out=ghTb[:], in_=psb[7][0:64, 0:P]).then_inc(hV, 1)
                    vector.wait_ge(hS, G_s(b, 5))
                    vector.wait_ge(hS, G_s(b, 6))
                    vector.tensor_tensor(gout[:], olnt[:], lngb[:], ALU.mult)
                    vector.tensor_add(gout[:], gout[:], lnbb[:])
                    nc.vector.tensor_tensor(gout[:], gout[:], gsig[:], ALU.mult).then_inc(hV, 1)
                    vector.wait_ge(hT, G_t(b, 7))
                    vector.tensor_copy(out=goutT[:, 0:512], in_=psb[6][:, :])
                    nc.vector.tensor_copy(out=goutT[:, 512:1024], in_=psb[7][:, :]).then_inc(hV, 1)
                    vector.wait_ge(hT, G_t(b, 8))
                    vector.tensor_add(xb[:, 0:512], psb[0][:, :], xb[:, 0:512])
                    nc.vector.tensor_add(xb[:, 512:1024], psb[1][:, :], xb[:, 512:1024]).then_inc(hV, 1)

            @block.scalar
            def _(scalar):
                scalar.wait_ge(hV, b_v)
                for b in range(NB):
                    scalar.wait_ge(hV, G_v(b, 1))
                    nc.scalar.activation(out=xlnc[:], in_=xlnc[:], func=AF.Identity,
                                         scale=rstdc[:, b:b + 1]).then_inc(hS, 1)
                    scalar.wait_ge(hT, G_t(b, 2))
                    nc.scalar.activation(out=qsil[:, 0:512], in_=psb[0][:, :], func=AF.Silu)
                    nc.scalar.activation(out=qsil[:, 512:1024], in_=psb[1][:, :],
                                         func=AF.Silu).then_inc(hS, 1)
                    scalar.wait_ge(hV, G_v(b, 4))
                    nc.scalar.activation(out=sdu[:], in_=psb[2][:, :], func=AF.Square,
                                         accum_out=colp[:, 8:9])
                    nc.scalar.activation(out=sdu[:], in_=psb[3][:, :], func=AF.Square,
                                         accum_out=colp[:, 2:3])
                    nc.scalar.activation(out=colp[:, 1:2], in_=onescol[:], func=AF.Identity)
                    nc.scalar.activation(out=colp[:, 8:9], in_=colp[:, 8:9], func=AF.Identity,
                                         bias=colp[:, 2:3])
                    nc.scalar.activation(out=colp[:, 5:6], in_=colp[:, 4:5], func=AF.Identity,
                                         scale=1.0 / 1024.0)
                    nc.scalar.activation(out=colp[:, 6:7], in_=colp[:, 5:6],
                                         func=AF.Square).then_inc(hS, 1)
                    scalar.wait_ge(hV, G_v(b, 5))
                    nc.scalar.activation(out=colp[:, 9:10], in_=colp[:, 7:8], func=AF.Sqrt,
                                         bias=epscol[:, 0:1]).then_inc(hS, 1)
                    scalar.wait_ge(hV, G_v(b, 6))
                    nc.scalar.activation(out=colp[:, 11:12], in_=colp[:, 5:6], func=AF.Identity,
                                         scale=colp[:, 10:11])
                    nc.scalar.activation(out=colp[:, 1:2], in_=onescol[:], func=AF.Identity)
                    nc.scalar.activation(out=colp[:, 11:12], in_=colp[:, 11:12], func=AF.Identity,
                                         scale=-1.0)
                    nc.scalar.activation(out=colp[:, 1:2], in_=onescol[:], func=AF.Identity)
                    nc.scalar.activation(out=olnt[:, 0:512], in_=psb[2][:, :], func=AF.Identity,
                                         scale=colp[:, 10:11], bias=colp[:, 11:12])
                    nc.scalar.activation(out=olnt[:, 512:1024], in_=psb[3][:, :], func=AF.Identity,
                                         scale=colp[:, 10:11], bias=colp[:, 11:12]).then_inc(hS, 1)
                    scalar.wait_ge(hT, G_t(b, 6))
                    nc.scalar.activation(out=gsig[:, 0:512], in_=psb[4][:, :], func=AF.Sigmoid)
                    nc.scalar.activation(out=gsig[:, 512:1024], in_=psb[5][:, :],
                                         func=AF.Sigmoid).then_inc(hS, 1)

            @block.tensor
            def _(tensor):
                for b in range(NB):
                    tensor.wait_ge(hS, G_s(b, 1))
                    if b >= 1:
                        tensor.wait_ge(hS, G_s(b - 1, 6))
                    for f in range(8):
                        mm = tensor.transpose(psb[4 + f // 4][:, (f % 4) * P:(f % 4 + 1) * P],
                                              xlnc[:, f * P:(f + 1) * P], id_f[:])
                    mm.then_inc(hT, 1)
                    tensor.wait_ge(hV, G_v(b, 2))
                    if b >= 1:
                        tensor.wait_ge(hV, G_v(b - 1, 10))
                    for f in range(8):
                        st_ = (f == 0)
                        o = f * 3072
                        tensor.matmul(psb[0][:, :], xlnT[:, f * P:(f + 1) * P], qkvw[:, o:o + 512], start=st_, stop=False)
                        tensor.matmul(psb[1][:, :], xlnT[:, f * P:(f + 1) * P], qkvw[:, o + 512:o + 1024], start=st_, stop=False)
                    tensor.matmul(psb[0][:, :], onesb[:], brow[0:1, 0:512], start=False, stop=True)
                    mm = tensor.matmul(psb[1][:, :], onesb[:], brow[0:1, 512:1024], start=False, stop=True)
                    mm.then_inc(hT, 1)
                    tensor.wait_ge(hS, G_s(b, 2))
                    if b >= 1:
                        tensor.wait_ge(hS, G_s(b - 1, 5))
                    for h in range(16):
                        mm = tensor.transpose(psb[2 + h // 4][0:64, (h % 4) * P:(h % 4 + 1) * P],
                                              qsil[:, h * 64:(h + 1) * 64], id_f[:])
                    mm.then_inc(hT, 1)
                    tensor.wait_ge(hV, G_v(b, 3))
                    if b >= 1:
                        tensor.wait_ge(hS, G_s(b - 1, 5))
                    for h in range(16):
                        pb_ = psb[2 + h // 8]
                        c0 = (h % 8) * 64
                        lhs = qfmT[0:64, h * P:(h + 1) * P]
                        tensor.matmul(pb_[:, c0:c0 + 64], lhs,
                                      kvnb[0:64, h * 64:(h + 1) * 64],
                                      start=True, stop=True)
                    for f in range(8):
                        st_ = (f == 0)
                        tensor.matmul(psb[6][:, 0:64], xlnT[:, f * P:(f + 1) * P],
                                      wg1w[:, f * 64:(f + 1) * 64], start=st_, stop=False)
                    mm = tensor.matmul(psb[6][:, 0:64], onesb[:], bg1r[0:1, 0:64], start=False, stop=True)
                    mm.then_inc(hT, 1)
                    tensor.wait_ge(hV, G_v(b, 4))
                    tensor.transpose(psb[7][0:64, 0:P], ght[:], id_f[:]).then_inc(hT, 1)
                    tensor.wait_ge(hV, G_v(b, 7))
                    tensor.matmul(psb[4][:, :], ghTb[:], wg2w[:, 0:512], start=True, stop=False)
                    tensor.matmul(psb[5][:, :], ghTb[:], wg2w[:, 512:1024], start=True, stop=False)
                    tensor.matmul(psb[4][:, :], onesb[:], bg2r[0:1, 0:512], start=False, stop=True)
                    mm = tensor.matmul(psb[5][:, :], onesb[:], bg2r[0:1, 512:1024], start=False, stop=True)
                    mm.then_inc(hT, 1)
                    tensor.wait_ge(hV, G_v(b, 8))
                    for f in range(8):
                        mm = tensor.transpose(psb[6 + f // 4][:, (f % 4) * P:(f % 4 + 1) * P],
                                              gout[:, f * P:(f + 1) * P], id_f[:])
                    mm.then_inc(hT, 1)
                    tensor.wait_ge(hV, G_v(b, 9))
                    for f in range(8):
                        st_ = (f == 0)
                        o = f * 1024
                        tensor.matmul(psb[0][:, :], goutT[:, f * P:(f + 1) * P], woutw[:, o:o + 512], start=st_, stop=False)
                        tensor.matmul(psb[1][:, :], goutT[:, f * P:(f + 1) * P], woutw[:, o + 512:o + 1024], start=st_, stop=False)
                    tensor.matmul(psb[0][:, :], onesb[:], botr[0:1, 0:512], start=False, stop=True)
                    mm = tensor.matmul(psb[1][:, :], onesb[:], botr[0:1, 512:1024], start=False, stop=True)
                    mm.then_inc(hT, 1)

            BIN = b_in + KIN2 * NB
            BOUT = b_out + KOUT2 * NB
            BT = b_t + KT2 * NB
            BV = b_v + KV2 * NB
            BS = b_s + KS2 * NB
            cctx.close()

        if "C" in phases:
            emit_attnb(io["C"][0], io["C"][1])
        if "D" in phases:
            emit_ffn("fd", io["D"][0], io["D"][1], use_stash=("B" in phases))

        @block.sync
        def _(sync):
            sync.wait_ge(hOUT, BOUT)

    return nc



def _host_tables(n_blk):
    p = np.arange(P, dtype=np.float32)
    blk = np.arange(n_blk, dtype=np.float32)
    ipos1 = (blk[None, :] * P + p[:, None] + 1.0).astype(np.float32)
    posidx = (blk[None, :] * P + p[:, None]).astype(np.float32)
    jrev = (127.0 - p)[:, None].astype(np.float32)
    ip1 = (p + 1.0)[:, None].astype(np.float32)
    pairidx = np.repeat(np.arange(64, dtype=np.float32), 2)[None, :]
    i_idx = np.arange(P, dtype=np.float32)
    diffT = (i_idx[None, :] - i_idx[:, None]).astype(np.float32)
    causT = (i_idx[None, :] >= i_idx[:, None]).astype(np.float32)
    idf = np.eye(P, dtype=np.float32)
    return dict(ipos1=ipos1, posidx=posidx, jrev=jrev, ip1=ip1, pairidx=pairidx,
                diffT=diffT, causT=causT, idf=idf)


def make_inputs_for_core(core_inputs, n_blk=32):
    t = _host_tables(n_blk)
    c = lambda a: np.ascontiguousarray(a, dtype=np.float32)
    m = {
        "x": c(core_inputs["x"][: n_blk * P]),
        "wq": c(core_inputs["wq"]), "wk": c(core_inputs["wk"]),
        "wv": c(core_inputs["wv"]), "wg": c(core_inputs["wg"]), "wo": c(core_inputs["wo"]),
        "wgate": c(core_inputs["w_gate"].T), "wup": c(core_inputs["w_up"].T),
        "wdown": c(core_inputs["w_down"].T),
        "wqkv": c(core_inputs["w_qkv"].T), "bqkv": c(core_inputs["b_qkv"][None, :]),
        "wg1": c(core_inputs["w_g1"].T), "bg1": c(core_inputs["b_g1"][None, :]),
        "wg2": c(core_inputs["w_g2"].T), "bg2": c(core_inputs["b_g2"][None, :]),
        "wout": c(core_inputs["w_out"].T), "bout": c(core_inputs["b_out"][None, :]),
        "lng": c(core_inputs["ln_g"][None, :]), "lnb": c(core_inputs["ln_b"][None, :]),
    }
    m.update({k: np.ascontiguousarray(v) for k, v in t.items()})
    return m


def kernel(L0_attn_a_x, L0_attn_a_wq, L0_attn_a_wk, L0_attn_a_wv, L0_attn_a_wg,
           L0_attn_a_wo, L0_attn_b_w_qkv, L0_attn_b_b_qkv, L0_attn_b_w_g1,
           L0_attn_b_b_g1, L0_attn_b_w_g2, L0_attn_b_b_g2, L0_attn_b_w_out,
           L0_attn_b_b_out, L0_attn_b_ln_g, L0_attn_b_ln_b, L0_ffn_w_gate,
           L0_ffn_w_up, L0_ffn_w_down):
    full = dict(
        wq=L0_attn_a_wq, wk=L0_attn_a_wk, wv=L0_attn_a_wv, wg=L0_attn_a_wg,
        wo=L0_attn_a_wo, w_qkv=L0_attn_b_w_qkv, b_qkv=L0_attn_b_b_qkv,
        w_g1=L0_attn_b_w_g1, b_g1=L0_attn_b_b_g1, w_g2=L0_attn_b_w_g2,
        b_g2=L0_attn_b_b_g2, w_out=L0_attn_b_w_out, b_out=L0_attn_b_b_out,
        ln_g=L0_attn_b_ln_g, ln_b=L0_attn_b_ln_b, w_gate=L0_ffn_w_gate,
        w_up=L0_ffn_w_up, w_down=L0_ffn_w_down,
    )
    nblk = 32
    nc = build_nc(n_blk=nblk, phases=("A", "B", "C", "D"))
    in_maps = []
    for core in range(8):
        ci = dict(full)
        ci["x"] = np.ascontiguousarray(L0_attn_a_x[core])
        in_maps.append(make_inputs_for_core(ci, n_blk=nblk))
    res = run_bass_kernel_spmd(nc, in_maps, core_ids=list(range(8)))
    out = np.stack([np.asarray(r["y"]) for r in res.results], axis=0)
    return out.astype(np.float32)


# version 07 (final)
# speedup vs baseline: 1.0263x; 1.0263x over previous
import math
from contextlib import ExitStack

import numpy as np

import concourse.bass as bass
import concourse.mybir as mybir
from concourse.bass_utils import run_bass_kernel_spmd

F32 = mybir.dt.float32
BF16 = mybir.dt.bfloat16
AF = mybir.ActivationFunctionType
ALU = mybir.AluOpType
AX = mybir.AxisListType

P = 128
DM = 1024

_TWO_PI = 2.0 * math.pi
CW_C1 = 6.28125
CW_C2 = float(int((_TWO_PI - CW_C1) * 2 ** 21) / 2 ** 21)
CW_C3 = float(np.float32(_TWO_PI - CW_C1 - CW_C2))
INV_2PI = float(np.float32(1.0 / _TWO_PI))
EPS = 1e-5


def _stair(vec, t, g, nbits):
    for k in range(nbits - 1, -1, -1):
        h = float(2 ** k)
        vec.tensor_single_scalar(g, t, h - 0.5, ALU.is_ge)
        vec.scalar_tensor_tensor(t, g, -h, t, ALU.mult, ALU.add)


def _ev(ap2d):
    return ap2d.rearrange("p (c two) -> p c two", two=2)[:, :, 0:1]


def _od(ap2d):
    return ap2d.rearrange("p (c two) -> p c two", two=2)[:, :, 1:2]


def build_nc(n_blk=32, phases=("A", "B", "C", "D"), dbg=False):
    nc = bass.Bass("TRN2", target_bir_lowering=False, detect_race_conditions=False)
    NT = n_blk * P
    NB = n_blk

    x_in = nc.dram_tensor("x", [NT, DM], F32, kind="ExternalInput").ap()
    y_out = nc.dram_tensor("y", [NT, DM], F32, kind="ExternalOutput").ap()
    wq_d = nc.dram_tensor("wq", [DM, DM], F32, kind="ExternalInput").ap()
    wk_d = nc.dram_tensor("wk", [DM, DM], F32, kind="ExternalInput").ap()
    wv_d = nc.dram_tensor("wv", [DM, 2048], F32, kind="ExternalInput").ap()
    wg_d = nc.dram_tensor("wg", [DM, 2048], F32, kind="ExternalInput").ap()
    wo_d = nc.dram_tensor("wo", [2048, DM], F32, kind="ExternalInput").ap()
    wgate_d = nc.dram_tensor("wgate", [DM, 4096], F32, kind="ExternalInput").ap()
    wup_d = nc.dram_tensor("wup", [DM, 4096], F32, kind="ExternalInput").ap()
    wdown_d = nc.dram_tensor("wdown", [4096, DM], F32, kind="ExternalInput").ap()
    wqkv_d = nc.dram_tensor("wqkv", [DM, 3072], F32, kind="ExternalInput").ap()
    bqkv_d = nc.dram_tensor("bqkv", [1, 3072], F32, kind="ExternalInput").ap()
    wg1_d = nc.dram_tensor("wg1", [DM, 64], F32, kind="ExternalInput").ap()
    bg1_d = nc.dram_tensor("bg1", [1, 64], F32, kind="ExternalInput").ap()
    wg2_d = nc.dram_tensor("wg2", [64, DM], F32, kind="ExternalInput").ap()
    bg2_d = nc.dram_tensor("bg2", [1, DM], F32, kind="ExternalInput").ap()
    wout_d = nc.dram_tensor("wout", [DM, DM], F32, kind="ExternalInput").ap()
    bout_d = nc.dram_tensor("bout", [1, DM], F32, kind="ExternalInput").ap()
    lng_d = nc.dram_tensor("lng", [1, DM], F32, kind="ExternalInput").ap()
    lnb_d = nc.dram_tensor("lnb", [1, DM], F32, kind="ExternalInput").ap()
    ipos1_d = nc.dram_tensor("ipos1", [P, NB], F32, kind="ExternalInput").ap()
    posidx_d = nc.dram_tensor("posidx", [P, NB], F32, kind="ExternalInput").ap()
    jrev_d = nc.dram_tensor("jrev", [P, 1], F32, kind="ExternalInput").ap()
    ip1_d = nc.dram_tensor("ip1", [P, 1], F32, kind="ExternalInput").ap()
    pairidx_d = nc.dram_tensor("pairidx", [1, P], F32, kind="ExternalInput").ap()
    diffT_d = nc.dram_tensor("diffT", [P, P], F32, kind="ExternalInput").ap()
    causT_d = nc.dram_tensor("causT", [P, P], F32, kind="ExternalInput").ap()
    idf_d = nc.dram_tensor("idf", [P, P], F32, kind="ExternalInput").ap()
    xA_d = nc.dram_tensor("xA", [NT, DM], F32, kind="Internal").ap()
    xB_d = nc.dram_tensor("xB", [NT, DM], F32, kind="Internal").ap()
    xC_d = nc.dram_tensor("xC", [NT, DM], F32, kind="Internal").ap()
    sin_d = nc.dram_tensor("sin_t", [NT, P], BF16, kind="Internal").ap()
    cos_d = nc.dram_tensor("cos_t", [NT, P], BF16, kind="Internal").ap()
    hq_d = nc.dram_tensor("hq", [NB * 4 * P, 1024], BF16, kind="Internal").ap()
    tgate_d = nc.dram_tensor("tgate", [DM, 4096], BF16, kind="Internal").ap()
    tup_d = nc.dram_tensor("tup", [DM, 4096], BF16, kind="Internal").ap()
    tdown_d = nc.dram_tensor("tdown", [4096, DM], BF16, kind="Internal").ap()

    io = {}
    src = x_in
    scratch_for = {"A": xA_d, "B": xB_d, "C": xC_d, "D": None}
    for ph in phases:
        dst = y_out if ph == phases[-1] else scratch_for[ph]
        io[ph] = (src, dst)
        src = dst

    has_ffn = ("B" in phases) or ("D" in phases)
    hasA = "A" in phases

    top = ExitStack()
    with top:
        sbuf = lambda name, shape, dt=F32: top.enter_context(nc.sbuf_tensor(name, shape, dt))
        id_f = sbuf("id_f", [P, P])
        id_b = sbuf("id_b", [P, P], BF16)
        onesb = sbuf("onesb", [1, P], BF16)
        onesf = sbuf("onesf", [1, P])
        onescol = sbuf("onescol", [P, 1])
        dcol8 = sbuf("dcol8", [P, 8])
        ccol8 = sbuf("ccol8", [P, 8])
        l128c8 = sbuf("l128c8", [P, 8])
        rsR = sbuf("rsR", [P, 8 * NB])
        D8 = sbuf("D8", [P, 8 * P])
        dhc = sbuf("dhc", [P, 8])
        rs128c = sbuf("rs128c", [P, 1])
        wscol = sbuf("wscol", [P, 8])
        nmu_c = sbuf("nmu_c", [P, NB])
        rstdc = sbuf("rstdc", [P, NB])
        shc = sbuf("shc", [P, NB])
        colp = sbuf("colp", [P, 20])
        epscol = sbuf("epscol", [P, 1])
        monecol = sbuf("monecol", [P, 1])
        negepscol = sbuf("negepscol", [P, 1])
        nthrcol = sbuf("nthrcol", [P, 1])
        prow = sbuf("prow", [1, 336])
        psb = [top.enter_context(nc.psum_tensor(f"pb{i}", [P, 512], F32)) for i in range(8)]

        hIN = top.enter_context(nc.semaphore("sIN"))
        hOUT = top.enter_context(nc.semaphore("sOUT"))
        hT = top.enter_context(nc.semaphore("sT"))
        hV = top.enter_context(nc.semaphore("sV"))
        hS = top.enter_context(nc.semaphore("sS"))
        hQs = top.enter_context(nc.semaphore("sQs"))
        hQv = top.enter_context(nc.semaphore("sQv"))
        SEMO = {"T": hT, "V": hV, "S": hS}

        block = top.enter_context(nc.Block())

        aw = awo = Ssb = Sbf = None
        actx = ExitStack()
        asb = lambda name, shape, dt=F32: actx.enter_context(nc.sbuf_tensor(name, shape, dt))
        if hasA:
            aw = asb("aw", [P, 8 * 6144], BF16)
            awo = asb("awo", [P, 16 * DM], BF16)
            Ssb = asb("Ssb", [P, 8 * 257])
            Sbf = asb("Sbf", [P, 8 * 257], BF16)
        stg_ctx = ExitStack()
        psbuf = lambda name, shape, dt=F32: stg_ctx.enter_context(nc.sbuf_tensor(name, shape, dt))
        ipos1 = psbuf("ipos1_s", [P, NB])
        posidx = psbuf("posidx_s", [P, NB])
        jrev = psbuf("jrev_s", [P, 1])
        ip1 = psbuf("ip1_s", [P, 1])
        pairidx = psbuf("pairidx_s", [1, P])
        diffT = psbuf("diffT_s", [P, P])
        causT = psbuf("causT_s", [P, P])
        scg = [psbuf(f"scg{i}", [P, P], BF16) for i in range(2)]
        tga = psbuf("tga", [P, P])
        tgb = psbuf("tgb", [P, P])
        tgc = psbuf("tgc", [P, P])
        targ = psbuf("targ", [P, P])
        tsc1 = psbuf("tsc1", [P, P])
        tsc2 = psbuf("tsc2", [P, max(P, NB)])
        tsc3 = psbuf("tsc3", [P, max(P, NB)])
        angb = psbuf("angb", [P, P])
        lamr = psbuf("lamr", [1, 8])
        wred = psbuf("wred", [P, 4])
        stgA = psbuf("stgA", [P, 6144])
        stgB = psbuf("stgB", [P, 6144])

        nWS = 48 if has_ffn else 0
        nQK = 8 if hasA else 0
        nWO = 16 if hasA else 0

        steps = []

        def st(eng):
            def deco(fn):
                steps.append((eng, fn))
                return None
            return deco

        @st("V")
        def _(v):
            v.memset(onesb[:], 1.0)
            v.memset(onesf[:], 1.0)
            v.memset(onescol[:], 1.0)
            v.memset(prow[0:1, 300:301], 63.0)
            v.memset(prow[0:1, 302:303], 10000.0)
            v.memset(prow[0:1, 305:306], 128.0)
            v.memset(epscol[:], EPS)
            v.memset(monecol[:], -1.0)
            v.memset(negepscol[:], -EPS)
            v.memset(nthrcol[:], -127.5)
            v.tensor_copy(out=id_b[:], in_=id_f[:])
            v.reciprocal(prow[0:1, 301:302], prow[0:1, 300:301])
            v.tensor_scalar_mul(colp[:, 0:1], onescol[:], 128.0)
            for h in range(8):
                v.memset(lamr[0:1, h:h + 1], 1.0 - 2.0 ** (-(5 + h)))
            return v.tensor_scalar_mul(prow[0:1, 0:128], pairidx[:], prow[0:1, 301:302])

        @st("S")
        def _(s):
            s.activation(out=prow[0:1, 303:304], in_=prow[0:1, 302:303], func=AF.Ln)
            s.activation(out=prow[0:1, 304:305], in_=prow[0:1, 303:304], func=AF.Identity, scale=-1.0)
            s.activation(out=prow[0:1, 128:256], in_=prow[0:1, 0:128], func=AF.Exp, scale=prow[0:1, 304:305])
            s.activation(out=prow[0:1, 256:264], in_=lamr[0:1, :], func=AF.Ln)
            return s.activation(out=prow[0:1, 306:307], in_=prow[0:1, 305:306], func=AF.Sqrt)

        @st("V")
        def _(v):
            return v.reciprocal(prow[0:1, 307:308], prow[0:1, 306:307])

        @st("T")
        def _(t):
            t.matmul(psb[0][:, 0:P], onesf[:], prow[0:1, 128:256], start=True, stop=True)
            t.matmul(psb[1][:, 0:8], onesf[:], prow[0:1, 256:264], start=True, stop=True)
            return t.matmul(psb[2][:, 0:1], onesf[:], prow[0:1, 307:308], start=True, stop=True)

        @st("V")
        def _(v):
            v.tensor_copy(out=angb[:], in_=psb[0][:, 0:P])
            v.tensor_copy(out=dhc[:], in_=psb[1][:, 0:8])
            return v.tensor_copy(out=rs128c[:], in_=psb[2][:, 0:1])

        if has_ffn:
            @st("T")
            def _(t):
                return t.matmul(psb[3][0:1, 0:4], onescol[:], wred[:, 0:4], start=True, stop=True)

            @st("V")
            def _(v):
                v.tensor_copy(out=prow[0:1, 264:268], in_=psb[3][0:1, 0:4])
                v.tensor_copy(out=prow[0:1, 310:311], in_=onescol[0:1, 0:1])
                v.tensor_scalar_mul(prow[0:1, 272:276], prow[0:1, 264:268], 1.0 / 4194304.0)
                v.tensor_copy(out=prow[0:1, 311:312], in_=onescol[0:1, 0:1])
                v.reciprocal(prow[0:1, 276:280], prow[0:1, 272:276])
                v.tensor_copy(out=prow[0:1, 312:313], in_=onescol[0:1, 0:1])
                return v.tensor_copy(out=prow[0:1, 280:284], in_=prow[0:1, 276:280])

            @st("T")
            def _(t):
                return t.matmul(psb[4][:, 0:8], onesf[:], prow[0:1, 272:280], start=True, stop=True)

            @st("V")
            def _(v):
                return v.tensor_copy(out=wscol[:], in_=psb[4][:, 0:8])

        for h in range(8):
            @st("S")
            def _(s, h=h):
                s.activation(out=tsc1[:, 0:P], in_=diffT[:], func=AF.Exp, scale=dhc[:, h:h + 1])
                s.activation(out=dcol8[:, h:h + 1], in_=jrev[:], func=AF.Exp, scale=dhc[:, h:h + 1])
                s.activation(out=ccol8[:, h:h + 1], in_=ip1[:], func=AF.Exp, scale=dhc[:, h:h + 1])
                s.activation(out=l128c8[:, h:h + 1], in_=colp[:, 0:1], func=AF.Exp, scale=dhc[:, h:h + 1])
                s.activation(out=colp[:, 1:2], in_=onescol[:], func=AF.Exp, scale=dhc[:, h:h + 1])
                return s.activation(out=tsc2[:, 0:NB], in_=ipos1[:], func=AF.Exp, scale=dhc[:, h:h + 1])

            @st("V")
            def _(v, h=h):
                v.tensor_tensor(D8[:, h * P:(h + 1) * P], tsc1[:, 0:P], causT[:], ALU.mult)
                v.tensor_scalar(colp[:, 2:3], colp[:, 1:2], -1.0, 1.0, ALU.mult, ALU.add)
                v.tensor_scalar(tsc3[:, 0:NB], tsc2[:, 0:NB], -1.0, 1.0, ALU.mult, ALU.add)
                v.reciprocal(colp[:, 3:4], colp[:, 2:3])
                v.tensor_copy(out=colp[:, 4:5], in_=onescol[:])
                return v.tensor_scalar_mul(tsc3[:, 0:NB], tsc3[:, 0:NB], colp[:, 3:4])

            @st("S")
            def _(s, h=h):
                return s.activation(out=tsc2[:, 0:NB], in_=tsc3[:, 0:NB], func=AF.Sqrt)

            @st("V")
            def _(v, h=h):
                return v.reciprocal(rsR[:, h * NB:(h + 1) * NB], tsc2[:, 0:NB])

        if hasA:
            @st("V")
            def _(v):
                v.memset(Ssb[:], 0.0)
                return v.memset(Sbf[:], 0.0)

        cT = sum(1 for e, _ in steps if e == "T")
        cV = sum(1 for e, _ in steps if e == "V")
        cS = sum(1 for e, _ in steps if e == "S")

        IN_TAB = 8 * 16
        IN_WS = lambda i: IN_TAB + 16 * (i + 1)
        IN_QK = lambda f: IN_TAB + 16 * nWS + 64 * (f + 1)
        IN_WO = lambda v: IN_TAB + 16 * nWS + 64 * nQK + 16 * (v + 1)
        IN_PRE_END = IN_TAB + 16 * nWS + 64 * nQK + 16 * nWO

        V_WS = lambda i: i + 1
        V_STEP_BASE = nWS
        V_QKV = lambda f: nWS + cV + f + 1
        V_WO = lambda v2: nWS + cV + nQK + v2 + 1
        V_SC = lambda b, k: nWS + cV + nQK + nWO + 3 * b + k
        V_PRE_END = nWS + cV + nQK + nWO + 3 * NB + 1
        S_SC = lambda b, k: cS + 2 * b + k
        S_PRE_END = cS + 2 * NB
        T_PRE_END = cT
        OUT_SC = lambda b: 32 * (b + 1)
        OUT_PRE_END = 32 * NB

        def emit_steps(eng_name, eng, base):
            cnt_local = {"T": 0, "V": 0, "S": 0}
            for i, (en, fn) in enumerate(steps):
                if en == eng_name:
                    if i > 0:
                        pe = steps[i - 1][0]
                        if pe != eng_name:
                            eng.wait_ge(SEMO[pe], base[pe] + cnt_local[pe])
                    last = fn(eng)
                    assert last is not None, f"step {i} ({en}) returned no instruction"
                    last.then_inc(SEMO[eng_name], 1)
                cnt_local[en] += 1

        STEP_BASE = {"T": 0, "V": V_STEP_BASE, "S": 0}

        @block.sync
        def _(sync):
            for d, s in ((idf_d, id_f), (ipos1_d, ipos1), (posidx_d, posidx),
                         (jrev_d, jrev), (ip1_d, ip1), (pairidx_d, pairidx),
                         (diffT_d, diffT), (causT_d, causT)):
                sync.dma_start(out=s[:], in_=d[:]).then_inc(hIN, 16)
            for i in range(nWS):
                buf = (stgA, stgB)[i % 2]
                if i >= 2:
                    sync.wait_ge(hV, V_WS(i - 2))
                if i < 8:
                    srcap, w = wgate_d[i * P:(i + 1) * P, :], 4096
                elif i < 16:
                    srcap, w = wup_d[(i - 8) * P:(i - 7) * P, :], 4096
                else:
                    srcap, w = wdown_d[(i - 16) * P:(i - 15) * P, :], 1024
                sync.dma_start(out=buf[:, 0:w], in_=srcap).then_inc(hIN, 16)
            for f in range(nQK):
                buf = (stgA, stgB)[f % 2]
                if f >= 2:
                    sync.wait_ge(hV, V_QKV(f - 2))
                elif nWS:
                    sync.wait_ge(hV, V_WS(nWS - 2 + f))
                sync.dma_start(out=buf[:, 0:1024], in_=wq_d[f * P:(f + 1) * P, :]).then_inc(hIN, 16)
                sync.dma_start(out=buf[:, 1024:2048], in_=wk_d[f * P:(f + 1) * P, :]).then_inc(hIN, 16)
                sync.dma_start(out=buf[:, 2048:4096], in_=wv_d[f * P:(f + 1) * P, :]).then_inc(hIN, 16)
                sync.dma_start(out=buf[:, 4096:6144], in_=wg_d[f * P:(f + 1) * P, :]).then_inc(hIN, 16)
            for v2 in range(nWO):
                buf = (stgA, stgB)[v2 % 2]
                if v2 >= 2:
                    sync.wait_ge(hV, V_WO(v2 - 2))
                else:
                    sync.wait_ge(hV, V_QKV(nQK - 2 + v2))
                sync.dma_start(out=buf[:, 0:1024], in_=wo_d[v2 * P:(v2 + 1) * P, :]).then_inc(hIN, 16)

        @block.vector
        def _(vector):
            vector.wait_ge(hIN, IN_TAB + (32 if nWS >= 2 else (128 if nQK else 0)))
            vector.memset(wred[:], 0.0)
            vector.memset(wred[:, 3:4], 1.0)
            for i in range(nWS):
                buf = (stgA, stgB)[i % 2]
                w = 4096 if i < 16 else 1024
                wi = 0 if i < 8 else (1 if i < 16 else 2)
                vector.wait_ge(hIN, IN_WS(i) + ((64 if nQK else 0) if i == nWS - 1 else 16))
                vector.tensor_reduce(colp[:, 5:6], buf[:, 0:w], AX.X, ALU.add, apply_absolute_value=True)
                vector.tensor_copy(out=colp[:, 6:7], in_=wred[:, 3:4])
                nc.vector.tensor_add(wred[:, wi:wi + 1], wred[:, wi:wi + 1], colp[:, 5:6]).then_inc(hV, 1)
            emit_steps("V", vector, STEP_BASE)
            for f in range(nQK):
                buf = (stgA, stgB)[f % 2]
                vector.wait_ge(hIN, IN_QK(f) + ((16 if nWO else 0) if f == nQK - 1 else 64))
                o = f * 6144
                vector.tensor_copy(out=aw[:, o:o + 1024], in_=buf[:, 0:1024])
                vector.tensor_scalar_mul(aw[:, o + 1024:o + 2048], buf[:, 1024:2048], rs128c[:, 0:1])
                vector.tensor_copy(out=aw[:, o + 2048:o + 4096], in_=buf[:, 2048:4096])
                nc.vector.tensor_copy(out=aw[:, o + 4096:o + 6144], in_=buf[:, 4096:6144]).then_inc(hV, 1)
            for v2 in range(nWO):
                buf = (stgA, stgB)[v2 % 2]
                vector.wait_ge(hIN, IN_WO(v2) + (0 if v2 == nWO - 1 else 16))
                nc.vector.tensor_copy(out=awo[:, v2 * 1024:(v2 + 1) * 1024], in_=buf[:, 0:1024]).then_inc(hV, 1)
            for b in range(NB):
                sb_s, sb_c = scg[0], scg[1]
                if b >= 1:
                    vector.wait_ge(hOUT, OUT_SC(b - 1))
                vector.tensor_scalar_mul(targ[:], angb[:], posidx[:, b:b + 1])
                vector.tensor_scalar_mul(tsc1[:, 0:P], targ[:], INV_2PI)
                vector.tensor_scalar_mul(tsc3[:, 0:P], targ[:], INV_2PI)
                _stair(vector, tsc1[:, 0:P], tsc2[:, 0:P], 10)
                vector.tensor_sub(tsc3[:, 0:P], tsc3[:, 0:P], tsc1[:, 0:P])
                vector.scalar_tensor_tensor(tsc2[:, 0:P], tsc3[:, 0:P], -CW_C1, targ[:], ALU.mult, ALU.add)
                vector.scalar_tensor_tensor(tsc2[:, 0:P], tsc3[:, 0:P], -CW_C2, tsc2[:, 0:P], ALU.mult, ALU.add)
                nc.vector.scalar_tensor_tensor(targ[:], tsc3[:, 0:P], -CW_C3, tsc2[:, 0:P], ALU.mult, ALU.add).then_inc(hV, 1)
                vector.wait_ge(hS, S_SC(b, 1))
                vector.tensor_tensor(tgb[:], tga[:], tga[:], ALU.mult)
                nc.vector.tensor_scalar(tgb[:], tgb[:], -1.0, 1.0, ALU.mult, ALU.add).then_inc(hV, 1)
                vector.wait_ge(hS, S_SC(b, 2))
                vector.scalar_tensor_tensor(tga[:], tga[:], 2.0, tgc[:], ALU.mult, ALU.mult)
                vector.tensor_scalar(tgb[:], tgb[:], 2.0, -1.0, ALU.mult, ALU.add)
                vector.scalar_tensor_tensor(sb_s[:], tga[:], 2.0, tgb[:], ALU.mult, ALU.mult)
                vector.tensor_tensor(tgc[:], tga[:], tga[:], ALU.mult)
                nc.vector.tensor_scalar(sb_c[:], tgc[:], -2.0, 1.0, ALU.mult, ALU.add).then_inc(hV, 1)
            nc.vector.engine_nop().then_inc(hV, 1)

        @block.scalar
        def _(scalar):
            emit_steps("S", scalar, STEP_BASE)
            for b in range(NB):
                scalar.wait_ge(hV, V_SC(b, 1))
                nc.scalar.activation(out=tga[:], in_=targ[:], func=AF.Sin, scale=0.25).then_inc(hS, 1)
                scalar.wait_ge(hV, V_SC(b, 2))
                nc.scalar.activation(out=tgc[:], in_=tgb[:], func=AF.Sqrt).then_inc(hS, 1)

        @block.tensor
        def _(tensor):
            emit_steps("T", tensor, STEP_BASE)

        @block.gpsimd
        def _(g):
            for b in range(NB):
                g.wait_ge(hV, V_SC(b, 3))
                sb_s, sb_c = scg[0], scg[1]
                g.dma_start(out=sin_d[b * P:(b + 1) * P, :], in_=sb_s[:]).then_inc(hOUT, 16)
                g.dma_start(out=cos_d[b * P:(b + 1) * P, :], in_=sb_c[:]).then_inc(hOUT, 16)

        BIN = IN_PRE_END
        BV = V_PRE_END
        BS = S_PRE_END
        BT = T_PRE_END
        BOUT = OUT_PRE_END

        V_CASTS_DONE = nWS + cV + nQK + nWO
        stg_ctx.close()
        if hasA:
            xsrc, xdst = io["A"]
            xblk = [asb("xblk0", [P, DM]), asb("xblk1", [P, DM])]
            sinrep = asb("sinrep", [P, DM], BF16)
            cosrep = asb("cosrep", [P, DM], BF16)
            xlnb = asb("xlnb", [P, DM])
            xlnT = asb("xlnT", [P, DM], BF16)
            qr_t = asb("qr_t", [P, DM])
            kr_t = asb("kr_t", [P, DM])
            krd = asb("krd", [P, DM], BF16)
            qrT = asb("qrT", [P, DM], BF16)
            krT = asb("krT", [P, DM], BF16)
            vr1 = asb("vr1", [P, 8 * 257], BF16)
            gsilu = asb("gsilu", [P, 2048], BF16)
            gated = asb("gated", [P, 2048])
            gatedT = asb("gatedT", [P, 2048], BF16)
            Mbf = asb("Mbf", [P, P], BF16)
            ocomb = asb("ocomb", [P, 257])
            osct = asb("osct", [P, 256])
            crossS = asb("crossS", [P, 257])
            dumpb = asb("dumpb", [P, 512], BF16)
            tmpe = asb("tmpe", [P, 256])
            tmpo = asb("tmpo", [P, 256])

            KIN, KOUT, KT, KV, KS = 48, 16, 22, 49, 28
            DBG_OUT = 22 * 16 if dbg else 0
            A_in = lambda b, k: BIN + KIN * b + 16 * k
            A_out = lambda b: BOUT + DBG_OUT + KOUT * (b + 1)
            A_t = lambda b, m: BT + KT * b + m
            A_v = lambda b, m: BV + KV * b + m
            V_oc = lambda h: 9 + 5 * h
            V_Su = lambda h: 10 + 5 * h
            V_inv = lambda h: 11 + 5 * h
            V_M = lambda h: 8 if h == 0 else 7 + 5 * h
            V_gate = lambda h: (13 + 5 * h) if h < 7 else 47
            A_s = lambda b, m: BS + KS * b + m
            S_cS = lambda h: 5 + 3 * h
            S_den = lambda h: 6 + 3 * h
            S_g2 = lambda h: 7 + 3 * h

            @block.sync
            def _(sync):
                sync.wait_ge(hV, V_PRE_END)
                sync.wait_ge(hOUT, OUT_PRE_END)
                for b in range(NB):
                    if b >= 2:
                        sync.wait_ge(hOUT, A_out(b - 2) + 16)
                    sync.dma_start(out=xblk[b % 2][:], in_=xsrc[b * P:(b + 1) * P, :]).then_inc(hIN, 16)
                    if b >= 1:
                        sync.wait_ge(hV, A_v(b - 1, 5))
                    sync.wait_ge(hOUT, OUT_SC(b))
                    sin_ap = bass.AP(sin_d.tensor, b * P * P, [[P, P], [0, 8], [1, P]])
                    cos_ap = bass.AP(cos_d.tensor, b * P * P, [[P, P], [0, 8], [1, P]])
                    srep = sinrep[:].rearrange("p (r c) -> p r c", r=8)
                    crep = cosrep[:].rearrange("p (r c) -> p r c", r=8)
                    sync.dma_start(out=srep, in_=sin_ap).then_inc(hIN, 16)
                    sync.dma_start(out=crep, in_=cos_ap).then_inc(hIN, 16)

            if dbg:
                dbg_specs = [
                    ("d_xlnb", xlnb, ("S", 3)), ("d_xlnT", xlnT, ("V", 4)),
                    ("d_qr", qr_t, ("V", 5)), ("d_kr", kr_t, ("V", 5)), ("d_krd", krd, ("V", 5)),
                    ("d_qrT", qrT, ("V", 6)), ("d_krT", krT, ("V", 6)),
                    ("d_vr1", vr1, ("V", 7)), ("d_gsilu", gsilu, ("S", 4)),
                    ("d_Mbf", Mbf, ("V", 8)), ("d_gated", gated, ("V", 48)),
                    ("d_Ssb", Ssb, ("V", 48)), ("d_colp", colp, ("V", 48)),
                    ("d_sinrep", sinrep, ("IN", 3)), ("d_cosrep", cosrep, ("IN", 3)),
                    ("d_rsR", rsR, ("V", 3)), ("d_ccol", ccol8, ("V", 3)),
                    ("d_dcol", dcol8, ("V", 3)), ("d_l128", l128c8, ("V", 3)),
                    ("d_D8", D8, ("V", 3)), ("d_ocomb", ocomb, ("V", 48)),
                    ("d_osct", osct, ("V", 48)), ("d_crossS", crossS, ("V", 48)),
                ]
                dbg_outs = {}
                for nm, tile, _w in dbg_specs:
                    shape = list(tile[:].shape)
                    dt = F32 if nm not in ("d_xlnb", "d_xlnT", "d_qr2", "d_krd", "d_qrT", "d_krT",
                                           "d_vr1", "d_gsilu", "d_Mbf", "d_sinrep", "d_cosrep") else F32
                    dtt = tile[:].dtype
                    dbg_outs[nm] = nc.dram_tensor(nm, shape, dtt, kind="ExternalOutput").ap()

            @block.gpsimd
            def _(g):
                if dbg:
                    for nm, tile, (sem, val) in dbg_specs:
                        hh = {"S": hS, "V": hV, "T": hT, "IN": hIN}[sem]
                        base = {"S": A_s, "V": A_v, "T": A_t, "IN": A_in}[sem]
                        g.wait_ge(hh, base(0, val))
                        g.dma_start(out=dbg_outs[nm], in_=tile[:]).then_inc(hOUT, 16)
                for b in range(NB):
                    g.wait_ge(hV, A_v(b, 49))
                    g.dma_start(out=xdst[b * P:(b + 1) * P, :], in_=xblk[b % 2][:]).then_inc(hOUT, 16)

            @block.tensor
            def _(tensor):
                for b in range(NB):
                    tensor.wait_ge(hS, A_s(b, 3))
                    if b >= 1:
                        tensor.wait_ge(hS, A_s(b - 1, 4))
                    for f in range(8):
                        mm = tensor.transpose(psb[f // 4][:, (f % 4) * P:(f % 4 + 1) * P],
                                              xlnb[:, f * P:(f + 1) * P], id_f[:])
                    mm.then_inc(hT, 1)
                    tensor.wait_ge(hV, A_v(b, 4))
                    for f in range(8):
                        st_, sp = (f == 0), (f == 7)
                        o = f * 6144
                        tensor.matmul(psb[2][:, :], xlnT[:, f * P:(f + 1) * P], aw[:, o:o + 512], start=st_, stop=sp)
                        tensor.matmul(psb[3][:, :], xlnT[:, f * P:(f + 1) * P], aw[:, o + 512:o + 1024], start=st_, stop=sp)
                        tensor.matmul(psb[4][:, :], xlnT[:, f * P:(f + 1) * P], aw[:, o + 1024:o + 1536], start=st_, stop=sp)
                        mm = tensor.matmul(psb[5][:, :], xlnT[:, f * P:(f + 1) * P], aw[:, o + 1536:o + 2048], start=st_, stop=sp)
                    mm.then_inc(hT, 1)
                    tensor.wait_ge(hV, A_v(b, 5))
                    for f in range(8):
                        tensor.transpose(psb[f // 4][:, (f % 4) * P:(f % 4 + 1) * P],
                                         qr_t[:, f * P:(f + 1) * P], id_f[:])
                    for f in range(8):
                        mm = tensor.transpose(psb[6 + f // 4][:, (f % 4) * P:(f % 4 + 1) * P],
                                              kr_t[:, f * P:(f + 1) * P], id_f[:])
                    mm.then_inc(hT, 1)
                    tensor.wait_ge(hV, A_v(b, 6))
                    for f in range(8):
                        st_, sp = (f == 0), (f == 7)
                        o = f * 6144 + 2048
                        for j in range(4):
                            tensor.matmul(psb[2 + j][:, :], xlnT[:, f * P:(f + 1) * P],
                                          aw[:, o + j * 512:o + (j + 1) * 512], start=st_, stop=sp)
                        o = f * 6144 + 4096
                        for j, pbix in enumerate((0, 1, 6, 7)):
                            mm = tensor.matmul(psb[pbix][:, :], xlnT[:, f * P:(f + 1) * P],
                                               aw[:, o + j * 512:o + (j + 1) * 512], start=st_, stop=sp)
                    mm.then_inc(hT, 1)
                    tensor.wait_ge(hV, A_v(b, 7))
                    def hbanks(h):
                        return (psb[2], psb[3], psb[4], psb[5]) if h % 2 == 0 else (psb[0], psb[1], psb[6], psb[7])
                    pA, pC, pAV, pU = hbanks(0)
                    tensor.matmul(pA[:, 0:P], krT[:, 0:P], qrT[:, 0:P], start=True, stop=True).then_inc(hT, 1)
                    for h in range(8):
                        pA, pC, pAV, pU = hbanks(h)
                        tensor.wait_ge(hV, A_v(b, V_M(h)))
                        sl = slice(h * 257, (h + 1) * 257)
                        tensor.matmul(pC[:, 0:257], qrT[:, h * P:(h + 1) * P], Sbf[:, sl], start=True, stop=True)
                        tensor.matmul(pAV[:, 0:257], Mbf[:], vr1[:, sl], start=True, stop=True)
                        mm = tensor.matmul(pU[:, 0:257], krd[:, h * P:(h + 1) * P], vr1[:, sl], start=True, stop=True)
                        mm.then_inc(hT, 1)
                        if h < 7:
                            if h == 0:
                                tensor.wait_ge(hS, A_s(b, 4))
                            else:
                                tensor.wait_ge(hV, A_v(b, V_Su(h - 1)))
                                tensor.wait_ge(hS, A_s(b, S_cS(h - 1)))
                            pA2 = hbanks(h + 1)[0]
                            tensor.matmul(pA2[:, 0:P], krT[:, (h + 1) * P:(h + 2) * P],
                                          qrT[:, (h + 1) * P:(h + 2) * P], start=True, stop=True).then_inc(hT, 1)
                    tensor.wait_ge(hV, A_v(b, V_gate(7)))
                    tensor.wait_ge(hS, A_s(b, S_cS(6)))
                    for j in range(16):
                        mm = tensor.transpose(psb[2 + j // 4][:, (j % 4) * P:(j % 4 + 1) * P],
                                              gated[:, j * P:(j + 1) * P], id_f[:])
                    mm.then_inc(hT, 1)
                    tensor.wait_ge(hV, A_v(b, 48))
                    tensor.wait_ge(hS, A_s(b, S_cS(7)))
                    for c in range(16):
                        st_, sp = (c == 0), (c == 15)
                        tensor.matmul(psb[6][:, :], gatedT[:, c * P:(c + 1) * P],
                                      awo[:, c * 1024:c * 1024 + 512], start=st_, stop=sp)
                        mm = tensor.matmul(psb[7][:, :], gatedT[:, c * P:(c + 1) * P],
                                           awo[:, c * 1024 + 512:(c + 1) * 1024], start=st_, stop=sp)
                    mm.then_inc(hT, 1)

            @block.vector
            def _(vector):
                for h in range(8):
                    vector.memset(vr1[:, h * 257 + 256:h * 257 + 257], 1.0)
                for b in range(NB):
                    xb = xblk[b % 2]
                    vector.wait_ge(hIN, A_in(b, 1) + (48 if b == 0 and NB > 1 else 32))
                    nc.vector.tensor_reduce(colp[:, 4:5], xb[:], AX.X, ALU.add).then_inc(hV, 1)
                    vector.wait_ge(hS, A_s(b, 1))
                    vector.scalar_tensor_tensor(colp[:, 7:8], colp[:, 8:9], 1.0 / 1024.0,
                                                colp[:, 6:7], ALU.mult, ALU.subtract)
                    if b >= 1:
                        vector.wait_ge(hT, A_t(b - 1, 1))
                    nc.vector.tensor_scalar_sub(xlnb[:], xb[:], colp[:, 5:6]).then_inc(hV, 1)
                    vector.wait_ge(hS, A_s(b, 2))
                    nc.vector.reciprocal(colp[:, 10:11], colp[:, 9:10]).then_inc(hV, 1)
                    vector.wait_ge(hT, A_t(b, 1))
                    vector.tensor_copy(out=xlnT[:, 0:512], in_=psb[0][:, :])
                    nc.vector.tensor_copy(out=xlnT[:, 512:1024], in_=psb[1][:, :]).then_inc(hV, 1)
                    vector.wait_ge(hT, A_t(b, 2))
                    vector.wait_ge(hIN, A_in(b, 3) + (16 if b == 0 and NB > 1 else 0))
                    for (pq, dst) in ((2, qr_t), (4, kr_t)):
                        for j in range(2):
                            pb = psb[pq + j]
                            c0 = j * 512
                            pe, po = _ev(pb[:, :]), _od(pb[:, :])
                            ce, co = _ev(cosrep[:, c0:c0 + 512]), _od(cosrep[:, c0:c0 + 512])
                            se, so = _ev(sinrep[:, c0:c0 + 512]), _od(sinrep[:, c0:c0 + 512])
                            vector.tensor_tensor(tmpe[:], pe, ce, ALU.mult)
                            vector.tensor_tensor(tmpo[:], po, se, ALU.mult)
                            vector.tensor_tensor(osct[:, 0:256], po, co, ALU.mult)
                            vector.tensor_tensor(crossS[:, 0:256], pe, so, ALU.mult)
                            vector.tensor_tensor(_ev(dst[:, c0:c0 + 512]), tmpe[:], tmpo[:], ALU.subtract)
                            vector.tensor_tensor(_od(dst[:, c0:c0 + 512]), osct[:, 0:256], crossS[:, 0:256], ALU.add)
                    for h in range(8):
                        mm = vector.tensor_scalar_mul(krd[:, h * P:(h + 1) * P],
                                                      kr_t[:, h * P:(h + 1) * P], dcol8[:, h:h + 1])
                    mm.then_inc(hV, 1)
                    vector.wait_ge(hT, A_t(b, 3))
                    vector.tensor_copy(out=qrT[:, 0:512], in_=psb[0][:, :])
                    vector.tensor_copy(out=qrT[:, 512:1024], in_=psb[1][:, :])
                    vector.tensor_copy(out=krT[:, 0:512], in_=psb[6][:, :])
                    nc.vector.tensor_copy(out=krT[:, 512:1024], in_=psb[7][:, :]).then_inc(hV, 1)
                    vector.wait_ge(hT, A_t(b, 4))
                    for j in range(4):
                        dst = vr1[:].rearrange("p (h c) -> p h c", c=257)[:, 2 * j:2 * j + 2, 0:256]
                        srcp = psb[2 + j][:, :].rearrange("p (h c) -> p h c", c=256)
                        mm = vector.tensor_copy(out=dst, in_=srcp)
                    mm.then_inc(hV, 1)
                    vector.wait_ge(hT, A_t(b, 5))
                    nc.vector.tensor_tensor(Mbf[:], psb[2][:, 0:P], D8[:, 0:P], ALU.mult).then_inc(hV, 1)
                    for h in range(8):
                        pA, pC, pAV, pU = ((psb[2], psb[3], psb[4], psb[5]) if h % 2 == 0
                                           else (psb[0], psb[1], psb[6], psb[7]))
                        sl = slice(h * 257, (h + 1) * 257)
                        vector.wait_ge(hT, A_t(b, 6 + 2 * h))
                        vector.wait_ge(hS, A_s(b, S_cS(h)))
                        nc.vector.tensor_add(ocomb[:], pAV[:, 0:257], crossS[:]).then_inc(hV, 1)
                        vector.wait_ge(hS, A_s(b, S_den(h)))
                        vector.reciprocal(colp[:, 18:19], colp[:, 17:18])
                        vector.scalar_tensor_tensor(Ssb[:, sl], Ssb[:, sl], l128c8[:, h:h + 1],
                                                    pU[:, 0:257], ALU.mult, ALU.add)
                        nc.vector.tensor_copy(out=Sbf[:, sl], in_=Ssb[:, sl]).then_inc(hV, 1)
                        nc.vector.tensor_tensor(colp[:, 12:13], colp[:, 18:19],
                                                rsR[:, h * NB + b:h * NB + b + 1], ALU.mult).then_inc(hV, 1)
                        vector.wait_ge(hS, A_s(b, S_g2(h)))
                        vector.reciprocal(colp[:, 15:16], colp[:, 14:15])
                        if h < 7:
                            vector.wait_ge(hT, A_t(b, 7 + 2 * h))
                            pA2 = (psb[2] if (h + 1) % 2 == 0 else psb[0])
                            nc.vector.tensor_tensor(Mbf[:], pA2[:, 0:P],
                                                    D8[:, (h + 1) * P:(h + 2) * P], ALU.mult).then_inc(hV, 1)
                        else:
                            vector.tensor_copy(out=colp[:, 19:20], in_=onescol[:])
                        nc.vector.scalar_tensor_tensor(gated[:, h * 256:(h + 1) * 256], osct[:],
                                                       colp[:, 15:16], gsilu[:, h * 256:(h + 1) * 256],
                                                       ALU.mult, ALU.mult).then_inc(hV, 1)
                    vector.wait_ge(hT, A_t(b, 21))
                    for j in range(4):
                        mm = vector.tensor_copy(out=gatedT[:, j * 512:(j + 1) * 512], in_=psb[2 + j][:, :])
                    mm.then_inc(hV, 1)
                    vector.wait_ge(hT, A_t(b, 22))
                    vector.tensor_add(xb[:, 0:512], psb[6][:, :], xb[:, 0:512])
                    nc.vector.tensor_add(xb[:, 512:1024], psb[7][:, :], xb[:, 512:1024]).then_inc(hV, 1)

            @block.scalar
            def _(scalar):
                for b in range(NB):
                    xb = xblk[b % 2]
                    scalar.wait_ge(hIN, A_in(b, 1) + (48 if b == 0 and NB > 1 else 32))
                    if b >= 1:
                        scalar.wait_ge(hV, A_v(b - 1, 2))
                    nc.scalar.activation(out=dumpb[:], in_=xb[:, 0:512], func=AF.Square,
                                         accum_out=colp[:, 2:3])
                    nc.scalar.activation(out=dumpb[:], in_=xb[:, 512:1024], func=AF.Square,
                                         accum_out=colp[:, 8:9])
                    nc.scalar.activation(out=colp[:, 1:2], in_=onescol[:], func=AF.Identity)
                    nc.scalar.activation(out=colp[:, 8:9], in_=colp[:, 8:9], func=AF.Identity,
                                         bias=colp[:, 2:3])
                    scalar.wait_ge(hV, A_v(b, 1))
                    nc.scalar.activation(out=colp[:, 5:6], in_=colp[:, 4:5], func=AF.Identity,
                                         scale=1.0 / 1024.0)
                    nc.scalar.activation(out=colp[:, 6:7], in_=colp[:, 5:6],
                                         func=AF.Square).then_inc(hS, 1)
                    scalar.wait_ge(hV, A_v(b, 2))
                    nc.scalar.activation(out=colp[:, 9:10], in_=colp[:, 7:8], func=AF.Sqrt,
                                         bias=epscol[:, 0:1]).then_inc(hS, 1)
                    scalar.wait_ge(hV, A_v(b, 3))
                    nc.scalar.activation(out=xlnb[:], in_=xlnb[:], func=AF.Identity,
                                         scale=colp[:, 10:11]).then_inc(hS, 1)
                    scalar.wait_ge(hT, A_t(b, 4))
                    for j, pbix in enumerate((0, 1, 6, 7)):
                        mm = nc.scalar.activation(out=gsilu[:, j * 512:(j + 1) * 512],
                                                  in_=psb[pbix][:, :], func=AF.Silu)
                    mm.then_inc(hS, 1)
                    for h in range(8):
                        pC = psb[3] if h % 2 == 0 else psb[1]
                        scalar.wait_ge(hT, A_t(b, 6 + 2 * h))
                        nc.scalar.activation(out=crossS[:], in_=pC[:, 0:257], func=AF.Identity,
                                             scale=ccol8[:, h:h + 1]).then_inc(hS, 1)
                        scalar.wait_ge(hV, A_v(b, V_oc(h)))
                        nc.scalar.activation(out=colp[:, 16:17], in_=ocomb[:, 256:257], func=AF.Identity,
                                             scale=rsR[:, h * NB + b:h * NB + b + 1])
                        nc.scalar.activation(out=colp[:, 16:17], in_=colp[:, 16:17], func=AF.Abs)
                        nc.scalar.activation(out=colp[:, 17:18], in_=colp[:, 16:17], func=AF.Relu,
                                             bias=monecol[:, 0:1])
                        nc.scalar.activation(out=colp[:, 17:18], in_=colp[:, 17:18], func=AF.Identity,
                                             bias=onescol[:, 0:1]).then_inc(hS, 1)
                        scalar.wait_ge(hV, A_v(b, V_inv(h)))
                        nc.scalar.activation(out=osct[:], in_=ocomb[:, 0:256], func=AF.Identity,
                                             scale=colp[:, 12:13])
                        nc.scalar.activation(out=dumpb[:, 0:256], in_=osct[:], func=AF.Square,
                                             accum_out=colp[:, 13:14])
                        nc.scalar.activation(out=colp[:, 14:15], in_=colp[:, 13:14], func=AF.Sqrt,
                                             scale=1.0 / 256.0, bias=epscol[:, 0:1]).then_inc(hS, 1)

            BIN += KIN * NB
            BOUT += DBG_OUT + KOUT * NB
            BT += KT * NB
            BV += KV * NB
            BS += KS * NB
        actx.close()

        qstair = [0]

        def emit_ffn(tag, xsrc, xdst, use_stash, dbg_ffn=False):
            nonlocal BIN, BOUT, BT, BV, BS
            fctx = ExitStack()
            fsb = lambda n_, s_, dt=F32: fctx.enter_context(nc.sbuf_tensor(f"{tag}{n_}", s_, dt))
            guw = fsb("guw", [P, 8 * 8192], BF16)
            stgc = [fsb("stg0", [P, 2048])]
            qa = fsb("qa", [P, 2048], BF16)
            qb = fsb("qb", [P, 2048], BF16)
            xblk = [fsb("xb0", [P, DM]), fsb("xb1", [P, DM])]
            xlnf = fsb("xlnf", [P, DM])
            xqs = fsb("xqs", [P, DM])
            xqf = fsb("xqf", [P, DM])
            xqT = fsb("xqT", [P, DM], BF16)
            gq2 = fsb("gq2", [P, 1024])
            hsv = fsb("hsv", [P, 1024])
            ht = fsb("ht", [P, 4096])
            rgt = fsb("rgt", [P, 512])
            sqt = fsb("sqt", [P, 512])
            sdump = fsb("sdump", [P, 512])
            hqb = fsb("hqb", [P, 1024], BF16)

            b_in, b_out, b_t, b_v, b_s = BIN, BOUT, BT, BV, BS
            nGU = 32
            PRE_IN = lambda i: b_in + 16 * (i + 1)
            PRE_V = lambda i: b_v + (i + 1)
            pV = b_v + (0 if use_stash else nGU)
            pOUT = b_out + (0 if use_stash else nGU * 16)
            pIN = b_in + nGU * 16
            KIN, KOUT, KT, KV, KS = 16, 64, 13, 22, 15
            F_in = lambda b: pIN + KIN * b + 16
            F_t = lambda b, m: b_t + KT * b + m
            F_v = lambda b, m: pV + KV * b + m
            F_s = lambda b, m: b_s + KS * b + m
            F_out = lambda b, t: pOUT + KOUT * b + 16 * (t + 1)
            QB = qstair[0]
            qstair[0] += 32 * NB

            @block.sync
            def _(sync):
                sync.wait_ge(hV, b_v)
                sync.wait_ge(hS, b_s)
                sync.wait_ge(hT, b_t)
                sync.wait_ge(hOUT, b_out)
                for i in range(nGU):
                    buf = stgc[0]
                    wd_ = wgate_d if i < 16 else wup_d
                    f, half = (i % 16) // 2, i % 2
                    row0 = ((i % 16) // 2) * P
                    col0 = (i % 2) * 2048
                    if use_stash:
                        td_ = tgate_d if i < 16 else tup_d
                        dst = guw[:, f * 8192 + (0 if i < 16 else 4096) + col0:
                                  f * 8192 + (0 if i < 16 else 4096) + col0 + 2048]
                        sync.dma_start(out=dst, in_=td_[row0:row0 + P, col0:col0 + 2048]).then_inc(hIN, 16)
                    else:
                        if i >= 1:
                            sync.wait_ge(hV, PRE_V(i - 1))
                        sync.dma_start(out=buf[:], in_=wd_[row0:row0 + P, col0:col0 + 2048]).then_inc(hIN, 16)
                for b in range(NB):
                    if b >= 2:
                        sync.wait_ge(hV, F_v(b - 2, 2))
                        sync.wait_ge(hS, F_s(b - 2, 5))
                    sync.dma_start(out=xblk[b % 2][:], in_=xsrc[b * P:(b + 1) * P, :]).then_inc(hIN, 16)

            @block.vector
            def _(vector):
                vector.wait_ge(hS, b_s)
                vector.wait_ge(hT, b_t)
                if not use_stash:
                    for i in range(nGU):
                        buf = stgc[0]
                        wi = 0 if i < 16 else 1
                        f = (i % 16) // 2
                        col0 = (i % 2) * 2048
                        dst = guw[:, f * 8192 + (0 if i < 16 else 4096) + col0:
                                  f * 8192 + (0 if i < 16 else 4096) + col0 + 2048]
                        vector.wait_ge(hIN, PRE_IN(i) + (16 * min(NB, 2) if i == nGU - 1 else 0))
                        vector.tensor_scalar(qb[:], buf[:], wscol[:, 4 + wi:5 + wi], -0.5, ALU.mult, ALU.is_le)
                        vector.tensor_scalar(qa[:], buf[:], wscol[:, 4 + wi:5 + wi], 0.5, ALU.mult, ALU.is_ge)
                        vector.tensor_copy(out=colp[:, 3:4], in_=onescol[:])
                        nc.vector.tensor_sub(dst, qa[:], qb[:]).then_inc(hV, 1)
                for b in range(NB):
                    xb = xblk[b % 2]
                    vector.wait_ge(hIN, F_in(b) + (0 if b == NB - 1 else 16))
                    nc.vector.tensor_reduce(colp[:, 4:5], xb[:], AX.X, ALU.add).then_inc(hV, 1)
                    vector.wait_ge(hS, F_s(b, 1))
                    vector.scalar_tensor_tensor(colp[:, 7:8], colp[:, 8:9], 1.0 / 1024.0,
                                                colp[:, 6:7], ALU.mult, ALU.subtract)
                    nc.vector.tensor_scalar_sub(xlnf[:], xb[:], colp[:, 5:6]).then_inc(hV, 1)
                    vector.wait_ge(hS, F_s(b, 2))
                    nc.vector.reciprocal(colp[:, 10:11], colp[:, 9:10]).then_inc(hV, 1)
                    vector.wait_ge(hS, F_s(b, 3))
                    nc.vector.tensor_reduce(colp[:, 4:5], xlnf[:], AX.X, ALU.max,
                                            apply_absolute_value=True).then_inc(hV, 1)
                    vector.wait_ge(hS, F_s(b, 4))
                    vector.tensor_scalar_mul(colp[:, 11:12], colp[:, 5:6], 1.0 / 127.0)
                    vector.reciprocal(colp[:, 6:7], colp[:, 5:6])
                    vector.tensor_tensor(colp[:, 12:13], colp[:, 11:12], wscol[:, 0:1], ALU.mult)
                    nc.vector.tensor_tensor(colp[:, 13:14], colp[:, 11:12], wscol[:, 1:2], ALU.mult).then_inc(hV, 1)
                    vector.wait_ge(hS, F_s(b, 5))
                    vector.tensor_copy(out=xqs[:], in_=xlnf[:])
                    _stair(vector, xlnf[:], xqf[:], 8)
                    vector.tensor_scalar_add(xqs[:], xqs[:], -128.0)
                    nc.vector.tensor_sub(xqf[:], xqs[:], xlnf[:]).then_inc(hV, 1)
                    vector.wait_ge(hT, F_t(b, 1))
                    vector.tensor_copy(out=xqT[:, 0:512], in_=psb[0][:, :])
                    nc.vector.tensor_copy(out=xqT[:, 512:1024], in_=psb[1][:, :]).then_inc(hV, 1)
                    for j in range(8):
                        bg, bu = 2 * (j % 4), 2 * (j % 4) + 1
                        vector.wait_ge(hS, F_s(b, 6 + j))
                        vector.tensor_tensor(sqt[:], rgt[:], rgt[:], ALU.mult)
                        nc.vector.scalar_tensor_tensor(ht[:, j * 512:(j + 1) * 512], psb[bu][:, :],
                                                       colp[:, 13:14], sqt[:], ALU.mult,
                                                       ALU.mult).then_inc(hV, 1)
                    nc.vector.tensor_reduce(colp[:, 4:5], ht[:], AX.X, ALU.max,
                                            apply_absolute_value=True).then_inc(hV, 1)
                    vector.wait_ge(hS, F_s(b, 14))
                    vector.tensor_scalar_mul(colp[:, 11:12], colp[:, 5:6], 1.0 / 127.0)
                    vector.reciprocal(colp[:, 6:7], colp[:, 5:6])
                    nc.vector.tensor_tensor(shc[:, b:b + 1], colp[:, 11:12], wscol[:, 2:3],
                                            ALU.mult).then_inc(hV, 1)
                    vector.wait_ge(hS, F_s(b, 15))
                    for pi, pair in enumerate(((0, 1), (2, 3))):
                        pbq = QB + b * 32 + pi * 16
                        for p in range(8):
                            for qi, q in enumerate(pair):
                                sT = (gq2, xqf)[qi]
                                wT = (hsv, xqs)[qi]
                                vector.wait_ge(hQs, pbq + p * 2 + qi + 1)
                                src = ht[:, q * 1024:(q + 1) * 1024] if p == 0 else wT[:]
                                nc.vector.scalar_tensor_tensor(wT[:], sT[:], -float(2.0 ** (6 - p)),
                                                               src, ALU.mult, ALU.add).then_inc(hQv, 1)
                        for qi, q in enumerate(pair):
                            wT = (hsv, xqs)[qi]
                            hsl = ht[:, q * 1024:(q + 1) * 1024]
                            vector.tensor_sub(hsl, hsl, wT[:])
                            mm = vector.tensor_scalar_add(hsl, hsl, -0.5)
                        if pi == 0:
                            mm.then_inc(hV, 1)
                    for t in range(4):
                        vector.wait_ge(hT, F_t(b, 10 + t))
                        if t >= 1:
                            vector.wait_ge(hOUT, F_out(b, t - 1))
                        elif b >= 1:
                            vector.wait_ge(hOUT, F_out(b - 1, 3))
                        vector.tensor_copy(out=hqb[:, 0:512], in_=psb[0][:, :])
                        nc.vector.tensor_copy(out=hqb[:, 512:1024], in_=psb[1][:, :]).then_inc(hV, 1)

            @block.scalar
            def _(scalar):
                scalar.wait_ge(hV, pV)
                for b in range(NB):
                    xb = xblk[b % 2]
                    scalar.wait_ge(hIN, F_in(b) + (0 if b == NB - 1 else 16))
                    if b >= 1:
                        scalar.wait_ge(hV, F_v(b - 1, 2))
                    nc.scalar.activation(out=sdump[:], in_=xb[:, 0:512], func=AF.Square,
                                         accum_out=colp[:, 2:3])
                    nc.scalar.activation(out=sdump[:], in_=xb[:, 512:1024], func=AF.Square,
                                         accum_out=colp[:, 8:9])
                    nc.scalar.activation(out=colp[:, 1:2], in_=onescol[:], func=AF.Identity)
                    nc.scalar.activation(out=colp[:, 8:9], in_=colp[:, 8:9], func=AF.Identity,
                                         bias=colp[:, 2:3])
                    scalar.wait_ge(hV, F_v(b, 1))
                    nc.scalar.activation(out=colp[:, 5:6], in_=colp[:, 4:5], func=AF.Identity,
                                         scale=1.0 / 1024.0)
                    nc.scalar.activation(out=colp[:, 6:7], in_=colp[:, 5:6],
                                         func=AF.Square).then_inc(hS, 1)
                    scalar.wait_ge(hV, F_v(b, 2))
                    nc.scalar.activation(out=colp[:, 9:10], in_=colp[:, 7:8], func=AF.Sqrt,
                                         bias=epscol[:, 0:1]).then_inc(hS, 1)
                    scalar.wait_ge(hV, F_v(b, 3))
                    nc.scalar.activation(out=xlnf[:], in_=xlnf[:], func=AF.Identity,
                                         scale=colp[:, 10:11]).then_inc(hS, 1)
                    scalar.wait_ge(hV, F_v(b, 4))
                    nc.scalar.activation(out=colp[:, 5:6], in_=colp[:, 4:5], func=AF.Relu,
                                         bias=negepscol[:, 0:1])
                    nc.scalar.activation(out=colp[:, 5:6], in_=colp[:, 5:6], func=AF.Identity,
                                         bias=epscol[:, 0:1]).then_inc(hS, 1)
                    scalar.wait_ge(hV, F_v(b, 5))
                    nc.scalar.activation(out=xlnf[:], in_=xlnf[:], func=AF.Identity,
                                         scale=colp[:, 6:7])
                    nc.scalar.activation(out=xlnf[:], in_=xlnf[:], func=AF.Identity,
                                         scale=127.0, bias=colp[:, 0:1]).then_inc(hS, 1)
                    for j in range(8):
                        bg = 2 * (j % 4)
                        scalar.wait_ge(hT, F_t(b, 2 + j))
                        if j >= 1:
                            scalar.wait_ge(hV, F_v(b, 8 + j - 1))
                        elif b >= 1:
                            scalar.wait_ge(hV, F_v(b - 1, 15))
                        nc.scalar.activation(out=rgt[:], in_=psb[bg][:, :], func=AF.Relu,
                                             scale=colp[:, 12:13]).then_inc(hS, 1)
                    scalar.wait_ge(hV, F_v(b, 16))
                    nc.scalar.activation(out=colp[:, 5:6], in_=colp[:, 4:5], func=AF.Relu,
                                         bias=negepscol[:, 0:1])
                    nc.scalar.activation(out=colp[:, 5:6], in_=colp[:, 5:6], func=AF.Identity,
                                         bias=epscol[:, 0:1]).then_inc(hS, 1)
                    scalar.wait_ge(hV, F_v(b, 17))
                    nc.scalar.activation(out=ht[:], in_=ht[:], func=AF.Identity, scale=colp[:, 6:7])
                    nc.scalar.activation(out=ht[:], in_=ht[:], func=AF.Identity,
                                         scale=127.0, bias=colp[:, 0:1]).then_inc(hS, 1)
                    for pi, pair in enumerate(((0, 1), (2, 3))):
                        pbq = QB + b * 32 + pi * 16
                        for p in range(8):
                            for qi, q in enumerate(pair):
                                sT = (gq2, xqf)[qi]
                                wT = (hsv, xqs)[qi]
                                if p == 0:
                                    scalar.wait_ge(hQv, pbq)
                                else:
                                    scalar.wait_ge(hQv, pbq + (p - 1) * 2 + qi + 1)
                                src = ht[:, q * 1024:(q + 1) * 1024] if p == 0 else wT[:]
                                nc.scalar.activation(out=sT[:], in_=src, func=AF.Sign,
                                                     bias=nthrcol[:, 0:1]).then_inc(hQs, 1)

            @block.tensor
            def _(tensor):
                tensor.wait_ge(hV, pV)
                for b in range(NB):
                    tensor.wait_ge(hV, F_v(b, 6))
                    if b >= 1:
                        tensor.wait_ge(hV, F_v(b - 1, 22))
                    for f in range(8):
                        mm = tensor.transpose(psb[f // 4][:, (f % 4) * P:(f % 4 + 1) * P],
                                              xqf[:, f * P:(f + 1) * P], id_f[:])
                    mm.then_inc(hT, 1)
                    tensor.wait_ge(hV, F_v(b, 7))
                    for j in range(8):
                        bg, bu = 2 * (j % 4), 2 * (j % 4) + 1
                        if j >= 4:
                            tensor.wait_ge(hV, F_v(b, 8 + j - 4))
                        c0 = j * 512
                        for f in range(8):
                            st_, sp = (f == 0), (f == 7)
                            tensor.matmul(psb[bg][:, :], xqT[:, f * P:(f + 1) * P],
                                          guw[:, f * 8192 + c0:f * 8192 + c0 + 512], start=st_, stop=sp)
                            mm = tensor.matmul(psb[bu][:, :], xqT[:, f * P:(f + 1) * P],
                                               guw[:, f * 8192 + 4096 + c0:f * 8192 + 4096 + c0 + 512],
                                               start=st_, stop=sp)
                        mm.then_inc(hT, 1)
                    tensor.wait_ge(hV, F_v(b, 18))
                    for t in range(4):
                        if t >= 1:
                            tensor.wait_ge(hV, F_v(b, 19 + t - 1))
                        for c in range(8):
                            ch = t * 8 + c
                            mm = tensor.transpose(psb[c // 4][:, (c % 4) * P:(c % 4 + 1) * P],
                                                  ht[:, ch * P:(ch + 1) * P], id_f[:])
                        mm.then_inc(hT, 1)

            @block.gpsimd
            def _(g):
                if not use_stash:
                    for i in range(nGU):
                        wi = 0 if i < 16 else 1
                        f = (i % 16) // 2
                        col0 = (i % 2) * 2048
                        td_ = tgate_d if i < 16 else tup_d
                        src = guw[:, f * 8192 + (0 if i < 16 else 4096) + col0:
                                  f * 8192 + (0 if i < 16 else 4096) + col0 + 2048]
                        g.wait_ge(hV, PRE_V(i))
                        g.dma_start(out=td_[f * P:(f + 1) * P, col0:col0 + 2048],
                                    in_=src).then_inc(hOUT, 16)
                for b in range(NB):
                    for t in range(4):
                        g.wait_ge(hV, F_v(b, 19 + t))
                        row0 = (b * 4 + t) * P
                        g.dma_start(out=hq_d[row0:row0 + P, :], in_=hqb[:]).then_inc(hOUT, 16)

            if dbg_ffn:
                vend, send = pV + KV * NB, b_s + KS * NB
                specs = [("xqf", xqf), ("xqT", xqT), ("ht", ht), ("shc", shc),
                         ("colpF", colp), ("xlnf", xlnf), ("guw0", None)]
                douts = {}
                for nm, tile in specs:
                    if tile is None:
                        douts[nm] = nc.dram_tensor(f"d_{tag}{nm}", [P, 8192], BF16, kind="ExternalOutput").ap()
                    else:
                        douts[nm] = nc.dram_tensor(f"d_{tag}{nm}", list(tile[:].shape), tile[:].dtype,
                                                   kind="ExternalOutput").ap()

                @block.gpsimd
                def _(g):
                    g.wait_ge(hV, vend)
                    g.wait_ge(hS, send)
                    for nm, tile in specs:
                        src = guw[:, 0:8192] if tile is None else tile[:]
                        g.dma_start(out=douts[nm], in_=src).then_inc(hOUT, 16)
            BIN = pIN + KIN * NB
            BOUT = pOUT + KOUT * NB + (len(specs) * 16 if dbg_ffn else 0)
            BT = b_t + KT * NB
            BV = pV + KV * NB
            BS = b_s + KS * NB
            fctx.close()

            f2 = ExitStack()
            f2sb = lambda n_, s_, dt=F32: f2.enter_context(nc.sbuf_tensor(f"{tag}2{n_}", s_, dt))
            wd = f2sb("wd", [P, 32 * DM], BF16)
            st2 = [f2sb(f"st{i}", [P, DM]) for i in range(2)]
            qa2 = f2sb("qa2", [P, DM], BF16)
            qb2 = f2sb("qb2", [P, DM], BF16)
            x2 = [f2sb("x20", [P, DM]), f2sb("x21", [P, DM])]
            hqt = [f2sb(f"hqt{i}", [P, 1024], BF16) for i in range(8)]

            b_in, b_out, b_t, b_v, b_s = BIN, BOUT, BT, BV, BS
            nWD = 32
            P2_IN = lambda i: b_in + 16 * (i + 1)
            P2_V = lambda i: b_v + (i + 1)
            pV2 = b_v + (0 if use_stash else nWD)
            pOUT2 = b_out + (0 if use_stash else nWD * 16)
            pIN2 = b_in + nWD * 16
            K2IN, K2OUT, K2T, K2V = 80, 16, 1, 1
            G_in = lambda b: pIN2 + K2IN * b + 80
            G_t = lambda b: b_t + K2T * b + 1
            G_v = lambda b: pV2 + K2V * b + 1
            G_out = lambda b: pOUT2 + K2OUT * (b + 1)

            @block.sync
            def _(sync):
                sync.wait_ge(hV, pV2 - (0 if use_stash else nWD))
                sync.wait_ge(hS, b_s)
                sync.wait_ge(hT, b_t)
                sync.wait_ge(hOUT, b_out)
                for i in range(nWD):
                    if use_stash:
                        sync.dma_start(out=wd[:, i * DM:(i + 1) * DM],
                                       in_=tdown_d[i * P:(i + 1) * P, :]).then_inc(hIN, 16)
                    else:
                        if i >= 2:
                            sync.wait_ge(hV, P2_V(i - 2))
                        sync.dma_start(out=st2[i % 2][:], in_=wdown_d[i * P:(i + 1) * P, :]).then_inc(hIN, 16)
                for b in range(NB):
                    if b == 0 and not use_stash:
                        sync.wait_ge(hV, P2_V(nWD - 1))
                    if b >= 2:
                        sync.wait_ge(hV, G_v(b - 2))
                        sync.wait_ge(hOUT, G_out(b - 2) + 16)
                    sync.dma_start(out=x2[b % 2][:], in_=xsrc[b * P:(b + 1) * P, :]).then_inc(hIN, 16)
                    for t in range(4):
                        row0 = (b * 4 + t) * P
                        sync.dma_start(out=hqt[4 * (b % 2) + t][:], in_=hq_d[row0:row0 + P, :]).then_inc(hIN, 16)

            @block.vector
            def _(vector):
                if not use_stash:
                    vector.wait_ge(hT, b_t)
                    for i in range(nWD):
                        buf = st2[i % 2]
                        vector.wait_ge(hIN, P2_IN(i) + (0 if i == nWD - 1 else 16))
                        vector.tensor_scalar(qb2[:], buf[:], wscol[:, 6:7], -0.5, ALU.mult, ALU.is_le)
                        vector.tensor_scalar(qa2[:], buf[:], wscol[:, 6:7], 0.5, ALU.mult, ALU.is_ge)
                        vector.tensor_copy(out=colp[:, 3:4], in_=onescol[:])
                        nc.vector.tensor_sub(wd[:, i * DM:(i + 1) * DM], qa2[:], qb2[:]).then_inc(hV, 1)
                for b in range(NB):
                    vector.wait_ge(hT, G_t(b))
                    vector.scalar_tensor_tensor(x2[b % 2][:, 0:512], psb[0 + 2 * (b % 2)][:, :],
                                                shc[:, b:b + 1], x2[b % 2][:, 0:512], ALU.mult, ALU.add)
                    nc.vector.scalar_tensor_tensor(x2[b % 2][:, 512:1024], psb[1 + 2 * (b % 2)][:, :],
                                                   shc[:, b:b + 1], x2[b % 2][:, 512:1024],
                                                   ALU.mult, ALU.add).then_inc(hV, 1)

            @block.tensor
            def _(tensor):
                tensor.wait_ge(hV, pV2)
                for b in range(NB):
                    tensor.wait_ge(hIN, G_in(b) + (80 if b == 0 and NB > 1 else 0))
                    if b >= 2:
                        tensor.wait_ge(hV, G_v(b - 1))
                    o0, o1 = 0 + 2 * (b % 2), 1 + 2 * (b % 2)
                    for c in range(32):
                        st_, sp = (c == 0), (c == 31)
                        lhs = hqt[4 * (b % 2) + c // 8][:, (c % 8) * P:(c % 8 + 1) * P]
                        tensor.matmul(psb[o0][:, :], lhs, wd[:, c * DM:c * DM + 512], start=st_, stop=sp)
                        mm = tensor.matmul(psb[o1][:, :], lhs, wd[:, c * DM + 512:(c + 1) * DM],
                                           start=st_, stop=sp)
                    mm.then_inc(hT, 1)

            @block.gpsimd
            def _(g):
                if not use_stash:
                    for i in range(nWD):
                        g.wait_ge(hV, P2_V(i))
                        g.dma_start(out=tdown_d[i * P:(i + 1) * P, :],
                                    in_=wd[:, i * DM:(i + 1) * DM]).then_inc(hOUT, 16)
                for b in range(NB):
                    g.wait_ge(hV, G_v(b))
                    g.dma_start(out=xdst[b * P:(b + 1) * P, :], in_=x2[b % 2][:]).then_inc(hOUT, 16)

            BIN = pIN2 + K2IN * NB
            BOUT = pOUT2 + K2OUT * NB
            BT = b_t + K2T * NB
            BV = pV2 + K2V * NB
            BS = b_s
            f2.close()

        if "B" in phases:
            emit_ffn("fb", io["B"][0], io["B"][1], use_stash=False, dbg_ffn=dbg)
        def emit_attnb(xsrc, xdst):
            nonlocal BIN, BOUT, BT, BV, BS
            cctx = ExitStack()
            csb = lambda n_, s_, dt=F32: cctx.enter_context(nc.sbuf_tensor(f"c{n_}", s_, dt))
            qkvw = csb("qkvw", [P, 8 * 3072], BF16)
            woutw = csb("woutw", [P, 8 * 1024], BF16)
            wg1w = csb("wg1w", [P, 8 * 64], BF16)
            wg2w = csb("wg2w", [64, 1024], BF16)
            brow = csb("brow", [1, 3072], BF16)
            bg1r = csb("bg1r", [1, 64], BF16)
            bg2r = csb("bg2r", [1, 1024], BF16)
            botr = csb("botr", [1, 1024], BF16)
            lngb = csb("lngb", [P, 1024])
            lnbb = csb("lnbb", [P, 1024])
            kv_sb = csb("kv_sb", [64, 16 * 65])
            kvnb = csb("kvnb", [64, 16 * 64], BF16)
            negM8 = csb("negM8", [64, 16])
            z8 = csb("z8", [64, 16])
            mtmp = csb("mtmp", [64, 16])
            m2tmp = csb("m2tmp", [64, 16])
            d8 = csb("d8", [64, 16])
            rr = csb("rr", [64, 16])
            ztmp = csb("ztmp", [64, 16])
            rz8 = csb("rz8", [64, 16])
            stg3 = csb("stg3", [P, 3072])
            xb2 = [csb("xb0", [P, DM]), csb("xb1", [P, DM])]
            xlnc = csb("xlnc", [P, DM])
            xlnT = csb("xlnT", [P, DM], BF16)
            ktf = csb("ktf", [P, DM])
            kfm = csb("kfm", [64, 2048])
            ktil = csb("ktil", [64, 2048])
            ktil_tm = csb("ktil_tm", [P, DM], BF16)
            v1 = csb("v1", [P, 16 * 65], BF16)
            qsil = csb("qsil", [P, DM])
            qfmT = csb("qfmT", [64, 2048], BF16)
            olnt = csb("olnt", [P, DM])
            ght = csb("ght", [P, 64])
            ghTb = csb("ghTb", [64, P], BF16)
            gsig = csb("gsig", [P, DM])
            gout = csb("gout", [P, DM])
            goutT = csb("goutT", [P, DM], BF16)
            sdu = csb("sdu", [P, 512])

            b_in, b_out, b_t, b_v, b_s = BIN, BOUT, BT, BV, BS
            nL = 8 + 8 + 8 + 1 + 4 + 2
            C_PIN = lambda i: b_in + 16 * (i + 1)
            C_PV = lambda i: b_v + (i + 1)

            @block.sync
            def _(sync):
                sync.wait_ge(hV, b_v)
                sync.wait_ge(hS, b_s)
                sync.wait_ge(hT, b_t)
                sync.wait_ge(hOUT, b_out)
                i = 0
                for f in range(8):
                    if i >= 1:
                        sync.wait_ge(hV, C_PV(i - 1))
                    sync.dma_start(out=stg3[:, 0:3072], in_=wqkv_d[f * P:(f + 1) * P, :]).then_inc(hIN, 16)
                    i += 1
                for f in range(8):
                    sync.wait_ge(hV, C_PV(i - 1))
                    sync.dma_start(out=stg3[:, 0:1024], in_=wout_d[f * P:(f + 1) * P, :]).then_inc(hIN, 16)
                    i += 1
                for f in range(8):
                    sync.wait_ge(hV, C_PV(i - 1))
                    sync.dma_start(out=stg3[:, 0:64], in_=wg1_d[f * P:(f + 1) * P, :]).then_inc(hIN, 16)
                    i += 1
                sync.wait_ge(hV, C_PV(i - 1))
                sync.dma_start(out=stg3[0:64, 0:1024], in_=wg2_d[:, :]).then_inc(hIN, 16)
                i += 1
                for bi, bd in enumerate((bqkv_d, bg1_d, bg2_d, bout_d)):
                    sync.wait_ge(hV, C_PV(i - 1))
                    w_ = (3072, 64, 1024, 1024)[bi]
                    sync.dma_start(out=stg3[0:1, 0:w_], in_=bd[:, :]).then_inc(hIN, 16)
                    i += 1
                for li, (bd, _dst) in enumerate(((lng_d, lngb), (lnb_d, lnbb))):
                    sync.wait_ge(hV, C_PV(i - 1))
                    if li == 1:
                        sync.wait_ge(hT, b_t + 1)
                    sync.dma_start(out=stg3[0:1, 1024:2048], in_=bd[:, :]).then_inc(hIN, 16)
                    i += 1

            @block.vector
            def _(vector):
                vector.wait_ge(hS, b_s)
                vector.wait_ge(hT, b_t)
                vector.memset(kv_sb[:], 0.0)
                vector.memset(negM8[:], 1e4)
                vector.memset(z8[:], 0.0)
                for h in range(16):
                    vector.memset(v1[:, h * 65 + 64:h * 65 + 65], 1.0)
                i = 0
                for f in range(8):
                    vector.wait_ge(hIN, C_PIN(i))
                    nc.vector.tensor_copy(out=qkvw[:, f * 3072:(f + 1) * 3072],
                                          in_=stg3[:, 0:3072]).then_inc(hV, 1)
                    i += 1
                for f in range(8):
                    vector.wait_ge(hIN, C_PIN(i))
                    nc.vector.tensor_copy(out=woutw[:, f * 1024:(f + 1) * 1024],
                                          in_=stg3[:, 0:1024]).then_inc(hV, 1)
                    i += 1
                for f in range(8):
                    vector.wait_ge(hIN, C_PIN(i))
                    nc.vector.tensor_copy(out=wg1w[:, f * 64:(f + 1) * 64],
                                          in_=stg3[:, 0:64]).then_inc(hV, 1)
                    i += 1
                vector.wait_ge(hIN, C_PIN(i))
                nc.vector.tensor_copy(out=wg2w[:], in_=stg3[0:64, 0:1024]).then_inc(hV, 1)
                i += 1
                for bi, dst in enumerate((brow, bg1r, bg2r, botr)):
                    vector.wait_ge(hIN, C_PIN(i))
                    w_ = (3072, 64, 1024, 1024)[bi]
                    nc.vector.tensor_copy(out=dst[0:1, 0:w_], in_=stg3[0:1, 0:w_]).then_inc(hV, 1)
                    i += 1
                for li, dst in enumerate((lngb, lnbb)):
                    vector.wait_ge(hIN, C_PIN(i) + (16 * min(NB, 2) if li == 1 else 0))
                    vector.wait_ge(hT, b_t + li + 1)
                    vector.tensor_copy(out=dst[:, 0:512], in_=psb[4][:, :])
                    nc.vector.tensor_copy(out=dst[:, 512:1024], in_=psb[5][:, :]).then_inc(hV, 1)
                    i += 1

            @block.tensor
            def _(tensor):
                tensor.wait_ge(hV, b_v)
                for li in range(2):
                    tensor.wait_ge(hIN, C_PIN(nL - 2 + li) + (16 * min(NB, 2) if li == 1 else 0))
                    if li == 1:
                        tensor.wait_ge(hV, C_PV(nL - 2))
                    tensor.matmul(psb[4][:, :], onesf[:], stg3[0:1, 1024:1536], start=True, stop=True)
                    mm = tensor.matmul(psb[5][:, :], onesf[:], stg3[0:1, 1536:2048], start=True, stop=True)
                    mm.then_inc(hT, 1)

            BV = b_v + nL
            BT = b_t + 2
            BIN = b_in + nL * 16

            b_in, b_t, b_v, b_s = BIN, BT, BV, BS
            KIN1, KT1, KV1, KS1 = 16, 5, 10, 4
            C_in = lambda b: b_in + KIN1 * b + 16
            C_t = lambda b, m: b_t + KT1 * b + m
            C_v = lambda b, m: b_v + KV1 * b + m
            C_s = lambda b, m: b_s + KS1 * b + m

            @block.sync
            def _(sync):
                for b in range(NB):
                    if b >= 2:
                        sync.wait_ge(hV, C_v(b - 2, 2))
                        sync.wait_ge(hS, C_s(b - 2, 1))
                    sync.dma_start(out=xb2[b % 2][:], in_=xsrc[b * P:(b + 1) * P, :]).then_inc(hIN, 16)

            @block.vector
            def _(vector):
                for b in range(NB):
                    xb = xb2[b % 2]
                    vector.wait_ge(hIN, C_in(b) + (0 if b == NB - 1 else 16))
                    nc.vector.tensor_reduce(colp[:, 4:5], xb[:], AX.X, ALU.add).then_inc(hV, 1)
                    vector.wait_ge(hS, C_s(b, 1))
                    vector.scalar_tensor_tensor(colp[:, 7:8], colp[:, 8:9], 1.0 / 1024.0,
                                                colp[:, 6:7], ALU.mult, ALU.subtract)
                    vector.tensor_copy(out=nmu_c[:, b:b + 1], in_=colp[:, 5:6])
                    if b >= 1:
                        vector.wait_ge(hT, C_t(b - 1, 1))
                    nc.vector.tensor_scalar_sub(xlnc[:], xb[:], colp[:, 5:6]).then_inc(hV, 1)
                    vector.wait_ge(hS, C_s(b, 2))
                    nc.vector.reciprocal(colp[:, 10:11], colp[:, 9:10]).then_inc(hV, 1)
                    vector.wait_ge(hT, C_t(b, 1))
                    vector.tensor_copy(out=xlnT[:, 0:512], in_=psb[4][:, :])
                    vector.tensor_copy(out=xlnT[:, 512:1024], in_=psb[5][:, :])
                    nc.vector.tensor_copy(out=rstdc[:, b:b + 1], in_=colp[:, 10:11]).then_inc(hV, 1)
                    vector.wait_ge(hT, C_t(b, 2))
                    vector.tensor_copy(out=ktf[:, 0:512], in_=psb[0][:, :])
                    vector.tensor_copy(out=ktf[:, 512:1024], in_=psb[1][:, :])
                    for j in range(2):
                        dst = v1[:].rearrange("p (h c) -> p h c", c=65)[:, 8 * j:8 * j + 8, 0:64]
                        srcp = psb[2 + j][:, :].rearrange("p (h c) -> p h c", c=64)
                        mm = vector.tensor_copy(out=dst, in_=srcp)
                    mm.then_inc(hV, 1)
                    vector.wait_ge(hT, C_t(b, 3))
                    for j in range(4):
                        mm = vector.tensor_copy(out=kfm[:, j * 512:(j + 1) * 512], in_=psb[j][0:64, :])
                    mm.then_inc(hV, 1)
                    for c in range(16):
                        vector.tensor_reduce(mtmp[:, c:c + 1], kfm[:, c * P:(c + 1) * P],
                                             AX.X, ALU.max, negate=True)
                    for c in range(16):
                        vector.tensor_tensor(m2tmp[:, c:c + 1], negM8[:, c:c + 1],
                                             mtmp[:, c:c + 1], ALU.min)
                    for c in range(16):
                        mm = vector.tensor_sub(d8[:, c:c + 1], m2tmp[:, c:c + 1], negM8[:, c:c + 1])
                    mm.then_inc(hV, 1)
                    vector.wait_ge(hS, C_s(b, 4))
                    for c in range(16):
                        vector.tensor_reduce(ztmp[:, c:c + 1], ktil[:, c * P:(c + 1) * P], AX.X, ALU.add)
                    for c in range(16):
                        vector.scalar_tensor_tensor(z8[:, c:c + 1], z8[:, c:c + 1], rr[:, c:c + 1],
                                                    ztmp[:, c:c + 1], ALU.mult, ALU.add)
                    for c in range(16):
                        mm = vector.tensor_copy(out=negM8[:, c:c + 1], in_=m2tmp[:, c:c + 1])
                    mm.then_inc(hV, 1)
                    vector.wait_ge(hT, C_t(b, 4))
                    vector.tensor_copy(out=ktil_tm[:, 0:512], in_=psb[4][:, :])
                    nc.vector.tensor_copy(out=ktil_tm[:, 512:1024], in_=psb[5][:, :]).then_inc(hV, 1)
                    vector.wait_ge(hT, C_t(b, 5))
                    for h in range(16):
                        pb_ = (psb[6], psb[7], psb[0])[h // 7]
                        c0 = (h % 7) * 65
                        sl = slice(h * 65, (h + 1) * 65)
                        mm = vector.scalar_tensor_tensor(kv_sb[:, sl], kv_sb[:, sl], rr[:, h:h + 1],
                                                         pb_[0:64, c0:c0 + 65], ALU.mult, ALU.add)
                    mm.then_inc(hV, 1)

            @block.scalar
            def _(scalar):
                scalar.wait_ge(hV, b_v)
                for b in range(NB):
                    xb = xb2[b % 2]
                    scalar.wait_ge(hIN, C_in(b) + (0 if b == NB - 1 else 16))
                    if b >= 1:
                        scalar.wait_ge(hV, C_v(b - 1, 2))
                    nc.scalar.activation(out=sdu[:], in_=xb[:, 0:512], func=AF.Square,
                                         accum_out=colp[:, 2:3])
                    nc.scalar.activation(out=sdu[:], in_=xb[:, 512:1024], func=AF.Square,
                                         accum_out=colp[:, 8:9])
                    nc.scalar.activation(out=colp[:, 1:2], in_=onescol[:], func=AF.Identity)
                    nc.scalar.activation(out=colp[:, 8:9], in_=colp[:, 8:9], func=AF.Identity,
                                         bias=colp[:, 2:3])
                    scalar.wait_ge(hV, C_v(b, 1))
                    nc.scalar.activation(out=colp[:, 5:6], in_=colp[:, 4:5], func=AF.Identity,
                                         scale=1.0 / 1024.0)
                    nc.scalar.activation(out=colp[:, 6:7], in_=colp[:, 5:6],
                                         func=AF.Square).then_inc(hS, 1)
                    scalar.wait_ge(hV, C_v(b, 2))
                    nc.scalar.activation(out=colp[:, 9:10], in_=colp[:, 7:8], func=AF.Sqrt,
                                         bias=epscol[:, 0:1]).then_inc(hS, 1)
                    scalar.wait_ge(hV, C_v(b, 3))
                    nc.scalar.activation(out=xlnc[:], in_=xlnc[:], func=AF.Identity,
                                         scale=colp[:, 10:11]).then_inc(hS, 1)
                    scalar.wait_ge(hV, C_v(b, 7))
                    for c in range(16):
                        nc.scalar.activation(out=rr[:, c:c + 1], in_=d8[:, c:c + 1], func=AF.Exp)
                    for c in range(16):
                        mm = nc.scalar.activation(out=ktil[:, c * P:(c + 1) * P],
                                                  in_=kfm[:, c * P:(c + 1) * P], func=AF.Exp,
                                                  bias=m2tmp[:, c:c + 1])
                    mm.then_inc(hS, 1)

            @block.tensor
            def _(tensor):
                tensor.wait_ge(hV, b_v)
                for b in range(NB):
                    tensor.wait_ge(hS, C_s(b, 3))
                    if b >= 1:
                        tensor.wait_ge(hV, C_v(b - 1, 9))
                    for f in range(8):
                        mm = tensor.transpose(psb[4 + f // 4][:, (f % 4) * P:(f % 4 + 1) * P],
                                              xlnc[:, f * P:(f + 1) * P], id_f[:])
                    mm.then_inc(hT, 1)
                    tensor.wait_ge(hV, C_v(b, 4))
                    if b >= 1:
                        tensor.wait_ge(hV, C_v(b - 1, 10))
                    for f in range(8):
                        st_ = (f == 0)
                        o = f * 3072 + 1024
                        tensor.matmul(psb[0][:, :], xlnT[:, f * P:(f + 1) * P], qkvw[:, o:o + 512], start=st_, stop=False)
                        tensor.matmul(psb[1][:, :], xlnT[:, f * P:(f + 1) * P], qkvw[:, o + 512:o + 1024], start=st_, stop=False)
                        tensor.matmul(psb[2][:, :], xlnT[:, f * P:(f + 1) * P], qkvw[:, o + 1024:o + 1536], start=st_, stop=False)
                        tensor.matmul(psb[3][:, :], xlnT[:, f * P:(f + 1) * P], qkvw[:, o + 1536:o + 2048], start=st_, stop=False)
                    tensor.matmul(psb[0][:, :], onesb[:], brow[0:1, 1024:1536], start=False, stop=True)
                    tensor.matmul(psb[1][:, :], onesb[:], brow[0:1, 1536:2048], start=False, stop=True)
                    tensor.matmul(psb[2][:, :], onesb[:], brow[0:1, 2048:2560], start=False, stop=True)
                    mm = tensor.matmul(psb[3][:, :], onesb[:], brow[0:1, 2560:3072], start=False, stop=True)
                    mm.then_inc(hT, 1)
                    tensor.wait_ge(hV, C_v(b, 5))
                    for h in range(16):
                        mm = tensor.transpose(psb[h // 4][0:64, (h % 4) * P:(h % 4 + 1) * P],
                                              ktf[:, h * 64:(h + 1) * 64], id_f[:])
                    mm.then_inc(hT, 1)
                    tensor.wait_ge(hS, C_s(b, 4))
                    for h in range(16):
                        mm = tensor.transpose(psb[4 + h // 8][:, (h % 8) * 64:(h % 8 + 1) * 64],
                                              ktil[:, h * P:(h + 1) * P], id_f[0:64, 0:64])
                    mm.then_inc(hT, 1)
                    tensor.wait_ge(hV, C_v(b, 9))
                    for h in range(16):
                        pb_ = (psb[6], psb[7], psb[0])[h // 7]
                        c0 = (h % 7) * 65
                        mm = tensor.matmul(pb_[0:64, c0:c0 + 65],
                                           ktil_tm[:, h * 64:(h + 1) * 64],
                                           v1[:, h * 65:(h + 1) * 65], start=True, stop=True)
                    mm.then_inc(hT, 1)

            BIN = b_in + KIN1 * NB
            BT = b_t + KT1 * NB
            BV = b_v + KV1 * NB
            BS = b_s + KS1 * NB

            b_v = BV
            @block.vector
            def _(vector):
                vector.reciprocal(rz8[:], z8[:])
                vector.tensor_copy(out=colp[:, 3:4], in_=onescol[:])
                for h in range(16):
                    mm = vector.tensor_scalar_mul(kvnb[:, h * 64:(h + 1) * 64],
                                                  kv_sb[:, h * 65:h * 65 + 64], rz8[:, h:h + 1])
                mm.then_inc(hV, 1)
            BV = b_v + 1

            b_in, b_out, b_t, b_v, b_s = BIN, BOUT, BT, BV, BS
            KIN2, KOUT2, KT2, KV2, KS2 = 16, 16, 8, 10, 6
            G_in = lambda b: b_in + KIN2 * b + 16
            G_out = lambda b: b_out + KOUT2 * (b + 1)
            G_t = lambda b, m: b_t + KT2 * b + m
            G_v = lambda b, m: b_v + KV2 * b + m
            G_s = lambda b, m: b_s + KS2 * b + m

            @block.sync
            def _(sync):
                sync.wait_ge(hV, b_v)
                sync.wait_ge(hS, b_s)
                for b in range(NB):
                    if b >= 2:
                        sync.wait_ge(hV, G_v(b - 2, 10))
                        sync.wait_ge(hOUT, G_out(b - 2) + 16)
                    sync.dma_start(out=xb2[b % 2][:], in_=xsrc[b * P:(b + 1) * P, :]).then_inc(hIN, 16)

            @block.gpsimd
            def _(g):
                for b in range(NB):
                    g.wait_ge(hV, G_v(b, 10))
                    g.dma_start(out=xdst[b * P:(b + 1) * P, :], in_=xb2[b % 2][:]).then_inc(hOUT, 16)

            @block.vector
            def _(vector):
                for b in range(NB):
                    xb = xb2[b % 2]
                    vector.wait_ge(hIN, G_in(b) + (16 if b == 0 and NB > 1 else 0))
                    if b >= 1:
                        vector.wait_ge(hT, G_t(b - 1, 1))
                    nc.vector.tensor_scalar_sub(xlnc[:], xb[:], nmu_c[:, b:b + 1]).then_inc(hV, 1)
                    vector.wait_ge(hT, G_t(b, 1))
                    vector.tensor_copy(out=xlnT[:, 0:512], in_=psb[4][:, :])
                    nc.vector.tensor_copy(out=xlnT[:, 512:1024], in_=psb[5][:, :]).then_inc(hV, 1)
                    vector.wait_ge(hT, G_t(b, 3))
                    for j in range(4):
                        mm = vector.tensor_copy(out=qfmT[:, j * 512:(j + 1) * 512], in_=psb[2 + j][0:64, :])
                    mm.then_inc(hV, 1)
                    vector.wait_ge(hT, G_t(b, 4))
                    vector.tensor_reduce(colp[:, 2:3], psb[2][:, :], AX.X, ALU.add)
                    vector.tensor_reduce(colp[:, 3:4], psb[3][:, :], AX.X, ALU.add)
                    vector.tensor_copy(out=ght[:], in_=psb[6][:, 0:64])
                    nc.vector.tensor_add(colp[:, 4:5], colp[:, 2:3], colp[:, 3:4]).then_inc(hV, 1)
                    vector.wait_ge(hS, G_s(b, 3))
                    nc.vector.scalar_tensor_tensor(colp[:, 7:8], colp[:, 8:9], 1.0 / 1024.0,
                                                   colp[:, 6:7], ALU.mult, ALU.subtract).then_inc(hV, 1)
                    vector.wait_ge(hS, G_s(b, 4))
                    nc.vector.reciprocal(colp[:, 10:11], colp[:, 9:10]).then_inc(hV, 1)
                    vector.wait_ge(hT, G_t(b, 5))
                    nc.vector.tensor_copy(out=ghTb[:], in_=psb[7][0:64, 0:P]).then_inc(hV, 1)
                    vector.wait_ge(hS, G_s(b, 5))
                    vector.wait_ge(hS, G_s(b, 6))
                    vector.tensor_tensor(gout[:], olnt[:], lngb[:], ALU.mult)
                    vector.tensor_add(gout[:], gout[:], lnbb[:])
                    nc.vector.tensor_tensor(gout[:], gout[:], gsig[:], ALU.mult).then_inc(hV, 1)
                    vector.wait_ge(hT, G_t(b, 7))
                    vector.tensor_copy(out=goutT[:, 0:512], in_=psb[6][:, :])
                    nc.vector.tensor_copy(out=goutT[:, 512:1024], in_=psb[7][:, :]).then_inc(hV, 1)
                    vector.wait_ge(hT, G_t(b, 8))
                    vector.tensor_add(xb[:, 0:512], psb[0][:, :], xb[:, 0:512])
                    nc.vector.tensor_add(xb[:, 512:1024], psb[1][:, :], xb[:, 512:1024]).then_inc(hV, 1)

            @block.scalar
            def _(scalar):
                scalar.wait_ge(hV, b_v)
                for b in range(NB):
                    scalar.wait_ge(hV, G_v(b, 1))
                    nc.scalar.activation(out=xlnc[:], in_=xlnc[:], func=AF.Identity,
                                         scale=rstdc[:, b:b + 1]).then_inc(hS, 1)
                    scalar.wait_ge(hT, G_t(b, 2))
                    nc.scalar.activation(out=qsil[:, 0:512], in_=psb[0][:, :], func=AF.Silu)
                    nc.scalar.activation(out=qsil[:, 512:1024], in_=psb[1][:, :],
                                         func=AF.Silu).then_inc(hS, 1)
                    scalar.wait_ge(hV, G_v(b, 4))
                    nc.scalar.activation(out=sdu[:], in_=psb[2][:, :], func=AF.Square,
                                         accum_out=colp[:, 8:9])
                    nc.scalar.activation(out=sdu[:], in_=psb[3][:, :], func=AF.Square,
                                         accum_out=colp[:, 2:3])
                    nc.scalar.activation(out=colp[:, 1:2], in_=onescol[:], func=AF.Identity)
                    nc.scalar.activation(out=colp[:, 8:9], in_=colp[:, 8:9], func=AF.Identity,
                                         bias=colp[:, 2:3])
                    nc.scalar.activation(out=colp[:, 5:6], in_=colp[:, 4:5], func=AF.Identity,
                                         scale=1.0 / 1024.0)
                    nc.scalar.activation(out=colp[:, 6:7], in_=colp[:, 5:6],
                                         func=AF.Square).then_inc(hS, 1)
                    scalar.wait_ge(hV, G_v(b, 5))
                    nc.scalar.activation(out=colp[:, 9:10], in_=colp[:, 7:8], func=AF.Sqrt,
                                         bias=epscol[:, 0:1]).then_inc(hS, 1)
                    scalar.wait_ge(hV, G_v(b, 6))
                    nc.scalar.activation(out=colp[:, 11:12], in_=colp[:, 5:6], func=AF.Identity,
                                         scale=colp[:, 10:11])
                    nc.scalar.activation(out=colp[:, 1:2], in_=onescol[:], func=AF.Identity)
                    nc.scalar.activation(out=colp[:, 11:12], in_=colp[:, 11:12], func=AF.Identity,
                                         scale=-1.0)
                    nc.scalar.activation(out=colp[:, 1:2], in_=onescol[:], func=AF.Identity)
                    nc.scalar.activation(out=olnt[:, 0:512], in_=psb[2][:, :], func=AF.Identity,
                                         scale=colp[:, 10:11], bias=colp[:, 11:12])
                    nc.scalar.activation(out=olnt[:, 512:1024], in_=psb[3][:, :], func=AF.Identity,
                                         scale=colp[:, 10:11], bias=colp[:, 11:12]).then_inc(hS, 1)
                    scalar.wait_ge(hT, G_t(b, 6))
                    nc.scalar.activation(out=gsig[:, 0:512], in_=psb[4][:, :], func=AF.Sigmoid)
                    nc.scalar.activation(out=gsig[:, 512:1024], in_=psb[5][:, :],
                                         func=AF.Sigmoid).then_inc(hS, 1)

            @block.tensor
            def _(tensor):
                for b in range(NB):
                    tensor.wait_ge(hS, G_s(b, 1))
                    if b >= 1:
                        tensor.wait_ge(hS, G_s(b - 1, 6))
                    for f in range(8):
                        mm = tensor.transpose(psb[4 + f // 4][:, (f % 4) * P:(f % 4 + 1) * P],
                                              xlnc[:, f * P:(f + 1) * P], id_f[:])
                    mm.then_inc(hT, 1)
                    tensor.wait_ge(hV, G_v(b, 2))
                    if b >= 1:
                        tensor.wait_ge(hV, G_v(b - 1, 10))
                    for f in range(8):
                        st_ = (f == 0)
                        o = f * 3072
                        tensor.matmul(psb[0][:, :], xlnT[:, f * P:(f + 1) * P], qkvw[:, o:o + 512], start=st_, stop=False)
                        tensor.matmul(psb[1][:, :], xlnT[:, f * P:(f + 1) * P], qkvw[:, o + 512:o + 1024], start=st_, stop=False)
                    tensor.matmul(psb[0][:, :], onesb[:], brow[0:1, 0:512], start=False, stop=True)
                    mm = tensor.matmul(psb[1][:, :], onesb[:], brow[0:1, 512:1024], start=False, stop=True)
                    mm.then_inc(hT, 1)
                    tensor.wait_ge(hS, G_s(b, 2))
                    if b >= 1:
                        tensor.wait_ge(hS, G_s(b - 1, 5))
                    for h in range(16):
                        mm = tensor.transpose(psb[2 + h // 4][0:64, (h % 4) * P:(h % 4 + 1) * P],
                                              qsil[:, h * 64:(h + 1) * 64], id_f[:])
                    mm.then_inc(hT, 1)
                    tensor.wait_ge(hV, G_v(b, 3))
                    if b >= 1:
                        tensor.wait_ge(hS, G_s(b - 1, 5))
                    for h in range(16):
                        pb_ = psb[2 + h // 8]
                        c0 = (h % 8) * 64
                        lhs = qfmT[0:64, h * P:(h + 1) * P]
                        tensor.matmul(pb_[:, c0:c0 + 64], lhs,
                                      kvnb[0:64, h * 64:(h + 1) * 64],
                                      start=True, stop=True)
                    for f in range(8):
                        st_ = (f == 0)
                        tensor.matmul(psb[6][:, 0:64], xlnT[:, f * P:(f + 1) * P],
                                      wg1w[:, f * 64:(f + 1) * 64], start=st_, stop=False)
                    mm = tensor.matmul(psb[6][:, 0:64], onesb[:], bg1r[0:1, 0:64], start=False, stop=True)
                    mm.then_inc(hT, 1)
                    tensor.wait_ge(hV, G_v(b, 4))
                    tensor.transpose(psb[7][0:64, 0:P], ght[:], id_f[:]).then_inc(hT, 1)
                    tensor.wait_ge(hV, G_v(b, 7))
                    tensor.matmul(psb[4][:, :], ghTb[:], wg2w[:, 0:512], start=True, stop=False)
                    tensor.matmul(psb[5][:, :], ghTb[:], wg2w[:, 512:1024], start=True, stop=False)
                    tensor.matmul(psb[4][:, :], onesb[:], bg2r[0:1, 0:512], start=False, stop=True)
                    mm = tensor.matmul(psb[5][:, :], onesb[:], bg2r[0:1, 512:1024], start=False, stop=True)
                    mm.then_inc(hT, 1)
                    tensor.wait_ge(hV, G_v(b, 8))
                    for f in range(8):
                        mm = tensor.transpose(psb[6 + f // 4][:, (f % 4) * P:(f % 4 + 1) * P],
                                              gout[:, f * P:(f + 1) * P], id_f[:])
                    mm.then_inc(hT, 1)
                    tensor.wait_ge(hV, G_v(b, 9))
                    for f in range(8):
                        st_ = (f == 0)
                        o = f * 1024
                        tensor.matmul(psb[0][:, :], goutT[:, f * P:(f + 1) * P], woutw[:, o:o + 512], start=st_, stop=False)
                        tensor.matmul(psb[1][:, :], goutT[:, f * P:(f + 1) * P], woutw[:, o + 512:o + 1024], start=st_, stop=False)
                    tensor.matmul(psb[0][:, :], onesb[:], botr[0:1, 0:512], start=False, stop=True)
                    mm = tensor.matmul(psb[1][:, :], onesb[:], botr[0:1, 512:1024], start=False, stop=True)
                    mm.then_inc(hT, 1)

            BIN = b_in + KIN2 * NB
            BOUT = b_out + KOUT2 * NB
            BT = b_t + KT2 * NB
            BV = b_v + KV2 * NB
            BS = b_s + KS2 * NB
            cctx.close()

        if "C" in phases:
            emit_attnb(io["C"][0], io["C"][1])
        if "D" in phases:
            emit_ffn("fd", io["D"][0], io["D"][1], use_stash=("B" in phases))

        @block.sync
        def _(sync):
            sync.wait_ge(hOUT, BOUT)

    return nc



def _host_tables(n_blk):
    p = np.arange(P, dtype=np.float32)
    blk = np.arange(n_blk, dtype=np.float32)
    ipos1 = (blk[None, :] * P + p[:, None] + 1.0).astype(np.float32)
    posidx = (blk[None, :] * P + p[:, None]).astype(np.float32)
    jrev = (127.0 - p)[:, None].astype(np.float32)
    ip1 = (p + 1.0)[:, None].astype(np.float32)
    pairidx = np.repeat(np.arange(64, dtype=np.float32), 2)[None, :]
    i_idx = np.arange(P, dtype=np.float32)
    diffT = (i_idx[None, :] - i_idx[:, None]).astype(np.float32)
    causT = (i_idx[None, :] >= i_idx[:, None]).astype(np.float32)
    idf = np.eye(P, dtype=np.float32)
    return dict(ipos1=ipos1, posidx=posidx, jrev=jrev, ip1=ip1, pairidx=pairidx,
                diffT=diffT, causT=causT, idf=idf)


def make_inputs_for_core(core_inputs, n_blk=32):
    t = _host_tables(n_blk)
    c = lambda a: np.ascontiguousarray(a, dtype=np.float32)
    m = {
        "x": c(core_inputs["x"][: n_blk * P]),
        "wq": c(core_inputs["wq"]), "wk": c(core_inputs["wk"]),
        "wv": c(core_inputs["wv"]), "wg": c(core_inputs["wg"]), "wo": c(core_inputs["wo"]),
        "wgate": c(core_inputs["w_gate"].T), "wup": c(core_inputs["w_up"].T),
        "wdown": c(core_inputs["w_down"].T),
        "wqkv": c(core_inputs["w_qkv"].T), "bqkv": c(core_inputs["b_qkv"][None, :]),
        "wg1": c(core_inputs["w_g1"].T), "bg1": c(core_inputs["b_g1"][None, :]),
        "wg2": c(core_inputs["w_g2"].T), "bg2": c(core_inputs["b_g2"][None, :]),
        "wout": c(core_inputs["w_out"].T), "bout": c(core_inputs["b_out"][None, :]),
        "lng": c(core_inputs["ln_g"][None, :]), "lnb": c(core_inputs["ln_b"][None, :]),
    }
    m.update({k: np.ascontiguousarray(v) for k, v in t.items()})
    return m


def kernel(L0_attn_a_x, L0_attn_a_wq, L0_attn_a_wk, L0_attn_a_wv, L0_attn_a_wg,
           L0_attn_a_wo, L0_attn_b_w_qkv, L0_attn_b_b_qkv, L0_attn_b_w_g1,
           L0_attn_b_b_g1, L0_attn_b_w_g2, L0_attn_b_b_g2, L0_attn_b_w_out,
           L0_attn_b_b_out, L0_attn_b_ln_g, L0_attn_b_ln_b, L0_ffn_w_gate,
           L0_ffn_w_up, L0_ffn_w_down):
    full = dict(
        wq=L0_attn_a_wq, wk=L0_attn_a_wk, wv=L0_attn_a_wv, wg=L0_attn_a_wg,
        wo=L0_attn_a_wo, w_qkv=L0_attn_b_w_qkv, b_qkv=L0_attn_b_b_qkv,
        w_g1=L0_attn_b_w_g1, b_g1=L0_attn_b_b_g1, w_g2=L0_attn_b_w_g2,
        b_g2=L0_attn_b_b_g2, w_out=L0_attn_b_w_out, b_out=L0_attn_b_b_out,
        ln_g=L0_attn_b_ln_g, ln_b=L0_attn_b_ln_b, w_gate=L0_ffn_w_gate,
        w_up=L0_ffn_w_up, w_down=L0_ffn_w_down,
    )
    nblk = 32
    nc = build_nc(n_blk=nblk, phases=("A", "B", "C", "D"))
    in_maps = []
    for core in range(8):
        ci = dict(full)
        ci["x"] = np.ascontiguousarray(L0_attn_a_x[core])
        in_maps.append(make_inputs_for_core(ci, n_blk=nblk))
    res = run_bass_kernel_spmd(nc, in_maps, core_ids=list(range(8)))
    out = np.stack([np.asarray(r["y"]) for r in res.results], axis=0)
    return out.astype(np.float32)


# version 08
# speedup vs baseline: 1.0268x; 1.0005x over previous
import math
from contextlib import ExitStack

import numpy as np

import concourse.bass as bass
import concourse.mybir as mybir
from concourse.bass_utils import run_bass_kernel_spmd

F32 = mybir.dt.float32
BF16 = mybir.dt.bfloat16
AF = mybir.ActivationFunctionType
ALU = mybir.AluOpType
AX = mybir.AxisListType

P = 128
DM = 1024

_TWO_PI = 2.0 * math.pi
CW_C1 = 6.28125
CW_C2 = float(int((_TWO_PI - CW_C1) * 2 ** 21) / 2 ** 21)
CW_C3 = float(np.float32(_TWO_PI - CW_C1 - CW_C2))
INV_2PI = float(np.float32(1.0 / _TWO_PI))
EPS = 1e-5


def _stair(vec, t, g, nbits):
    for k in range(nbits - 1, -1, -1):
        h = float(2 ** k)
        vec.tensor_single_scalar(g, t, h - 0.5, ALU.is_ge)
        vec.scalar_tensor_tensor(t, g, -h, t, ALU.mult, ALU.add)


def _ev(ap2d):
    return ap2d.rearrange("p (c two) -> p c two", two=2)[:, :, 0:1]


def _od(ap2d):
    return ap2d.rearrange("p (c two) -> p c two", two=2)[:, :, 1:2]


def build_nc(n_blk=32, phases=("A", "B", "C", "D"), dbg=False):
    nc = bass.Bass("TRN2", target_bir_lowering=False, detect_race_conditions=False)
    NT = n_blk * P
    NB = n_blk

    x_in = nc.dram_tensor("x", [NT, DM], F32, kind="ExternalInput").ap()
    y_out = nc.dram_tensor("y", [NT, DM], F32, kind="ExternalOutput").ap()
    wq_d = nc.dram_tensor("wq", [DM, DM], F32, kind="ExternalInput").ap()
    wk_d = nc.dram_tensor("wk", [DM, DM], F32, kind="ExternalInput").ap()
    wv_d = nc.dram_tensor("wv", [DM, 2048], F32, kind="ExternalInput").ap()
    wg_d = nc.dram_tensor("wg", [DM, 2048], F32, kind="ExternalInput").ap()
    wo_d = nc.dram_tensor("wo", [2048, DM], F32, kind="ExternalInput").ap()
    wgate_d = nc.dram_tensor("wgate", [DM, 4096], F32, kind="ExternalInput").ap()
    wup_d = nc.dram_tensor("wup", [DM, 4096], F32, kind="ExternalInput").ap()
    wdown_d = nc.dram_tensor("wdown", [4096, DM], F32, kind="ExternalInput").ap()
    wqkv_d = nc.dram_tensor("wqkv", [DM, 3072], F32, kind="ExternalInput").ap()
    bqkv_d = nc.dram_tensor("bqkv", [1, 3072], F32, kind="ExternalInput").ap()
    wg1_d = nc.dram_tensor("wg1", [DM, 64], F32, kind="ExternalInput").ap()
    bg1_d = nc.dram_tensor("bg1", [1, 64], F32, kind="ExternalInput").ap()
    wg2_d = nc.dram_tensor("wg2", [64, DM], F32, kind="ExternalInput").ap()
    bg2_d = nc.dram_tensor("bg2", [1, DM], F32, kind="ExternalInput").ap()
    wout_d = nc.dram_tensor("wout", [DM, DM], F32, kind="ExternalInput").ap()
    bout_d = nc.dram_tensor("bout", [1, DM], F32, kind="ExternalInput").ap()
    lng_d = nc.dram_tensor("lng", [1, DM], F32, kind="ExternalInput").ap()
    lnb_d = nc.dram_tensor("lnb", [1, DM], F32, kind="ExternalInput").ap()
    ipos1_d = nc.dram_tensor("ipos1", [P, NB], F32, kind="ExternalInput").ap()
    posidx_d = nc.dram_tensor("posidx", [P, NB], F32, kind="ExternalInput").ap()
    jrev_d = nc.dram_tensor("jrev", [P, 1], F32, kind="ExternalInput").ap()
    ip1_d = nc.dram_tensor("ip1", [P, 1], F32, kind="ExternalInput").ap()
    pairidx_d = nc.dram_tensor("pairidx", [1, P], F32, kind="ExternalInput").ap()
    diffT_d = nc.dram_tensor("diffT", [P, P], F32, kind="ExternalInput").ap()
    causT_d = nc.dram_tensor("causT", [P, P], F32, kind="ExternalInput").ap()
    idf_d = nc.dram_tensor("idf", [P, P], F32, kind="ExternalInput").ap()
    xA_d = nc.dram_tensor("xA", [NT, DM], F32, kind="Internal").ap()
    xB_d = nc.dram_tensor("xB", [NT, DM], F32, kind="Internal").ap()
    xC_d = nc.dram_tensor("xC", [NT, DM], F32, kind="Internal").ap()
    sin_d = nc.dram_tensor("sin_t", [NT, P], BF16, kind="Internal").ap()
    cos_d = nc.dram_tensor("cos_t", [NT, P], BF16, kind="Internal").ap()
    hq_d = nc.dram_tensor("hq", [NB * 4 * P, 1024], BF16, kind="Internal").ap()
    tgate_d = nc.dram_tensor("tgate", [DM, 4096], BF16, kind="Internal").ap()
    tup_d = nc.dram_tensor("tup", [DM, 4096], BF16, kind="Internal").ap()
    tdown_d = nc.dram_tensor("tdown", [4096, DM], BF16, kind="Internal").ap()

    io = {}
    src = x_in
    scratch_for = {"A": xA_d, "B": xB_d, "C": xC_d, "D": None}
    for ph in phases:
        dst = y_out if ph == phases[-1] else scratch_for[ph]
        io[ph] = (src, dst)
        src = dst

    has_ffn = ("B" in phases) or ("D" in phases)
    hasA = "A" in phases

    top = ExitStack()
    with top:
        sbuf = lambda name, shape, dt=F32: top.enter_context(nc.sbuf_tensor(name, shape, dt))
        id_f = sbuf("id_f", [P, P])
        id_b = sbuf("id_b", [P, P], BF16)
        onesb = sbuf("onesb", [1, P], BF16)
        onesf = sbuf("onesf", [1, P])
        onescol = sbuf("onescol", [P, 1])
        dcol8 = sbuf("dcol8", [P, 8])
        ccol8 = sbuf("ccol8", [P, 8])
        l128c8 = sbuf("l128c8", [P, 8])
        rsR = sbuf("rsR", [P, 8 * NB])
        D8 = sbuf("D8", [P, 8 * P])
        dhc = sbuf("dhc", [P, 8])
        rs128c = sbuf("rs128c", [P, 1])
        wscol = sbuf("wscol", [P, 8])
        nmu_c = sbuf("nmu_c", [P, NB])
        rstdc = sbuf("rstdc", [P, NB])
        shc = sbuf("shc", [P, NB])
        colp = sbuf("colp", [P, 20])
        epscol = sbuf("epscol", [P, 1])
        monecol = sbuf("monecol", [P, 1])
        negepscol = sbuf("negepscol", [P, 1])
        nthrcol = sbuf("nthrcol", [P, 1])
        prow = sbuf("prow", [1, 336])
        psb = [top.enter_context(nc.psum_tensor(f"pb{i}", [P, 512], F32)) for i in range(8)]

        hIN = top.enter_context(nc.semaphore("sIN"))
        hOUT = top.enter_context(nc.semaphore("sOUT"))
        hT = top.enter_context(nc.semaphore("sT"))
        hV = top.enter_context(nc.semaphore("sV"))
        hS = top.enter_context(nc.semaphore("sS"))
        hQs = top.enter_context(nc.semaphore("sQs"))
        hQv = top.enter_context(nc.semaphore("sQv"))
        SEMO = {"T": hT, "V": hV, "S": hS}

        block = top.enter_context(nc.Block())

        aw = awo = Ssb = Sbf = None
        actx = ExitStack()
        asb = lambda name, shape, dt=F32: actx.enter_context(nc.sbuf_tensor(name, shape, dt))
        if hasA:
            aw = asb("aw", [P, 8 * 6144], BF16)
            awo = asb("awo", [P, 16 * DM], BF16)
            Ssb = asb("Ssb", [P, 8 * 257])
            Sbf = asb("Sbf", [P, 8 * 257], BF16)
        stg_ctx = ExitStack()
        psbuf = lambda name, shape, dt=F32: stg_ctx.enter_context(nc.sbuf_tensor(name, shape, dt))
        ipos1 = psbuf("ipos1_s", [P, NB])
        posidx = psbuf("posidx_s", [P, NB])
        jrev = psbuf("jrev_s", [P, 1])
        ip1 = psbuf("ip1_s", [P, 1])
        pairidx = psbuf("pairidx_s", [1, P])
        diffT = psbuf("diffT_s", [P, P])
        causT = psbuf("causT_s", [P, P])
        scg = [psbuf(f"scg{i}", [P, P], BF16) for i in range(2)]
        tga = psbuf("tga", [P, P])
        tgb = psbuf("tgb", [P, P])
        tgc = psbuf("tgc", [P, P])
        targ = psbuf("targ", [P, P])
        tsc1 = psbuf("tsc1", [P, P])
        tsc2 = psbuf("tsc2", [P, max(P, NB)])
        tsc3 = psbuf("tsc3", [P, max(P, NB)])
        angb = psbuf("angb", [P, P])
        lamr = psbuf("lamr", [1, 8])
        wred = psbuf("wred", [P, 4])
        stgA = psbuf("stgA", [P, 6144])
        stgB = psbuf("stgB", [P, 6144])

        nWS = 48 if has_ffn else 0
        nQK = 8 if hasA else 0
        nWO = 16 if hasA else 0

        steps = []

        def st(eng):
            def deco(fn):
                steps.append((eng, fn))
                return None
            return deco

        @st("V")
        def _(v):
            v.memset(onesb[:], 1.0)
            v.memset(onesf[:], 1.0)
            v.memset(onescol[:], 1.0)
            v.memset(prow[0:1, 300:301], 63.0)
            v.memset(prow[0:1, 302:303], 10000.0)
            v.memset(prow[0:1, 305:306], 128.0)
            v.memset(epscol[:], EPS)
            v.memset(monecol[:], -1.0)
            v.memset(negepscol[:], -EPS)
            v.memset(nthrcol[:], -127.5)
            v.tensor_copy(out=id_b[:], in_=id_f[:])
            v.reciprocal(prow[0:1, 301:302], prow[0:1, 300:301])
            v.tensor_scalar_mul(colp[:, 0:1], onescol[:], 128.0)
            for h in range(8):
                v.memset(lamr[0:1, h:h + 1], 1.0 - 2.0 ** (-(5 + h)))
            return v.tensor_scalar_mul(prow[0:1, 0:128], pairidx[:], prow[0:1, 301:302])

        @st("S")
        def _(s):
            s.activation(out=prow[0:1, 303:304], in_=prow[0:1, 302:303], func=AF.Ln)
            s.activation(out=prow[0:1, 304:305], in_=prow[0:1, 303:304], func=AF.Identity, scale=-1.0)
            s.activation(out=prow[0:1, 128:256], in_=prow[0:1, 0:128], func=AF.Exp, scale=prow[0:1, 304:305])
            s.activation(out=prow[0:1, 256:264], in_=lamr[0:1, :], func=AF.Ln)
            return s.activation(out=prow[0:1, 306:307], in_=prow[0:1, 305:306], func=AF.Sqrt)

        @st("V")
        def _(v):
            return v.reciprocal(prow[0:1, 307:308], prow[0:1, 306:307])

        @st("T")
        def _(t):
            t.matmul(psb[0][:, 0:P], onesf[:], prow[0:1, 128:256], start=True, stop=True)
            t.matmul(psb[1][:, 0:8], onesf[:], prow[0:1, 256:264], start=True, stop=True)
            return t.matmul(psb[2][:, 0:1], onesf[:], prow[0:1, 307:308], start=True, stop=True)

        @st("V")
        def _(v):
            v.tensor_copy(out=angb[:], in_=psb[0][:, 0:P])
            v.tensor_copy(out=dhc[:], in_=psb[1][:, 0:8])
            return v.tensor_copy(out=rs128c[:], in_=psb[2][:, 0:1])

        if has_ffn:
            @st("T")
            def _(t):
                return t.matmul(psb[3][0:1, 0:4], onescol[:], wred[:, 0:4], start=True, stop=True)

            @st("V")
            def _(v):
                v.tensor_copy(out=prow[0:1, 264:268], in_=psb[3][0:1, 0:4])
                v.tensor_copy(out=prow[0:1, 310:311], in_=onescol[0:1, 0:1])
                v.tensor_scalar_mul(prow[0:1, 272:276], prow[0:1, 264:268], 1.0 / 4194304.0)
                v.tensor_copy(out=prow[0:1, 311:312], in_=onescol[0:1, 0:1])
                v.reciprocal(prow[0:1, 276:280], prow[0:1, 272:276])
                v.tensor_copy(out=prow[0:1, 312:313], in_=onescol[0:1, 0:1])
                return v.tensor_copy(out=prow[0:1, 280:284], in_=prow[0:1, 276:280])

            @st("T")
            def _(t):
                return t.matmul(psb[4][:, 0:8], onesf[:], prow[0:1, 272:280], start=True, stop=True)

            @st("V")
            def _(v):
                return v.tensor_copy(out=wscol[:], in_=psb[4][:, 0:8])

        for h in range(8):
            @st("S")
            def _(s, h=h):
                s.activation(out=tsc1[:, 0:P], in_=diffT[:], func=AF.Exp, scale=dhc[:, h:h + 1])
                s.activation(out=dcol8[:, h:h + 1], in_=jrev[:], func=AF.Exp, scale=dhc[:, h:h + 1])
                s.activation(out=ccol8[:, h:h + 1], in_=ip1[:], func=AF.Exp, scale=dhc[:, h:h + 1])
                s.activation(out=l128c8[:, h:h + 1], in_=colp[:, 0:1], func=AF.Exp, scale=dhc[:, h:h + 1])
                s.activation(out=colp[:, 1:2], in_=onescol[:], func=AF.Exp, scale=dhc[:, h:h + 1])
                return s.activation(out=tsc2[:, 0:NB], in_=ipos1[:], func=AF.Exp, scale=dhc[:, h:h + 1])

            @st("V")
            def _(v, h=h):
                v.tensor_tensor(D8[:, h * P:(h + 1) * P], tsc1[:, 0:P], causT[:], ALU.mult)
                v.tensor_scalar(colp[:, 2:3], colp[:, 1:2], -1.0, 1.0, ALU.mult, ALU.add)
                v.tensor_scalar(tsc3[:, 0:NB], tsc2[:, 0:NB], -1.0, 1.0, ALU.mult, ALU.add)
                v.reciprocal(colp[:, 3:4], colp[:, 2:3])
                v.tensor_copy(out=colp[:, 4:5], in_=onescol[:])
                return v.tensor_scalar_mul(tsc3[:, 0:NB], tsc3[:, 0:NB], colp[:, 3:4])

            @st("S")
            def _(s, h=h):
                return s.activation(out=tsc2[:, 0:NB], in_=tsc3[:, 0:NB], func=AF.Sqrt)

            @st("V")
            def _(v, h=h):
                return v.reciprocal(rsR[:, h * NB:(h + 1) * NB], tsc2[:, 0:NB])

        if hasA:
            @st("V")
            def _(v):
                v.memset(Ssb[:], 0.0)
                return v.memset(Sbf[:], 0.0)

        cT = sum(1 for e, _ in steps if e == "T")
        cV = sum(1 for e, _ in steps if e == "V")
        cS = sum(1 for e, _ in steps if e == "S")

        IN_TAB = 8 * 16
        IN_WS = lambda i: IN_TAB + 16 * (i + 1)
        IN_QK = lambda f: IN_TAB + 16 * nWS + 64 * (f + 1)
        IN_WO = lambda v: IN_TAB + 16 * nWS + 64 * nQK + 16 * (v + 1)
        IN_PRE_END = IN_TAB + 16 * nWS + 64 * nQK + 16 * nWO

        V_WS = lambda i: i + 1
        V_STEP_BASE = nWS
        V_QKV = lambda f: nWS + cV + f + 1
        V_WO = lambda v2: nWS + cV + nQK + v2 + 1
        V_SC = lambda b, k: nWS + cV + nQK + nWO + 3 * b + k
        V_PRE_END = nWS + cV + nQK + nWO + 3 * NB + 1
        S_SC = lambda b, k: cS + 2 * b + k
        S_PRE_END = cS + 2 * NB
        T_PRE_END = cT
        OUT_SC = lambda b: 32 * (b + 1)
        OUT_PRE_END = 32 * NB

        def emit_steps(eng_name, eng, base):
            cnt_local = {"T": 0, "V": 0, "S": 0}
            for i, (en, fn) in enumerate(steps):
                if en == eng_name:
                    if i > 0:
                        pe = steps[i - 1][0]
                        if pe != eng_name:
                            eng.wait_ge(SEMO[pe], base[pe] + cnt_local[pe])
                    last = fn(eng)
                    assert last is not None, f"step {i} ({en}) returned no instruction"
                    last.then_inc(SEMO[eng_name], 1)
                cnt_local[en] += 1

        STEP_BASE = {"T": 0, "V": V_STEP_BASE, "S": 0}

        @block.sync
        def _(sync):
            for d, s in ((idf_d, id_f), (ipos1_d, ipos1), (posidx_d, posidx),
                         (jrev_d, jrev), (ip1_d, ip1), (pairidx_d, pairidx),
                         (diffT_d, diffT), (causT_d, causT)):
                sync.dma_start(out=s[:], in_=d[:]).then_inc(hIN, 16)
            for i in range(nWS):
                buf = (stgA, stgB)[i % 2]
                if i >= 2:
                    sync.wait_ge(hV, V_WS(i - 2))
                if i < 8:
                    srcap, w = wgate_d[i * P:(i + 1) * P, :], 4096
                elif i < 16:
                    srcap, w = wup_d[(i - 8) * P:(i - 7) * P, :], 4096
                else:
                    srcap, w = wdown_d[(i - 16) * P:(i - 15) * P, :], 1024
                sync.dma_start(out=buf[:, 0:w], in_=srcap).then_inc(hIN, 16)
            for f in range(nQK):
                buf = (stgA, stgB)[f % 2]
                if f >= 2:
                    sync.wait_ge(hV, V_QKV(f - 2))
                elif nWS:
                    sync.wait_ge(hV, V_WS(nWS - 2 + f))
                sync.dma_start(out=buf[:, 0:1024], in_=wq_d[f * P:(f + 1) * P, :]).then_inc(hIN, 16)
                sync.dma_start(out=buf[:, 1024:2048], in_=wk_d[f * P:(f + 1) * P, :]).then_inc(hIN, 16)
                sync.dma_start(out=buf[:, 2048:4096], in_=wv_d[f * P:(f + 1) * P, :]).then_inc(hIN, 16)
                sync.dma_start(out=buf[:, 4096:6144], in_=wg_d[f * P:(f + 1) * P, :]).then_inc(hIN, 16)
            for v2 in range(nWO):
                buf = (stgA, stgB)[v2 % 2]
                if v2 >= 2:
                    sync.wait_ge(hV, V_WO(v2 - 2))
                else:
                    sync.wait_ge(hV, V_QKV(nQK - 2 + v2))
                sync.dma_start(out=buf[:, 0:1024], in_=wo_d[v2 * P:(v2 + 1) * P, :]).then_inc(hIN, 16)

        @block.vector
        def _(vector):
            vector.wait_ge(hIN, IN_TAB + (32 if nWS >= 2 else (128 if nQK else 0)))
            vector.memset(wred[:], 0.0)
            vector.memset(wred[:, 3:4], 1.0)
            for i in range(nWS):
                buf = (stgA, stgB)[i % 2]
                w = 4096 if i < 16 else 1024
                wi = 0 if i < 8 else (1 if i < 16 else 2)
                vector.wait_ge(hIN, IN_WS(i) + ((64 if nQK else 0) if i == nWS - 1 else 16))
                vector.tensor_reduce(colp[:, 5:6], buf[:, 0:w], AX.X, ALU.add, apply_absolute_value=True)
                vector.tensor_copy(out=colp[:, 6:7], in_=wred[:, 3:4])
                nc.vector.tensor_add(wred[:, wi:wi + 1], wred[:, wi:wi + 1], colp[:, 5:6]).then_inc(hV, 1)
            emit_steps("V", vector, STEP_BASE)
            for f in range(nQK):
                buf = (stgA, stgB)[f % 2]
                vector.wait_ge(hIN, IN_QK(f) + ((16 if nWO else 0) if f == nQK - 1 else 64))
                o = f * 6144
                vector.tensor_copy(out=aw[:, o:o + 1024], in_=buf[:, 0:1024])
                vector.tensor_scalar_mul(aw[:, o + 1024:o + 2048], buf[:, 1024:2048], rs128c[:, 0:1])
                vector.tensor_copy(out=aw[:, o + 2048:o + 4096], in_=buf[:, 2048:4096])
                nc.vector.tensor_copy(out=aw[:, o + 4096:o + 6144], in_=buf[:, 4096:6144]).then_inc(hV, 1)
            for v2 in range(nWO):
                buf = (stgA, stgB)[v2 % 2]
                vector.wait_ge(hIN, IN_WO(v2) + (0 if v2 == nWO - 1 else 16))
                nc.vector.tensor_copy(out=awo[:, v2 * 1024:(v2 + 1) * 1024], in_=buf[:, 0:1024]).then_inc(hV, 1)
            for b in range(NB):
                sb_s, sb_c = scg[0], scg[1]
                if b >= 1:
                    vector.wait_ge(hOUT, OUT_SC(b - 1))
                vector.tensor_scalar_mul(targ[:], angb[:], posidx[:, b:b + 1])
                vector.tensor_scalar_mul(tsc1[:, 0:P], targ[:], INV_2PI)
                vector.tensor_scalar_mul(tsc3[:, 0:P], targ[:], INV_2PI)
                _stair(vector, tsc1[:, 0:P], tsc2[:, 0:P], 10)
                vector.tensor_sub(tsc3[:, 0:P], tsc3[:, 0:P], tsc1[:, 0:P])
                vector.scalar_tensor_tensor(tsc2[:, 0:P], tsc3[:, 0:P], -CW_C1, targ[:], ALU.mult, ALU.add)
                vector.scalar_tensor_tensor(tsc2[:, 0:P], tsc3[:, 0:P], -CW_C2, tsc2[:, 0:P], ALU.mult, ALU.add)
                nc.vector.scalar_tensor_tensor(targ[:], tsc3[:, 0:P], -CW_C3, tsc2[:, 0:P], ALU.mult, ALU.add).then_inc(hV, 1)
                vector.wait_ge(hS, S_SC(b, 1))
                vector.tensor_tensor(tgb[:], tga[:], tga[:], ALU.mult)
                nc.vector.tensor_scalar(tgb[:], tgb[:], -1.0, 1.0, ALU.mult, ALU.add).then_inc(hV, 1)
                vector.wait_ge(hS, S_SC(b, 2))
                vector.scalar_tensor_tensor(tga[:], tga[:], 2.0, tgc[:], ALU.mult, ALU.mult)
                vector.tensor_scalar(tgb[:], tgb[:], 2.0, -1.0, ALU.mult, ALU.add)
                vector.scalar_tensor_tensor(sb_s[:], tga[:], 2.0, tgb[:], ALU.mult, ALU.mult)
                vector.tensor_tensor(tgc[:], tga[:], tga[:], ALU.mult)
                nc.vector.tensor_scalar(sb_c[:], tgc[:], -2.0, 1.0, ALU.mult, ALU.add).then_inc(hV, 1)
            nc.vector.engine_nop().then_inc(hV, 1)

        @block.scalar
        def _(scalar):
            emit_steps("S", scalar, STEP_BASE)
            for b in range(NB):
                scalar.wait_ge(hV, V_SC(b, 1))
                nc.scalar.activation(out=tga[:], in_=targ[:], func=AF.Sin, scale=0.25).then_inc(hS, 1)
                scalar.wait_ge(hV, V_SC(b, 2))
                nc.scalar.activation(out=tgc[:], in_=tgb[:], func=AF.Sqrt).then_inc(hS, 1)

        @block.tensor
        def _(tensor):
            emit_steps("T", tensor, STEP_BASE)

        @block.gpsimd
        def _(g):
            for b in range(NB):
                g.wait_ge(hV, V_SC(b, 3))
                sb_s, sb_c = scg[0], scg[1]
                g.dma_start(out=sin_d[b * P:(b + 1) * P, :], in_=sb_s[:]).then_inc(hOUT, 16)
                g.dma_start(out=cos_d[b * P:(b + 1) * P, :], in_=sb_c[:]).then_inc(hOUT, 16)

        BIN = IN_PRE_END
        BV = V_PRE_END
        BS = S_PRE_END
        BT = T_PRE_END
        BOUT = OUT_PRE_END

        V_CASTS_DONE = nWS + cV + nQK + nWO
        stg_ctx.close()
        if hasA:
            xsrc, xdst = io["A"]
            xblk = [asb("xblk0", [P, DM]), asb("xblk1", [P, DM])]
            sinrep = asb("sinrep", [P, DM], BF16)
            cosrep = asb("cosrep", [P, DM], BF16)
            xlnb = asb("xlnb", [P, DM])
            xlnT = asb("xlnT", [P, DM], BF16)
            qr_t = asb("qr_t", [P, DM])
            kr_t = asb("kr_t", [P, DM])
            krd = asb("krd", [P, DM], BF16)
            qrT = asb("qrT", [P, DM], BF16)
            krT = asb("krT", [P, DM], BF16)
            vr1 = asb("vr1", [P, 8 * 257], BF16)
            gsilu = asb("gsilu", [P, 2048], BF16)
            gated = asb("gated", [P, 2048])
            gatedT = asb("gatedT", [P, 2048], BF16)
            Mbf = asb("Mbf", [P, P], BF16)
            ocomb = asb("ocomb", [P, 257])
            osct = asb("osct", [P, 256])
            crossS = asb("crossS", [P, 257])
            dumpb = asb("dumpb", [P, 512], BF16)
            tmpe = asb("tmpe", [P, 256])
            tmpo = asb("tmpo", [P, 256])

            KIN, KOUT, KT, KV, KS = 48, 16, 22, 49, 28
            DBG_OUT = 22 * 16 if dbg else 0
            A_in = lambda b, k: BIN + KIN * b + 16 * k
            A_out = lambda b: BOUT + DBG_OUT + KOUT * (b + 1)
            A_t = lambda b, m: BT + KT * b + m
            A_v = lambda b, m: BV + KV * b + m
            V_oc = lambda h: 9 + 5 * h
            V_Su = lambda h: 10 + 5 * h
            V_inv = lambda h: 11 + 5 * h
            V_M = lambda h: 8 if h == 0 else 7 + 5 * h
            V_gate = lambda h: (13 + 5 * h) if h < 7 else 47
            A_s = lambda b, m: BS + KS * b + m
            S_cS = lambda h: 5 + 3 * h
            S_den = lambda h: 6 + 3 * h
            S_g2 = lambda h: 7 + 3 * h

            @block.sync
            def _(sync):
                sync.wait_ge(hV, V_PRE_END)
                sync.wait_ge(hOUT, OUT_PRE_END)
                for b in range(NB):
                    if b >= 2:
                        sync.wait_ge(hOUT, A_out(b - 2) + 16)
                    sync.dma_start(out=xblk[b % 2][:], in_=xsrc[b * P:(b + 1) * P, :]).then_inc(hIN, 16)
                    if b >= 1:
                        sync.wait_ge(hV, A_v(b - 1, 5))
                    sync.wait_ge(hOUT, OUT_SC(b))
                    sin_ap = bass.AP(sin_d.tensor, b * P * P, [[P, P], [0, 8], [1, P]])
                    cos_ap = bass.AP(cos_d.tensor, b * P * P, [[P, P], [0, 8], [1, P]])
                    srep = sinrep[:].rearrange("p (r c) -> p r c", r=8)
                    crep = cosrep[:].rearrange("p (r c) -> p r c", r=8)
                    sync.dma_start(out=srep, in_=sin_ap).then_inc(hIN, 16)
                    sync.dma_start(out=crep, in_=cos_ap).then_inc(hIN, 16)

            if dbg:
                dbg_specs = [
                    ("d_xlnb", xlnb, ("S", 3)), ("d_xlnT", xlnT, ("V", 4)),
                    ("d_qr", qr_t, ("V", 5)), ("d_kr", kr_t, ("V", 5)), ("d_krd", krd, ("V", 5)),
                    ("d_qrT", qrT, ("V", 6)), ("d_krT", krT, ("V", 6)),
                    ("d_vr1", vr1, ("V", 7)), ("d_gsilu", gsilu, ("S", 4)),
                    ("d_Mbf", Mbf, ("V", 8)), ("d_gated", gated, ("V", 48)),
                    ("d_Ssb", Ssb, ("V", 48)), ("d_colp", colp, ("V", 48)),
                    ("d_sinrep", sinrep, ("IN", 3)), ("d_cosrep", cosrep, ("IN", 3)),
                    ("d_rsR", rsR, ("V", 3)), ("d_ccol", ccol8, ("V", 3)),
                    ("d_dcol", dcol8, ("V", 3)), ("d_l128", l128c8, ("V", 3)),
                    ("d_D8", D8, ("V", 3)), ("d_ocomb", ocomb, ("V", 48)),
                    ("d_osct", osct, ("V", 48)), ("d_crossS", crossS, ("V", 48)),
                ]
                dbg_outs = {}
                for nm, tile, _w in dbg_specs:
                    shape = list(tile[:].shape)
                    dt = F32 if nm not in ("d_xlnb", "d_xlnT", "d_qr2", "d_krd", "d_qrT", "d_krT",
                                           "d_vr1", "d_gsilu", "d_Mbf", "d_sinrep", "d_cosrep") else F32
                    dtt = tile[:].dtype
                    dbg_outs[nm] = nc.dram_tensor(nm, shape, dtt, kind="ExternalOutput").ap()

            @block.gpsimd
            def _(g):
                if dbg:
                    for nm, tile, (sem, val) in dbg_specs:
                        hh = {"S": hS, "V": hV, "T": hT, "IN": hIN}[sem]
                        base = {"S": A_s, "V": A_v, "T": A_t, "IN": A_in}[sem]
                        g.wait_ge(hh, base(0, val))
                        g.dma_start(out=dbg_outs[nm], in_=tile[:]).then_inc(hOUT, 16)
                for b in range(NB):
                    g.wait_ge(hV, A_v(b, 49))
                    g.dma_start(out=xdst[b * P:(b + 1) * P, :], in_=xblk[b % 2][:]).then_inc(hOUT, 16)

            @block.tensor
            def _(tensor):
                for b in range(NB):
                    tensor.wait_ge(hS, A_s(b, 3))
                    if b >= 1:
                        tensor.wait_ge(hS, A_s(b - 1, 4))
                    for f in range(8):
                        mm = tensor.transpose(psb[f // 4][:, (f % 4) * P:(f % 4 + 1) * P],
                                              xlnb[:, f * P:(f + 1) * P], id_f[:])
                    mm.then_inc(hT, 1)
                    tensor.wait_ge(hV, A_v(b, 4))
                    for f in range(8):
                        st_, sp = (f == 0), (f == 7)
                        o = f * 6144
                        tensor.matmul(psb[2][:, :], xlnT[:, f * P:(f + 1) * P], aw[:, o:o + 512], start=st_, stop=sp)
                        tensor.matmul(psb[3][:, :], xlnT[:, f * P:(f + 1) * P], aw[:, o + 512:o + 1024], start=st_, stop=sp)
                        tensor.matmul(psb[4][:, :], xlnT[:, f * P:(f + 1) * P], aw[:, o + 1024:o + 1536], start=st_, stop=sp)
                        mm = tensor.matmul(psb[5][:, :], xlnT[:, f * P:(f + 1) * P], aw[:, o + 1536:o + 2048], start=st_, stop=sp)
                    mm.then_inc(hT, 1)
                    tensor.wait_ge(hV, A_v(b, 5))
                    for f in range(8):
                        tensor.transpose(psb[f // 4][:, (f % 4) * P:(f % 4 + 1) * P],
                                         qr_t[:, f * P:(f + 1) * P], id_f[:])
                    for f in range(8):
                        mm = tensor.transpose(psb[6 + f // 4][:, (f % 4) * P:(f % 4 + 1) * P],
                                              kr_t[:, f * P:(f + 1) * P], id_f[:])
                    mm.then_inc(hT, 1)
                    tensor.wait_ge(hV, A_v(b, 6))
                    for f in range(8):
                        st_, sp = (f == 0), (f == 7)
                        o = f * 6144 + 2048
                        for j in range(4):
                            tensor.matmul(psb[2 + j][:, :], xlnT[:, f * P:(f + 1) * P],
                                          aw[:, o + j * 512:o + (j + 1) * 512], start=st_, stop=sp)
                        o = f * 6144 + 4096
                        for j, pbix in enumerate((0, 1, 6, 7)):
                            mm = tensor.matmul(psb[pbix][:, :], xlnT[:, f * P:(f + 1) * P],
                                               aw[:, o + j * 512:o + (j + 1) * 512], start=st_, stop=sp)
                    mm.then_inc(hT, 1)
                    tensor.wait_ge(hV, A_v(b, 7))
                    def hbanks(h):
                        return (psb[2], psb[3], psb[4], psb[5]) if h % 2 == 0 else (psb[0], psb[1], psb[6], psb[7])
                    pA, pC, pAV, pU = hbanks(0)
                    tensor.matmul(pA[:, 0:P], krT[:, 0:P], qrT[:, 0:P], start=True, stop=True).then_inc(hT, 1)
                    for h in range(8):
                        pA, pC, pAV, pU = hbanks(h)
                        tensor.wait_ge(hV, A_v(b, V_M(h)))
                        sl = slice(h * 257, (h + 1) * 257)
                        tensor.matmul(pC[:, 0:257], qrT[:, h * P:(h + 1) * P], Sbf[:, sl], start=True, stop=True)
                        tensor.matmul(pAV[:, 0:257], Mbf[:], vr1[:, sl], start=True, stop=True)
                        mm = tensor.matmul(pU[:, 0:257], krd[:, h * P:(h + 1) * P], vr1[:, sl], start=True, stop=True)
                        mm.then_inc(hT, 1)
                        if h < 7:
                            if h == 0:
                                tensor.wait_ge(hS, A_s(b, 4))
                            else:
                                tensor.wait_ge(hV, A_v(b, V_Su(h - 1)))
                                tensor.wait_ge(hS, A_s(b, S_cS(h - 1)))
                            pA2 = hbanks(h + 1)[0]
                            tensor.matmul(pA2[:, 0:P], krT[:, (h + 1) * P:(h + 2) * P],
                                          qrT[:, (h + 1) * P:(h + 2) * P], start=True, stop=True).then_inc(hT, 1)
                    tensor.wait_ge(hV, A_v(b, V_gate(7)))
                    tensor.wait_ge(hS, A_s(b, S_cS(6)))
                    for j in range(16):
                        mm = tensor.transpose(psb[2 + j // 4][:, (j % 4) * P:(j % 4 + 1) * P],
                                              gated[:, j * P:(j + 1) * P], id_f[:])
                    mm.then_inc(hT, 1)
                    tensor.wait_ge(hV, A_v(b, 48))
                    tensor.wait_ge(hS, A_s(b, S_cS(7)))
                    for c in range(16):
                        st_, sp = (c == 0), (c == 15)
                        tensor.matmul(psb[6][:, :], gatedT[:, c * P:(c + 1) * P],
                                      awo[:, c * 1024:c * 1024 + 512], start=st_, stop=sp)
                        mm = tensor.matmul(psb[7][:, :], gatedT[:, c * P:(c + 1) * P],
                                           awo[:, c * 1024 + 512:(c + 1) * 1024], start=st_, stop=sp)
                    mm.then_inc(hT, 1)

            @block.vector
            def _(vector):
                for h in range(8):
                    vector.memset(vr1[:, h * 257 + 256:h * 257 + 257], 1.0)
                for b in range(NB):
                    xb = xblk[b % 2]
                    vector.wait_ge(hIN, A_in(b, 1) + (48 if b == 0 and NB > 1 else 32))
                    nc.vector.tensor_reduce(colp[:, 4:5], xb[:], AX.X, ALU.add).then_inc(hV, 1)
                    vector.wait_ge(hS, A_s(b, 1))
                    vector.scalar_tensor_tensor(colp[:, 7:8], colp[:, 8:9], 1.0 / 1024.0,
                                                colp[:, 6:7], ALU.mult, ALU.subtract)
                    if b >= 1:
                        vector.wait_ge(hT, A_t(b - 1, 1))
                    nc.vector.tensor_scalar_sub(xlnb[:], xb[:], colp[:, 5:6]).then_inc(hV, 1)
                    vector.wait_ge(hS, A_s(b, 2))
                    nc.vector.reciprocal(colp[:, 10:11], colp[:, 9:10]).then_inc(hV, 1)
                    vector.wait_ge(hT, A_t(b, 1))
                    vector.tensor_copy(out=xlnT[:, 0:512], in_=psb[0][:, :])
                    nc.vector.tensor_copy(out=xlnT[:, 512:1024], in_=psb[1][:, :]).then_inc(hV, 1)
                    vector.wait_ge(hT, A_t(b, 2))
                    vector.wait_ge(hIN, A_in(b, 3) + (16 if b == 0 and NB > 1 else 0))
                    for (pq, dst) in ((2, qr_t), (4, kr_t)):
                        for j in range(2):
                            pb = psb[pq + j]
                            c0 = j * 512
                            pe, po = _ev(pb[:, :]), _od(pb[:, :])
                            ce, co = _ev(cosrep[:, c0:c0 + 512]), _od(cosrep[:, c0:c0 + 512])
                            se, so = _ev(sinrep[:, c0:c0 + 512]), _od(sinrep[:, c0:c0 + 512])
                            vector.tensor_tensor(tmpe[:], pe, ce, ALU.mult)
                            vector.tensor_tensor(tmpo[:], po, se, ALU.mult)
                            vector.tensor_tensor(osct[:, 0:256], po, co, ALU.mult)
                            vector.tensor_tensor(crossS[:, 0:256], pe, so, ALU.mult)
                            vector.tensor_tensor(_ev(dst[:, c0:c0 + 512]), tmpe[:], tmpo[:], ALU.subtract)
                            vector.tensor_tensor(_od(dst[:, c0:c0 + 512]), osct[:, 0:256], crossS[:, 0:256], ALU.add)
                    for h in range(8):
                        mm = vector.tensor_scalar_mul(krd[:, h * P:(h + 1) * P],
                                                      kr_t[:, h * P:(h + 1) * P], dcol8[:, h:h + 1])
                    mm.then_inc(hV, 1)
                    vector.wait_ge(hT, A_t(b, 3))
                    vector.tensor_copy(out=qrT[:, 0:512], in_=psb[0][:, :])
                    vector.tensor_copy(out=qrT[:, 512:1024], in_=psb[1][:, :])
                    vector.tensor_copy(out=krT[:, 0:512], in_=psb[6][:, :])
                    nc.vector.tensor_copy(out=krT[:, 512:1024], in_=psb[7][:, :]).then_inc(hV, 1)
                    vector.wait_ge(hT, A_t(b, 4))
                    for j in range(4):
                        dst = vr1[:].rearrange("p (h c) -> p h c", c=257)[:, 2 * j:2 * j + 2, 0:256]
                        srcp = psb[2 + j][:, :].rearrange("p (h c) -> p h c", c=256)
                        mm = vector.tensor_copy(out=dst, in_=srcp)
                    mm.then_inc(hV, 1)
                    vector.wait_ge(hT, A_t(b, 5))
                    nc.vector.tensor_tensor(Mbf[:], psb[2][:, 0:P], D8[:, 0:P], ALU.mult).then_inc(hV, 1)
                    for h in range(8):
                        pA, pC, pAV, pU = ((psb[2], psb[3], psb[4], psb[5]) if h % 2 == 0
                                           else (psb[0], psb[1], psb[6], psb[7]))
                        sl = slice(h * 257, (h + 1) * 257)
                        vector.wait_ge(hT, A_t(b, 6 + 2 * h))
                        vector.wait_ge(hS, A_s(b, S_cS(h)))
                        nc.vector.tensor_add(ocomb[:], pAV[:, 0:257], crossS[:]).then_inc(hV, 1)
                        vector.wait_ge(hS, A_s(b, S_den(h)))
                        vector.reciprocal(colp[:, 18:19], colp[:, 17:18])
                        vector.scalar_tensor_tensor(Ssb[:, sl], Ssb[:, sl], l128c8[:, h:h + 1],
                                                    pU[:, 0:257], ALU.mult, ALU.add)
                        nc.vector.tensor_copy(out=Sbf[:, sl], in_=Ssb[:, sl]).then_inc(hV, 1)
                        nc.vector.tensor_tensor(colp[:, 12:13], colp[:, 18:19],
                                                rsR[:, h * NB + b:h * NB + b + 1], ALU.mult).then_inc(hV, 1)
                        vector.wait_ge(hS, A_s(b, S_g2(h)))
                        vector.reciprocal(colp[:, 15:16], colp[:, 14:15])
                        if h < 7:
                            vector.wait_ge(hT, A_t(b, 7 + 2 * h))
                            pA2 = (psb[2] if (h + 1) % 2 == 0 else psb[0])
                            nc.vector.tensor_tensor(Mbf[:], pA2[:, 0:P],
                                                    D8[:, (h + 1) * P:(h + 2) * P], ALU.mult).then_inc(hV, 1)
                        else:
                            vector.tensor_copy(out=colp[:, 19:20], in_=onescol[:])
                        nc.vector.scalar_tensor_tensor(gated[:, h * 256:(h + 1) * 256], osct[:],
                                                       colp[:, 15:16], gsilu[:, h * 256:(h + 1) * 256],
                                                       ALU.mult, ALU.mult).then_inc(hV, 1)
                    vector.wait_ge(hT, A_t(b, 21))
                    for j in range(4):
                        mm = vector.tensor_copy(out=gatedT[:, j * 512:(j + 1) * 512], in_=psb[2 + j][:, :])
                    mm.then_inc(hV, 1)
                    vector.wait_ge(hT, A_t(b, 22))
                    vector.tensor_add(xb[:, 0:512], psb[6][:, :], xb[:, 0:512])
                    nc.vector.tensor_add(xb[:, 512:1024], psb[7][:, :], xb[:, 512:1024]).then_inc(hV, 1)

            @block.scalar
            def _(scalar):
                for b in range(NB):
                    xb = xblk[b % 2]
                    scalar.wait_ge(hIN, A_in(b, 1) + (48 if b == 0 and NB > 1 else 32))
                    if b >= 1:
                        scalar.wait_ge(hV, A_v(b - 1, 2))
                    nc.scalar.activation(out=dumpb[:], in_=xb[:, 0:512], func=AF.Square,
                                         accum_out=colp[:, 2:3])
                    nc.scalar.activation(out=dumpb[:], in_=xb[:, 512:1024], func=AF.Square,
                                         accum_out=colp[:, 8:9])
                    nc.scalar.activation(out=colp[:, 1:2], in_=onescol[:], func=AF.Identity)
                    nc.scalar.activation(out=colp[:, 8:9], in_=colp[:, 8:9], func=AF.Identity,
                                         bias=colp[:, 2:3])
                    scalar.wait_ge(hV, A_v(b, 1))
                    nc.scalar.activation(out=colp[:, 5:6], in_=colp[:, 4:5], func=AF.Identity,
                                         scale=1.0 / 1024.0)
                    nc.scalar.activation(out=colp[:, 6:7], in_=colp[:, 5:6],
                                         func=AF.Square).then_inc(hS, 1)
                    scalar.wait_ge(hV, A_v(b, 2))
                    nc.scalar.activation(out=colp[:, 9:10], in_=colp[:, 7:8], func=AF.Sqrt,
                                         bias=epscol[:, 0:1]).then_inc(hS, 1)
                    scalar.wait_ge(hV, A_v(b, 3))
                    nc.scalar.activation(out=xlnb[:], in_=xlnb[:], func=AF.Identity,
                                         scale=colp[:, 10:11]).then_inc(hS, 1)
                    scalar.wait_ge(hT, A_t(b, 4))
                    for j, pbix in enumerate((0, 1, 6, 7)):
                        mm = nc.scalar.activation(out=gsilu[:, j * 512:(j + 1) * 512],
                                                  in_=psb[pbix][:, :], func=AF.Silu)
                    mm.then_inc(hS, 1)
                    for h in range(8):
                        pC = psb[3] if h % 2 == 0 else psb[1]
                        scalar.wait_ge(hT, A_t(b, 6 + 2 * h))
                        nc.scalar.activation(out=crossS[:], in_=pC[:, 0:257], func=AF.Identity,
                                             scale=ccol8[:, h:h + 1]).then_inc(hS, 1)
                        scalar.wait_ge(hV, A_v(b, V_oc(h)))
                        nc.scalar.activation(out=colp[:, 16:17], in_=ocomb[:, 256:257], func=AF.Identity,
                                             scale=rsR[:, h * NB + b:h * NB + b + 1])
                        nc.scalar.activation(out=colp[:, 16:17], in_=colp[:, 16:17], func=AF.Abs)
                        nc.scalar.activation(out=colp[:, 17:18], in_=colp[:, 16:17], func=AF.Relu,
                                             bias=monecol[:, 0:1])
                        nc.scalar.activation(out=colp[:, 17:18], in_=colp[:, 17:18], func=AF.Identity,
                                             bias=onescol[:, 0:1]).then_inc(hS, 1)
                        scalar.wait_ge(hV, A_v(b, V_inv(h)))
                        nc.scalar.activation(out=osct[:], in_=ocomb[:, 0:256], func=AF.Identity,
                                             scale=colp[:, 12:13])
                        nc.scalar.activation(out=dumpb[:, 0:256], in_=osct[:], func=AF.Square,
                                             accum_out=colp[:, 13:14])
                        nc.scalar.activation(out=colp[:, 14:15], in_=colp[:, 13:14], func=AF.Sqrt,
                                             scale=1.0 / 256.0, bias=epscol[:, 0:1]).then_inc(hS, 1)

            BIN += KIN * NB
            BOUT += DBG_OUT + KOUT * NB
            BT += KT * NB
            BV += KV * NB
            BS += KS * NB
        actx.close()

        qstair = [0]

        def emit_ffn(tag, xsrc, xdst, use_stash, dbg_ffn=False):
            nonlocal BIN, BOUT, BT, BV, BS
            fctx = ExitStack()
            fsb = lambda n_, s_, dt=F32: fctx.enter_context(nc.sbuf_tensor(f"{tag}{n_}", s_, dt))
            guw = fsb("guw", [P, 8 * 8192], BF16)
            stgc = [fsb("stg0", [P, 2048])]
            qa = fsb("qa", [P, 2048], BF16)
            qb = fsb("qb", [P, 2048], BF16)
            xblk = [fsb("xb0", [P, DM]), fsb("xb1", [P, DM])]
            xlnf = fsb("xlnf", [P, DM])
            xqs = fsb("xqs", [P, DM])
            xqf = fsb("xqf", [P, DM])
            xqT = fsb("xqT", [P, DM], BF16)
            gq2 = fsb("gq2", [P, 1024])
            hsv = fsb("hsv", [P, 1024])
            ht = fsb("ht", [P, 4096])
            rgt = fsb("rgt", [P, 512])
            sqt = fsb("sqt", [P, 512])
            sdump = fsb("sdump", [P, 512])
            hqb = fsb("hqb", [P, 1024], BF16)

            b_in, b_out, b_t, b_v, b_s = BIN, BOUT, BT, BV, BS
            nGU = 32
            PRE_IN = lambda i: b_in + 16 * (i + 1)
            PRE_V = lambda i: b_v + (i + 1)
            pV = b_v + (0 if use_stash else nGU)
            pOUT = b_out + (0 if use_stash else nGU * 16)
            pIN = b_in + nGU * 16
            KIN, KOUT, KT, KV, KS = 16, 64, 13, 22, 15
            F_in = lambda b: pIN + KIN * b + 16
            F_t = lambda b, m: b_t + KT * b + m
            F_v = lambda b, m: pV + KV * b + m
            F_s = lambda b, m: b_s + KS * b + m
            F_out = lambda b, t: pOUT + KOUT * b + 16 * (t + 1)
            QB = qstair[0]
            qstair[0] += 32 * NB

            @block.sync
            def _(sync):
                sync.wait_ge(hV, b_v)
                sync.wait_ge(hS, b_s)
                sync.wait_ge(hT, b_t)
                sync.wait_ge(hOUT, b_out)
                for i in range(nGU):
                    buf = stgc[0]
                    wd_ = wgate_d if i < 16 else wup_d
                    f, half = (i % 16) // 2, i % 2
                    row0 = ((i % 16) // 2) * P
                    col0 = (i % 2) * 2048
                    if use_stash:
                        td_ = tgate_d if i < 16 else tup_d
                        dst = guw[:, f * 8192 + (0 if i < 16 else 4096) + col0:
                                  f * 8192 + (0 if i < 16 else 4096) + col0 + 2048]
                        sync.dma_start(out=dst, in_=td_[row0:row0 + P, col0:col0 + 2048]).then_inc(hIN, 16)
                    else:
                        if i >= 1:
                            sync.wait_ge(hV, PRE_V(i - 1))
                        sync.dma_start(out=buf[:], in_=wd_[row0:row0 + P, col0:col0 + 2048]).then_inc(hIN, 16)
                for b in range(NB):
                    if b >= 2:
                        sync.wait_ge(hV, F_v(b - 2, 2))
                        sync.wait_ge(hS, F_s(b - 2, 5))
                    sync.dma_start(out=xblk[b % 2][:], in_=xsrc[b * P:(b + 1) * P, :]).then_inc(hIN, 16)

            @block.vector
            def _(vector):
                vector.wait_ge(hS, b_s)
                vector.wait_ge(hT, b_t)
                if not use_stash:
                    for i in range(nGU):
                        buf = stgc[0]
                        wi = 0 if i < 16 else 1
                        f = (i % 16) // 2
                        col0 = (i % 2) * 2048
                        dst = guw[:, f * 8192 + (0 if i < 16 else 4096) + col0:
                                  f * 8192 + (0 if i < 16 else 4096) + col0 + 2048]
                        vector.wait_ge(hIN, PRE_IN(i) + (16 * min(NB, 2) if i == nGU - 1 else 0))
                        vector.tensor_scalar(qb[:], buf[:], wscol[:, 4 + wi:5 + wi], -0.5, ALU.mult, ALU.is_le)
                        vector.tensor_scalar(qa[:], buf[:], wscol[:, 4 + wi:5 + wi], 0.5, ALU.mult, ALU.is_ge)
                        vector.tensor_copy(out=colp[:, 3:4], in_=onescol[:])
                        nc.vector.tensor_sub(dst, qa[:], qb[:]).then_inc(hV, 1)
                for b in range(NB):
                    xb = xblk[b % 2]
                    vector.wait_ge(hIN, F_in(b) + (0 if b == NB - 1 else 16))
                    nc.vector.tensor_reduce(colp[:, 4:5], xb[:], AX.X, ALU.add).then_inc(hV, 1)
                    vector.wait_ge(hS, F_s(b, 1))
                    vector.scalar_tensor_tensor(colp[:, 7:8], colp[:, 8:9], 1.0 / 1024.0,
                                                colp[:, 6:7], ALU.mult, ALU.subtract)
                    nc.vector.tensor_scalar_sub(xlnf[:], xb[:], colp[:, 5:6]).then_inc(hV, 1)
                    vector.wait_ge(hS, F_s(b, 2))
                    nc.vector.reciprocal(colp[:, 10:11], colp[:, 9:10]).then_inc(hV, 1)
                    vector.wait_ge(hS, F_s(b, 3))
                    nc.vector.tensor_reduce(colp[:, 4:5], xlnf[:], AX.X, ALU.max,
                                            apply_absolute_value=True).then_inc(hV, 1)
                    vector.wait_ge(hS, F_s(b, 4))
                    vector.tensor_scalar_mul(colp[:, 11:12], colp[:, 5:6], 1.0 / 127.0)
                    vector.reciprocal(colp[:, 6:7], colp[:, 5:6])
                    vector.tensor_tensor(colp[:, 12:13], colp[:, 11:12], wscol[:, 0:1], ALU.mult)
                    nc.vector.tensor_tensor(colp[:, 13:14], colp[:, 11:12], wscol[:, 1:2], ALU.mult).then_inc(hV, 1)
                    vector.wait_ge(hS, F_s(b, 5))
                    vector.tensor_copy(out=xqs[:], in_=xlnf[:])
                    _stair(vector, xlnf[:], xqf[:], 8)
                    vector.tensor_scalar_add(xqs[:], xqs[:], -128.0)
                    nc.vector.tensor_sub(xqf[:], xqs[:], xlnf[:]).then_inc(hV, 1)
                    vector.wait_ge(hT, F_t(b, 1))
                    vector.tensor_copy(out=xqT[:, 0:512], in_=psb[0][:, :])
                    nc.vector.tensor_copy(out=xqT[:, 512:1024], in_=psb[1][:, :]).then_inc(hV, 1)
                    for j in range(8):
                        bg, bu = 2 * (j % 4), 2 * (j % 4) + 1
                        vector.wait_ge(hS, F_s(b, 6 + j))
                        vector.tensor_tensor(sqt[:], rgt[:], rgt[:], ALU.mult)
                        nc.vector.scalar_tensor_tensor(ht[:, j * 512:(j + 1) * 512], psb[bu][:, :],
                                                       colp[:, 13:14], sqt[:], ALU.mult,
                                                       ALU.mult).then_inc(hV, 1)
                    nc.vector.tensor_reduce(colp[:, 4:5], ht[:], AX.X, ALU.max,
                                            apply_absolute_value=True).then_inc(hV, 1)
                    vector.wait_ge(hS, F_s(b, 14))
                    vector.tensor_scalar_mul(colp[:, 11:12], colp[:, 5:6], 1.0 / 127.0)
                    vector.reciprocal(colp[:, 6:7], colp[:, 5:6])
                    nc.vector.tensor_tensor(shc[:, b:b + 1], colp[:, 11:12], wscol[:, 2:3],
                                            ALU.mult).then_inc(hV, 1)
                    vector.wait_ge(hS, F_s(b, 15))
                    for pi, pair in enumerate(((0, 1), (2, 3))):
                        pbq = QB + b * 32 + pi * 16
                        for p in range(8):
                            for qi, q in enumerate(pair):
                                sT = (gq2, xqf)[qi]
                                wT = (hsv, xqs)[qi]
                                vector.wait_ge(hQs, pbq + p * 2 + qi + 1)
                                src = ht[:, q * 1024:(q + 1) * 1024] if p == 0 else wT[:]
                                nc.vector.scalar_tensor_tensor(wT[:], sT[:], -float(2.0 ** (6 - p)),
                                                               src, ALU.mult, ALU.add).then_inc(hQv, 1)
                        for qi, q in enumerate(pair):
                            wT = (hsv, xqs)[qi]
                            hsl = ht[:, q * 1024:(q + 1) * 1024]
                            vector.tensor_sub(hsl, hsl, wT[:])
                            mm = vector.tensor_scalar_add(hsl, hsl, -0.5)
                        if pi == 0:
                            mm.then_inc(hV, 1)
                    for t in range(4):
                        vector.wait_ge(hT, F_t(b, 10 + t))
                        if t >= 1:
                            vector.wait_ge(hOUT, F_out(b, t - 1))
                        elif b >= 1:
                            vector.wait_ge(hOUT, F_out(b - 1, 3))
                        vector.tensor_copy(out=hqb[:, 0:512], in_=psb[2 * t][:, :])
                        nc.vector.tensor_copy(out=hqb[:, 512:1024], in_=psb[2 * t + 1][:, :]).then_inc(hV, 1)

            @block.scalar
            def _(scalar):
                scalar.wait_ge(hV, pV)
                for b in range(NB):
                    xb = xblk[b % 2]
                    scalar.wait_ge(hIN, F_in(b) + (0 if b == NB - 1 else 16))
                    if b >= 1:
                        scalar.wait_ge(hV, F_v(b - 1, 2))
                    nc.scalar.activation(out=sdump[:], in_=xb[:, 0:512], func=AF.Square,
                                         accum_out=colp[:, 2:3])
                    nc.scalar.activation(out=sdump[:], in_=xb[:, 512:1024], func=AF.Square,
                                         accum_out=colp[:, 8:9])
                    nc.scalar.activation(out=colp[:, 1:2], in_=onescol[:], func=AF.Identity)
                    nc.scalar.activation(out=colp[:, 8:9], in_=colp[:, 8:9], func=AF.Identity,
                                         bias=colp[:, 2:3])
                    scalar.wait_ge(hV, F_v(b, 1))
                    nc.scalar.activation(out=colp[:, 5:6], in_=colp[:, 4:5], func=AF.Identity,
                                         scale=1.0 / 1024.0)
                    nc.scalar.activation(out=colp[:, 6:7], in_=colp[:, 5:6],
                                         func=AF.Square).then_inc(hS, 1)
                    scalar.wait_ge(hV, F_v(b, 2))
                    nc.scalar.activation(out=colp[:, 9:10], in_=colp[:, 7:8], func=AF.Sqrt,
                                         bias=epscol[:, 0:1]).then_inc(hS, 1)
                    scalar.wait_ge(hV, F_v(b, 3))
                    nc.scalar.activation(out=xlnf[:], in_=xlnf[:], func=AF.Identity,
                                         scale=colp[:, 10:11]).then_inc(hS, 1)
                    scalar.wait_ge(hV, F_v(b, 4))
                    nc.scalar.activation(out=colp[:, 5:6], in_=colp[:, 4:5], func=AF.Relu,
                                         bias=negepscol[:, 0:1])
                    nc.scalar.activation(out=colp[:, 5:6], in_=colp[:, 5:6], func=AF.Identity,
                                         bias=epscol[:, 0:1]).then_inc(hS, 1)
                    scalar.wait_ge(hV, F_v(b, 5))
                    nc.scalar.activation(out=xlnf[:], in_=xlnf[:], func=AF.Identity,
                                         scale=colp[:, 6:7])
                    nc.scalar.activation(out=xlnf[:], in_=xlnf[:], func=AF.Identity,
                                         scale=127.0, bias=colp[:, 0:1]).then_inc(hS, 1)
                    for j in range(8):
                        bg = 2 * (j % 4)
                        scalar.wait_ge(hT, F_t(b, 2 + j))
                        if j >= 1:
                            scalar.wait_ge(hV, F_v(b, 8 + j - 1))
                        elif b >= 1:
                            scalar.wait_ge(hV, F_v(b - 1, 15))
                        nc.scalar.activation(out=rgt[:], in_=psb[bg][:, :], func=AF.Relu,
                                             scale=colp[:, 12:13]).then_inc(hS, 1)
                    scalar.wait_ge(hV, F_v(b, 16))
                    nc.scalar.activation(out=colp[:, 5:6], in_=colp[:, 4:5], func=AF.Relu,
                                         bias=negepscol[:, 0:1])
                    nc.scalar.activation(out=colp[:, 5:6], in_=colp[:, 5:6], func=AF.Identity,
                                         bias=epscol[:, 0:1]).then_inc(hS, 1)
                    scalar.wait_ge(hV, F_v(b, 17))
                    nc.scalar.activation(out=ht[:], in_=ht[:], func=AF.Identity, scale=colp[:, 6:7])
                    nc.scalar.activation(out=ht[:], in_=ht[:], func=AF.Identity,
                                         scale=127.0, bias=colp[:, 0:1]).then_inc(hS, 1)
                    for pi, pair in enumerate(((0, 1), (2, 3))):
                        pbq = QB + b * 32 + pi * 16
                        for p in range(8):
                            for qi, q in enumerate(pair):
                                sT = (gq2, xqf)[qi]
                                wT = (hsv, xqs)[qi]
                                if p == 0:
                                    scalar.wait_ge(hQv, pbq)
                                else:
                                    scalar.wait_ge(hQv, pbq + (p - 1) * 2 + qi + 1)
                                src = ht[:, q * 1024:(q + 1) * 1024] if p == 0 else wT[:]
                                nc.scalar.activation(out=sT[:], in_=src, func=AF.Sign,
                                                     bias=nthrcol[:, 0:1]).then_inc(hQs, 1)

            @block.tensor
            def _(tensor):
                tensor.wait_ge(hV, pV)
                for b in range(NB):
                    tensor.wait_ge(hV, F_v(b, 6))
                    if b >= 1:
                        tensor.wait_ge(hV, F_v(b - 1, 19))
                    for f in range(8):
                        mm = tensor.transpose(psb[f // 4][:, (f % 4) * P:(f % 4 + 1) * P],
                                              xqf[:, f * P:(f + 1) * P], id_f[:])
                    mm.then_inc(hT, 1)
                    tensor.wait_ge(hV, F_v(b, 7))
                    for j in range(8):
                        bg, bu = 2 * (j % 4), 2 * (j % 4) + 1
                        if j >= 4:
                            tensor.wait_ge(hV, F_v(b, 8 + j - 4))
                        c0 = j * 512
                        for f in range(8):
                            st_, sp = (f == 0), (f == 7)
                            tensor.matmul(psb[bg][:, :], xqT[:, f * P:(f + 1) * P],
                                          guw[:, f * 8192 + c0:f * 8192 + c0 + 512], start=st_, stop=sp)
                            mm = tensor.matmul(psb[bu][:, :], xqT[:, f * P:(f + 1) * P],
                                               guw[:, f * 8192 + 4096 + c0:f * 8192 + 4096 + c0 + 512],
                                               start=st_, stop=sp)
                        mm.then_inc(hT, 1)
                    tensor.wait_ge(hV, F_v(b, 18))
                    for t in range(4):
                        if t == 2:
                            tensor.wait_ge(hV, F_v(b, 19))
                        for c in range(8):
                            ch = t * 8 + c
                            mm = tensor.transpose(psb[2 * t + c // 4][:, (c % 4) * P:(c % 4 + 1) * P],
                                                  ht[:, ch * P:(ch + 1) * P], id_f[:])
                        mm.then_inc(hT, 1)

            @block.gpsimd
            def _(g):
                if not use_stash:
                    for i in range(nGU):
                        wi = 0 if i < 16 else 1
                        f = (i % 16) // 2
                        col0 = (i % 2) * 2048
                        td_ = tgate_d if i < 16 else tup_d
                        src = guw[:, f * 8192 + (0 if i < 16 else 4096) + col0:
                                  f * 8192 + (0 if i < 16 else 4096) + col0 + 2048]
                        g.wait_ge(hV, PRE_V(i))
                        g.dma_start(out=td_[f * P:(f + 1) * P, col0:col0 + 2048],
                                    in_=src).then_inc(hOUT, 16)
                for b in range(NB):
                    for t in range(4):
                        g.wait_ge(hV, F_v(b, 19 + t))
                        row0 = (b * 4 + t) * P
                        g.dma_start(out=hq_d[row0:row0 + P, :], in_=hqb[:]).then_inc(hOUT, 16)

            if dbg_ffn:
                vend, send = pV + KV * NB, b_s + KS * NB
                specs = [("xqf", xqf), ("xqT", xqT), ("ht", ht), ("shc", shc),
                         ("colpF", colp), ("xlnf", xlnf), ("guw0", None)]
                douts = {}
                for nm, tile in specs:
                    if tile is None:
                        douts[nm] = nc.dram_tensor(f"d_{tag}{nm}", [P, 8192], BF16, kind="ExternalOutput").ap()
                    else:
                        douts[nm] = nc.dram_tensor(f"d_{tag}{nm}", list(tile[:].shape), tile[:].dtype,
                                                   kind="ExternalOutput").ap()

                @block.gpsimd
                def _(g):
                    g.wait_ge(hV, vend)
                    g.wait_ge(hS, send)
                    for nm, tile in specs:
                        src = guw[:, 0:8192] if tile is None else tile[:]
                        g.dma_start(out=douts[nm], in_=src).then_inc(hOUT, 16)
            BIN = pIN + KIN * NB
            BOUT = pOUT + KOUT * NB + (len(specs) * 16 if dbg_ffn else 0)
            BT = b_t + KT * NB
            BV = pV + KV * NB
            BS = b_s + KS * NB
            fctx.close()

            f2 = ExitStack()
            f2sb = lambda n_, s_, dt=F32: f2.enter_context(nc.sbuf_tensor(f"{tag}2{n_}", s_, dt))
            wd = f2sb("wd", [P, 32 * DM], BF16)
            st2 = [f2sb(f"st{i}", [P, DM]) for i in range(2)]
            qa2 = f2sb("qa2", [P, DM], BF16)
            qb2 = f2sb("qb2", [P, DM], BF16)
            x2 = [f2sb("x20", [P, DM]), f2sb("x21", [P, DM])]
            hqt = [f2sb(f"hqt{i}", [P, 1024], BF16) for i in range(8)]

            b_in, b_out, b_t, b_v, b_s = BIN, BOUT, BT, BV, BS
            nWD = 32
            P2_IN = lambda i: b_in + 16 * (i + 1)
            P2_V = lambda i: b_v + (i + 1)
            pV2 = b_v + (0 if use_stash else nWD)
            pOUT2 = b_out + (0 if use_stash else nWD * 16)
            pIN2 = b_in + nWD * 16
            K2IN, K2OUT, K2T, K2V = 80, 16, 1, 1
            G_in = lambda b: pIN2 + K2IN * b + 80
            G_t = lambda b: b_t + K2T * b + 1
            G_v = lambda b: pV2 + K2V * b + 1
            G_out = lambda b: pOUT2 + K2OUT * (b + 1)

            @block.sync
            def _(sync):
                sync.wait_ge(hV, pV2 - (0 if use_stash else nWD))
                sync.wait_ge(hS, b_s)
                sync.wait_ge(hT, b_t)
                sync.wait_ge(hOUT, b_out)
                for i in range(nWD):
                    if use_stash:
                        sync.dma_start(out=wd[:, i * DM:(i + 1) * DM],
                                       in_=tdown_d[i * P:(i + 1) * P, :]).then_inc(hIN, 16)
                    else:
                        if i >= 2:
                            sync.wait_ge(hV, P2_V(i - 2))
                        sync.dma_start(out=st2[i % 2][:], in_=wdown_d[i * P:(i + 1) * P, :]).then_inc(hIN, 16)
                for b in range(NB):
                    if b == 0 and not use_stash:
                        sync.wait_ge(hV, P2_V(nWD - 1))
                    if b >= 2:
                        sync.wait_ge(hV, G_v(b - 2))
                        sync.wait_ge(hOUT, G_out(b - 2) + 16)
                    sync.dma_start(out=x2[b % 2][:], in_=xsrc[b * P:(b + 1) * P, :]).then_inc(hIN, 16)
                    for t in range(4):
                        row0 = (b * 4 + t) * P
                        sync.dma_start(out=hqt[4 * (b % 2) + t][:], in_=hq_d[row0:row0 + P, :]).then_inc(hIN, 16)

            @block.vector
            def _(vector):
                if not use_stash:
                    vector.wait_ge(hT, b_t)
                    for i in range(nWD):
                        buf = st2[i % 2]
                        vector.wait_ge(hIN, P2_IN(i) + (0 if i == nWD - 1 else 16))
                        vector.tensor_scalar(qb2[:], buf[:], wscol[:, 6:7], -0.5, ALU.mult, ALU.is_le)
                        vector.tensor_scalar(qa2[:], buf[:], wscol[:, 6:7], 0.5, ALU.mult, ALU.is_ge)
                        vector.tensor_copy(out=colp[:, 3:4], in_=onescol[:])
                        nc.vector.tensor_sub(wd[:, i * DM:(i + 1) * DM], qa2[:], qb2[:]).then_inc(hV, 1)
                for b in range(NB):
                    vector.wait_ge(hT, G_t(b))
                    vector.scalar_tensor_tensor(x2[b % 2][:, 0:512], psb[0 + 2 * (b % 2)][:, :],
                                                shc[:, b:b + 1], x2[b % 2][:, 0:512], ALU.mult, ALU.add)
                    nc.vector.scalar_tensor_tensor(x2[b % 2][:, 512:1024], psb[1 + 2 * (b % 2)][:, :],
                                                   shc[:, b:b + 1], x2[b % 2][:, 512:1024],
                                                   ALU.mult, ALU.add).then_inc(hV, 1)

            @block.tensor
            def _(tensor):
                tensor.wait_ge(hV, pV2)
                for b in range(NB):
                    tensor.wait_ge(hIN, G_in(b) + (80 if b == 0 and NB > 1 else 0))
                    if b >= 2:
                        tensor.wait_ge(hV, G_v(b - 1))
                    o0, o1 = 0 + 2 * (b % 2), 1 + 2 * (b % 2)
                    for c in range(32):
                        st_, sp = (c == 0), (c == 31)
                        lhs = hqt[4 * (b % 2) + c // 8][:, (c % 8) * P:(c % 8 + 1) * P]
                        tensor.matmul(psb[o0][:, :], lhs, wd[:, c * DM:c * DM + 512], start=st_, stop=sp)
                        mm = tensor.matmul(psb[o1][:, :], lhs, wd[:, c * DM + 512:(c + 1) * DM],
                                           start=st_, stop=sp)
                    mm.then_inc(hT, 1)

            @block.gpsimd
            def _(g):
                if not use_stash:
                    for i in range(nWD):
                        g.wait_ge(hV, P2_V(i))
                        g.dma_start(out=tdown_d[i * P:(i + 1) * P, :],
                                    in_=wd[:, i * DM:(i + 1) * DM]).then_inc(hOUT, 16)
                for b in range(NB):
                    g.wait_ge(hV, G_v(b))
                    g.dma_start(out=xdst[b * P:(b + 1) * P, :], in_=x2[b % 2][:]).then_inc(hOUT, 16)

            BIN = pIN2 + K2IN * NB
            BOUT = pOUT2 + K2OUT * NB
            BT = b_t + K2T * NB
            BV = pV2 + K2V * NB
            BS = b_s
            f2.close()

        if "B" in phases:
            emit_ffn("fb", io["B"][0], io["B"][1], use_stash=False, dbg_ffn=dbg)
        def emit_attnb(xsrc, xdst):
            nonlocal BIN, BOUT, BT, BV, BS
            cctx = ExitStack()
            csb = lambda n_, s_, dt=F32: cctx.enter_context(nc.sbuf_tensor(f"c{n_}", s_, dt))
            qkvw = csb("qkvw", [P, 8 * 3072], BF16)
            woutw = csb("woutw", [P, 8 * 1024], BF16)
            wg1w = csb("wg1w", [P, 8 * 64], BF16)
            wg2w = csb("wg2w", [64, 1024], BF16)
            brow = csb("brow", [1, 3072], BF16)
            bg1r = csb("bg1r", [1, 64], BF16)
            bg2r = csb("bg2r", [1, 1024], BF16)
            botr = csb("botr", [1, 1024], BF16)
            lngb = csb("lngb", [P, 1024])
            lnbb = csb("lnbb", [P, 1024])
            kv_sb = csb("kv_sb", [64, 16 * 65])
            kvnb = csb("kvnb", [64, 16 * 64], BF16)
            negM8 = csb("negM8", [64, 16])
            z8 = csb("z8", [64, 16])
            mtmp = csb("mtmp", [64, 16])
            m2tmp = csb("m2tmp", [64, 16])
            d8 = csb("d8", [64, 16])
            rr = csb("rr", [64, 16])
            ztmp = csb("ztmp", [64, 16])
            rz8 = csb("rz8", [64, 16])
            stg3 = csb("stg3", [P, 3072])
            xb2 = [csb("xb0", [P, DM]), csb("xb1", [P, DM])]
            xlnc = csb("xlnc", [P, DM])
            xlnT = csb("xlnT", [P, DM], BF16)
            ktf = csb("ktf", [P, DM])
            kfm = csb("kfm", [64, 2048])
            ktil = csb("ktil", [64, 2048])
            ktil_tm = csb("ktil_tm", [P, DM], BF16)
            v1 = csb("v1", [P, 16 * 65], BF16)
            qsil = csb("qsil", [P, DM])
            qfmT = csb("qfmT", [64, 2048], BF16)
            olnt = csb("olnt", [P, DM])
            ght = csb("ght", [P, 64])
            ghTb = csb("ghTb", [64, P], BF16)
            gsig = csb("gsig", [P, DM])
            gout = csb("gout", [P, DM])
            goutT = csb("goutT", [P, DM], BF16)
            sdu = csb("sdu", [P, 512])

            b_in, b_out, b_t, b_v, b_s = BIN, BOUT, BT, BV, BS
            nL = 8 + 8 + 8 + 1 + 4 + 2
            C_PIN = lambda i: b_in + 16 * (i + 1)
            C_PV = lambda i: b_v + (i + 1)

            @block.sync
            def _(sync):
                sync.wait_ge(hV, b_v)
                sync.wait_ge(hS, b_s)
                sync.wait_ge(hT, b_t)
                sync.wait_ge(hOUT, b_out)
                i = 0
                for f in range(8):
                    if i >= 1:
                        sync.wait_ge(hV, C_PV(i - 1))
                    sync.dma_start(out=stg3[:, 0:3072], in_=wqkv_d[f * P:(f + 1) * P, :]).then_inc(hIN, 16)
                    i += 1
                for f in range(8):
                    sync.wait_ge(hV, C_PV(i - 1))
                    sync.dma_start(out=stg3[:, 0:1024], in_=wout_d[f * P:(f + 1) * P, :]).then_inc(hIN, 16)
                    i += 1
                for f in range(8):
                    sync.wait_ge(hV, C_PV(i - 1))
                    sync.dma_start(out=stg3[:, 0:64], in_=wg1_d[f * P:(f + 1) * P, :]).then_inc(hIN, 16)
                    i += 1
                sync.wait_ge(hV, C_PV(i - 1))
                sync.dma_start(out=stg3[0:64, 0:1024], in_=wg2_d[:, :]).then_inc(hIN, 16)
                i += 1
                for bi, bd in enumerate((bqkv_d, bg1_d, bg2_d, bout_d)):
                    sync.wait_ge(hV, C_PV(i - 1))
                    w_ = (3072, 64, 1024, 1024)[bi]
                    sync.dma_start(out=stg3[0:1, 0:w_], in_=bd[:, :]).then_inc(hIN, 16)
                    i += 1
                for li, (bd, _dst) in enumerate(((lng_d, lngb), (lnb_d, lnbb))):
                    sync.wait_ge(hV, C_PV(i - 1))
                    if li == 1:
                        sync.wait_ge(hT, b_t + 1)
                    sync.dma_start(out=stg3[0:1, 1024:2048], in_=bd[:, :]).then_inc(hIN, 16)
                    i += 1

            @block.vector
            def _(vector):
                vector.wait_ge(hS, b_s)
                vector.wait_ge(hT, b_t)
                vector.memset(kv_sb[:], 0.0)
                vector.memset(negM8[:], 1e4)
                vector.memset(z8[:], 0.0)
                for h in range(16):
                    vector.memset(v1[:, h * 65 + 64:h * 65 + 65], 1.0)
                i = 0
                for f in range(8):
                    vector.wait_ge(hIN, C_PIN(i))
                    nc.vector.tensor_copy(out=qkvw[:, f * 3072:(f + 1) * 3072],
                                          in_=stg3[:, 0:3072]).then_inc(hV, 1)
                    i += 1
                for f in range(8):
                    vector.wait_ge(hIN, C_PIN(i))
                    nc.vector.tensor_copy(out=woutw[:, f * 1024:(f + 1) * 1024],
                                          in_=stg3[:, 0:1024]).then_inc(hV, 1)
                    i += 1
                for f in range(8):
                    vector.wait_ge(hIN, C_PIN(i))
                    nc.vector.tensor_copy(out=wg1w[:, f * 64:(f + 1) * 64],
                                          in_=stg3[:, 0:64]).then_inc(hV, 1)
                    i += 1
                vector.wait_ge(hIN, C_PIN(i))
                nc.vector.tensor_copy(out=wg2w[:], in_=stg3[0:64, 0:1024]).then_inc(hV, 1)
                i += 1
                for bi, dst in enumerate((brow, bg1r, bg2r, botr)):
                    vector.wait_ge(hIN, C_PIN(i))
                    w_ = (3072, 64, 1024, 1024)[bi]
                    nc.vector.tensor_copy(out=dst[0:1, 0:w_], in_=stg3[0:1, 0:w_]).then_inc(hV, 1)
                    i += 1
                for li, dst in enumerate((lngb, lnbb)):
                    vector.wait_ge(hIN, C_PIN(i) + (16 * min(NB, 2) if li == 1 else 0))
                    vector.wait_ge(hT, b_t + li + 1)
                    vector.tensor_copy(out=dst[:, 0:512], in_=psb[4][:, :])
                    nc.vector.tensor_copy(out=dst[:, 512:1024], in_=psb[5][:, :]).then_inc(hV, 1)
                    i += 1

            @block.tensor
            def _(tensor):
                tensor.wait_ge(hV, b_v)
                for li in range(2):
                    tensor.wait_ge(hIN, C_PIN(nL - 2 + li) + (16 * min(NB, 2) if li == 1 else 0))
                    if li == 1:
                        tensor.wait_ge(hV, C_PV(nL - 2))
                    tensor.matmul(psb[4][:, :], onesf[:], stg3[0:1, 1024:1536], start=True, stop=True)
                    mm = tensor.matmul(psb[5][:, :], onesf[:], stg3[0:1, 1536:2048], start=True, stop=True)
                    mm.then_inc(hT, 1)

            BV = b_v + nL
            BT = b_t + 2
            BIN = b_in + nL * 16

            b_in, b_t, b_v, b_s = BIN, BT, BV, BS
            KIN1, KT1, KV1, KS1 = 16, 5, 10, 4
            C_in = lambda b: b_in + KIN1 * b + 16
            C_t = lambda b, m: b_t + KT1 * b + m
            C_v = lambda b, m: b_v + KV1 * b + m
            C_s = lambda b, m: b_s + KS1 * b + m

            @block.sync
            def _(sync):
                for b in range(NB):
                    if b >= 2:
                        sync.wait_ge(hV, C_v(b - 2, 2))
                        sync.wait_ge(hS, C_s(b - 2, 1))
                    sync.dma_start(out=xb2[b % 2][:], in_=xsrc[b * P:(b + 1) * P, :]).then_inc(hIN, 16)

            @block.vector
            def _(vector):
                for b in range(NB):
                    xb = xb2[b % 2]
                    vector.wait_ge(hIN, C_in(b) + (0 if b == NB - 1 else 16))
                    nc.vector.tensor_reduce(colp[:, 4:5], xb[:], AX.X, ALU.add).then_inc(hV, 1)
                    vector.wait_ge(hS, C_s(b, 1))
                    vector.scalar_tensor_tensor(colp[:, 7:8], colp[:, 8:9], 1.0 / 1024.0,
                                                colp[:, 6:7], ALU.mult, ALU.subtract)
                    vector.tensor_copy(out=nmu_c[:, b:b + 1], in_=colp[:, 5:6])
                    if b >= 1:
                        vector.wait_ge(hT, C_t(b - 1, 1))
                    nc.vector.tensor_scalar_sub(xlnc[:], xb[:], colp[:, 5:6]).then_inc(hV, 1)
                    vector.wait_ge(hS, C_s(b, 2))
                    nc.vector.reciprocal(colp[:, 10:11], colp[:, 9:10]).then_inc(hV, 1)
                    vector.wait_ge(hT, C_t(b, 1))
                    vector.tensor_copy(out=xlnT[:, 0:512], in_=psb[4][:, :])
                    vector.tensor_copy(out=xlnT[:, 512:1024], in_=psb[5][:, :])
                    nc.vector.tensor_copy(out=rstdc[:, b:b + 1], in_=colp[:, 10:11]).then_inc(hV, 1)
                    vector.wait_ge(hT, C_t(b, 2))
                    vector.tensor_copy(out=ktf[:, 0:512], in_=psb[0][:, :])
                    vector.tensor_copy(out=ktf[:, 512:1024], in_=psb[1][:, :])
                    for j in range(2):
                        dst = v1[:].rearrange("p (h c) -> p h c", c=65)[:, 8 * j:8 * j + 8, 0:64]
                        srcp = psb[2 + j][:, :].rearrange("p (h c) -> p h c", c=64)
                        mm = vector.tensor_copy(out=dst, in_=srcp)
                    mm.then_inc(hV, 1)
                    vector.wait_ge(hT, C_t(b, 3))
                    for j in range(4):
                        mm = vector.tensor_copy(out=kfm[:, j * 512:(j + 1) * 512], in_=psb[j][0:64, :])
                    mm.then_inc(hV, 1)
                    for c in range(16):
                        vector.tensor_reduce(mtmp[:, c:c + 1], kfm[:, c * P:(c + 1) * P],
                                             AX.X, ALU.max, negate=True)
                    for c in range(16):
                        vector.tensor_tensor(m2tmp[:, c:c + 1], negM8[:, c:c + 1],
                                             mtmp[:, c:c + 1], ALU.min)
                    for c in range(16):
                        mm = vector.tensor_sub(d8[:, c:c + 1], m2tmp[:, c:c + 1], negM8[:, c:c + 1])
                    mm.then_inc(hV, 1)
                    vector.wait_ge(hS, C_s(b, 4))
                    for c in range(16):
                        vector.tensor_reduce(ztmp[:, c:c + 1], ktil[:, c * P:(c + 1) * P], AX.X, ALU.add)
                    for c in range(16):
                        vector.scalar_tensor_tensor(z8[:, c:c + 1], z8[:, c:c + 1], rr[:, c:c + 1],
                                                    ztmp[:, c:c + 1], ALU.mult, ALU.add)
                    for c in range(16):
                        mm = vector.tensor_copy(out=negM8[:, c:c + 1], in_=m2tmp[:, c:c + 1])
                    mm.then_inc(hV, 1)
                    vector.wait_ge(hT, C_t(b, 4))
                    vector.tensor_copy(out=ktil_tm[:, 0:512], in_=psb[4][:, :])
                    nc.vector.tensor_copy(out=ktil_tm[:, 512:1024], in_=psb[5][:, :]).then_inc(hV, 1)
                    vector.wait_ge(hT, C_t(b, 5))
                    for h in range(16):
                        pb_ = (psb[6], psb[7], psb[0])[h // 7]
                        c0 = (h % 7) * 65
                        sl = slice(h * 65, (h + 1) * 65)
                        mm = vector.scalar_tensor_tensor(kv_sb[:, sl], kv_sb[:, sl], rr[:, h:h + 1],
                                                         pb_[0:64, c0:c0 + 65], ALU.mult, ALU.add)
                    mm.then_inc(hV, 1)

            @block.scalar
            def _(scalar):
                scalar.wait_ge(hV, b_v)
                for b in range(NB):
                    xb = xb2[b % 2]
                    scalar.wait_ge(hIN, C_in(b) + (0 if b == NB - 1 else 16))
                    if b >= 1:
                        scalar.wait_ge(hV, C_v(b - 1, 2))
                    nc.scalar.activation(out=sdu[:], in_=xb[:, 0:512], func=AF.Square,
                                         accum_out=colp[:, 2:3])
                    nc.scalar.activation(out=sdu[:], in_=xb[:, 512:1024], func=AF.Square,
                                         accum_out=colp[:, 8:9])
                    nc.scalar.activation(out=colp[:, 1:2], in_=onescol[:], func=AF.Identity)
                    nc.scalar.activation(out=colp[:, 8:9], in_=colp[:, 8:9], func=AF.Identity,
                                         bias=colp[:, 2:3])
                    scalar.wait_ge(hV, C_v(b, 1))
                    nc.scalar.activation(out=colp[:, 5:6], in_=colp[:, 4:5], func=AF.Identity,
                                         scale=1.0 / 1024.0)
                    nc.scalar.activation(out=colp[:, 6:7], in_=colp[:, 5:6],
                                         func=AF.Square).then_inc(hS, 1)
                    scalar.wait_ge(hV, C_v(b, 2))
                    nc.scalar.activation(out=colp[:, 9:10], in_=colp[:, 7:8], func=AF.Sqrt,
                                         bias=epscol[:, 0:1]).then_inc(hS, 1)
                    scalar.wait_ge(hV, C_v(b, 3))
                    nc.scalar.activation(out=xlnc[:], in_=xlnc[:], func=AF.Identity,
                                         scale=colp[:, 10:11]).then_inc(hS, 1)
                    scalar.wait_ge(hV, C_v(b, 7))
                    for c in range(16):
                        nc.scalar.activation(out=rr[:, c:c + 1], in_=d8[:, c:c + 1], func=AF.Exp)
                    for c in range(16):
                        mm = nc.scalar.activation(out=ktil[:, c * P:(c + 1) * P],
                                                  in_=kfm[:, c * P:(c + 1) * P], func=AF.Exp,
                                                  bias=m2tmp[:, c:c + 1])
                    mm.then_inc(hS, 1)

            @block.tensor
            def _(tensor):
                tensor.wait_ge(hV, b_v)
                for b in range(NB):
                    tensor.wait_ge(hS, C_s(b, 3))
                    if b >= 1:
                        tensor.wait_ge(hV, C_v(b - 1, 9))
                    for f in range(8):
                        mm = tensor.transpose(psb[4 + f // 4][:, (f % 4) * P:(f % 4 + 1) * P],
                                              xlnc[:, f * P:(f + 1) * P], id_f[:])
                    mm.then_inc(hT, 1)
                    tensor.wait_ge(hV, C_v(b, 4))
                    if b >= 1:
                        tensor.wait_ge(hV, C_v(b - 1, 10))
                    for f in range(8):
                        st_ = (f == 0)
                        o = f * 3072 + 1024
                        tensor.matmul(psb[0][:, :], xlnT[:, f * P:(f + 1) * P], qkvw[:, o:o + 512], start=st_, stop=False)
                        tensor.matmul(psb[1][:, :], xlnT[:, f * P:(f + 1) * P], qkvw[:, o + 512:o + 1024], start=st_, stop=False)
                        tensor.matmul(psb[2][:, :], xlnT[:, f * P:(f + 1) * P], qkvw[:, o + 1024:o + 1536], start=st_, stop=False)
                        tensor.matmul(psb[3][:, :], xlnT[:, f * P:(f + 1) * P], qkvw[:, o + 1536:o + 2048], start=st_, stop=False)
                    tensor.matmul(psb[0][:, :], onesb[:], brow[0:1, 1024:1536], start=False, stop=True)
                    tensor.matmul(psb[1][:, :], onesb[:], brow[0:1, 1536:2048], start=False, stop=True)
                    tensor.matmul(psb[2][:, :], onesb[:], brow[0:1, 2048:2560], start=False, stop=True)
                    mm = tensor.matmul(psb[3][:, :], onesb[:], brow[0:1, 2560:3072], start=False, stop=True)
                    mm.then_inc(hT, 1)
                    tensor.wait_ge(hV, C_v(b, 5))
                    for h in range(16):
                        mm = tensor.transpose(psb[h // 4][0:64, (h % 4) * P:(h % 4 + 1) * P],
                                              ktf[:, h * 64:(h + 1) * 64], id_f[:])
                    mm.then_inc(hT, 1)
                    tensor.wait_ge(hS, C_s(b, 4))
                    for h in range(16):
                        mm = tensor.transpose(psb[4 + h // 8][:, (h % 8) * 64:(h % 8 + 1) * 64],
                                              ktil[:, h * P:(h + 1) * P], id_f[0:64, 0:64])
                    mm.then_inc(hT, 1)
                    tensor.wait_ge(hV, C_v(b, 9))
                    for h in range(16):
                        pb_ = (psb[6], psb[7], psb[0])[h // 7]
                        c0 = (h % 7) * 65
                        mm = tensor.matmul(pb_[0:64, c0:c0 + 65],
                                           ktil_tm[:, h * 64:(h + 1) * 64],
                                           v1[:, h * 65:(h + 1) * 65], start=True, stop=True)
                    mm.then_inc(hT, 1)

            BIN = b_in + KIN1 * NB
            BT = b_t + KT1 * NB
            BV = b_v + KV1 * NB
            BS = b_s + KS1 * NB

            b_v = BV
            @block.vector
            def _(vector):
                vector.reciprocal(rz8[:], z8[:])
                vector.tensor_copy(out=colp[:, 3:4], in_=onescol[:])
                for h in range(16):
                    mm = vector.tensor_scalar_mul(kvnb[:, h * 64:(h + 1) * 64],
                                                  kv_sb[:, h * 65:h * 65 + 64], rz8[:, h:h + 1])
                mm.then_inc(hV, 1)
            BV = b_v + 1

            b_in, b_out, b_t, b_v, b_s = BIN, BOUT, BT, BV, BS
            KIN2, KOUT2, KT2, KV2, KS2 = 16, 16, 8, 10, 6
            G_in = lambda b: b_in + KIN2 * b + 16
            G_out = lambda b: b_out + KOUT2 * (b + 1)
            G_t = lambda b, m: b_t + KT2 * b + m
            G_v = lambda b, m: b_v + KV2 * b + m
            G_s = lambda b, m: b_s + KS2 * b + m

            @block.sync
            def _(sync):
                sync.wait_ge(hV, b_v)
                sync.wait_ge(hS, b_s)
                for b in range(NB):
                    if b >= 2:
                        sync.wait_ge(hV, G_v(b - 2, 10))
                        sync.wait_ge(hOUT, G_out(b - 2) + 16)
                    sync.dma_start(out=xb2[b % 2][:], in_=xsrc[b * P:(b + 1) * P, :]).then_inc(hIN, 16)

            @block.gpsimd
            def _(g):
                for b in range(NB):
                    g.wait_ge(hV, G_v(b, 10))
                    g.dma_start(out=xdst[b * P:(b + 1) * P, :], in_=xb2[b % 2][:]).then_inc(hOUT, 16)

            @block.vector
            def _(vector):
                for b in range(NB):
                    xb = xb2[b % 2]
                    vector.wait_ge(hIN, G_in(b) + (16 if b == 0 and NB > 1 else 0))
                    if b >= 1:
                        vector.wait_ge(hT, G_t(b - 1, 1))
                    nc.vector.tensor_scalar_sub(xlnc[:], xb[:], nmu_c[:, b:b + 1]).then_inc(hV, 1)
                    vector.wait_ge(hT, G_t(b, 1))
                    vector.tensor_copy(out=xlnT[:, 0:512], in_=psb[4][:, :])
                    nc.vector.tensor_copy(out=xlnT[:, 512:1024], in_=psb[5][:, :]).then_inc(hV, 1)
                    vector.wait_ge(hT, G_t(b, 3))
                    for j in range(4):
                        mm = vector.tensor_copy(out=qfmT[:, j * 512:(j + 1) * 512], in_=psb[2 + j][0:64, :])
                    mm.then_inc(hV, 1)
                    vector.wait_ge(hT, G_t(b, 4))
                    vector.tensor_reduce(colp[:, 2:3], psb[2][:, :], AX.X, ALU.add)
                    vector.tensor_reduce(colp[:, 3:4], psb[3][:, :], AX.X, ALU.add)
                    vector.tensor_copy(out=ght[:], in_=psb[6][:, 0:64])
                    nc.vector.tensor_add(colp[:, 4:5], colp[:, 2:3], colp[:, 3:4]).then_inc(hV, 1)
                    vector.wait_ge(hS, G_s(b, 3))
                    nc.vector.scalar_tensor_tensor(colp[:, 7:8], colp[:, 8:9], 1.0 / 1024.0,
                                                   colp[:, 6:7], ALU.mult, ALU.subtract).then_inc(hV, 1)
                    vector.wait_ge(hS, G_s(b, 4))
                    nc.vector.reciprocal(colp[:, 10:11], colp[:, 9:10]).then_inc(hV, 1)
                    vector.wait_ge(hT, G_t(b, 5))
                    nc.vector.tensor_copy(out=ghTb[:], in_=psb[7][0:64, 0:P]).then_inc(hV, 1)
                    vector.wait_ge(hS, G_s(b, 5))
                    vector.wait_ge(hS, G_s(b, 6))
                    vector.tensor_tensor(gout[:], olnt[:], lngb[:], ALU.mult)
                    vector.tensor_add(gout[:], gout[:], lnbb[:])
                    nc.vector.tensor_tensor(gout[:], gout[:], gsig[:], ALU.mult).then_inc(hV, 1)
                    vector.wait_ge(hT, G_t(b, 7))
                    vector.tensor_copy(out=goutT[:, 0:512], in_=psb[6][:, :])
                    nc.vector.tensor_copy(out=goutT[:, 512:1024], in_=psb[7][:, :]).then_inc(hV, 1)
                    vector.wait_ge(hT, G_t(b, 8))
                    vector.tensor_add(xb[:, 0:512], psb[0][:, :], xb[:, 0:512])
                    nc.vector.tensor_add(xb[:, 512:1024], psb[1][:, :], xb[:, 512:1024]).then_inc(hV, 1)

            @block.scalar
            def _(scalar):
                scalar.wait_ge(hV, b_v)
                for b in range(NB):
                    scalar.wait_ge(hV, G_v(b, 1))
                    nc.scalar.activation(out=xlnc[:], in_=xlnc[:], func=AF.Identity,
                                         scale=rstdc[:, b:b + 1]).then_inc(hS, 1)
                    scalar.wait_ge(hT, G_t(b, 2))
                    nc.scalar.activation(out=qsil[:, 0:512], in_=psb[0][:, :], func=AF.Silu)
                    nc.scalar.activation(out=qsil[:, 512:1024], in_=psb[1][:, :],
                                         func=AF.Silu).then_inc(hS, 1)
                    scalar.wait_ge(hV, G_v(b, 4))
                    nc.scalar.activation(out=sdu[:], in_=psb[2][:, :], func=AF.Square,
                                         accum_out=colp[:, 8:9])
                    nc.scalar.activation(out=sdu[:], in_=psb[3][:, :], func=AF.Square,
                                         accum_out=colp[:, 2:3])
                    nc.scalar.activation(out=colp[:, 1:2], in_=onescol[:], func=AF.Identity)
                    nc.scalar.activation(out=colp[:, 8:9], in_=colp[:, 8:9], func=AF.Identity,
                                         bias=colp[:, 2:3])
                    nc.scalar.activation(out=colp[:, 5:6], in_=colp[:, 4:5], func=AF.Identity,
                                         scale=1.0 / 1024.0)
                    nc.scalar.activation(out=colp[:, 6:7], in_=colp[:, 5:6],
                                         func=AF.Square).then_inc(hS, 1)
                    scalar.wait_ge(hV, G_v(b, 5))
                    nc.scalar.activation(out=colp[:, 9:10], in_=colp[:, 7:8], func=AF.Sqrt,
                                         bias=epscol[:, 0:1]).then_inc(hS, 1)
                    scalar.wait_ge(hV, G_v(b, 6))
                    nc.scalar.activation(out=colp[:, 11:12], in_=colp[:, 5:6], func=AF.Identity,
                                         scale=colp[:, 10:11])
                    nc.scalar.activation(out=colp[:, 1:2], in_=onescol[:], func=AF.Identity)
                    nc.scalar.activation(out=colp[:, 11:12], in_=colp[:, 11:12], func=AF.Identity,
                                         scale=-1.0)
                    nc.scalar.activation(out=colp[:, 1:2], in_=onescol[:], func=AF.Identity)
                    nc.scalar.activation(out=olnt[:, 0:512], in_=psb[2][:, :], func=AF.Identity,
                                         scale=colp[:, 10:11], bias=colp[:, 11:12])
                    nc.scalar.activation(out=olnt[:, 512:1024], in_=psb[3][:, :], func=AF.Identity,
                                         scale=colp[:, 10:11], bias=colp[:, 11:12]).then_inc(hS, 1)
                    scalar.wait_ge(hT, G_t(b, 6))
                    nc.scalar.activation(out=gsig[:, 0:512], in_=psb[4][:, :], func=AF.Sigmoid)
                    nc.scalar.activation(out=gsig[:, 512:1024], in_=psb[5][:, :],
                                         func=AF.Sigmoid).then_inc(hS, 1)

            @block.tensor
            def _(tensor):
                for b in range(NB):
                    tensor.wait_ge(hS, G_s(b, 1))
                    if b >= 1:
                        tensor.wait_ge(hS, G_s(b - 1, 6))
                    for f in range(8):
                        mm = tensor.transpose(psb[4 + f // 4][:, (f % 4) * P:(f % 4 + 1) * P],
                                              xlnc[:, f * P:(f + 1) * P], id_f[:])
                    mm.then_inc(hT, 1)
                    tensor.wait_ge(hV, G_v(b, 2))
                    if b >= 1:
                        tensor.wait_ge(hV, G_v(b - 1, 10))
                    for f in range(8):
                        st_ = (f == 0)
                        o = f * 3072
                        tensor.matmul(psb[0][:, :], xlnT[:, f * P:(f + 1) * P], qkvw[:, o:o + 512], start=st_, stop=False)
                        tensor.matmul(psb[1][:, :], xlnT[:, f * P:(f + 1) * P], qkvw[:, o + 512:o + 1024], start=st_, stop=False)
                    tensor.matmul(psb[0][:, :], onesb[:], brow[0:1, 0:512], start=False, stop=True)
                    mm = tensor.matmul(psb[1][:, :], onesb[:], brow[0:1, 512:1024], start=False, stop=True)
                    mm.then_inc(hT, 1)
                    tensor.wait_ge(hS, G_s(b, 2))
                    if b >= 1:
                        tensor.wait_ge(hS, G_s(b - 1, 5))
                    for h in range(16):
                        mm = tensor.transpose(psb[2 + h // 4][0:64, (h % 4) * P:(h % 4 + 1) * P],
                                              qsil[:, h * 64:(h + 1) * 64], id_f[:])
                    mm.then_inc(hT, 1)
                    tensor.wait_ge(hV, G_v(b, 3))
                    if b >= 1:
                        tensor.wait_ge(hS, G_s(b - 1, 5))
                    for h in range(16):
                        pb_ = psb[2 + h // 8]
                        c0 = (h % 8) * 64
                        lhs = qfmT[0:64, h * P:(h + 1) * P]
                        tensor.matmul(pb_[:, c0:c0 + 64], lhs,
                                      kvnb[0:64, h * 64:(h + 1) * 64],
                                      start=True, stop=True)
                    for f in range(8):
                        st_ = (f == 0)
                        tensor.matmul(psb[6][:, 0:64], xlnT[:, f * P:(f + 1) * P],
                                      wg1w[:, f * 64:(f + 1) * 64], start=st_, stop=False)
                    mm = tensor.matmul(psb[6][:, 0:64], onesb[:], bg1r[0:1, 0:64], start=False, stop=True)
                    mm.then_inc(hT, 1)
                    tensor.wait_ge(hV, G_v(b, 4))
                    tensor.transpose(psb[7][0:64, 0:P], ght[:], id_f[:]).then_inc(hT, 1)
                    tensor.wait_ge(hV, G_v(b, 7))
                    tensor.matmul(psb[4][:, :], ghTb[:], wg2w[:, 0:512], start=True, stop=False)
                    tensor.matmul(psb[5][:, :], ghTb[:], wg2w[:, 512:1024], start=True, stop=False)
                    tensor.matmul(psb[4][:, :], onesb[:], bg2r[0:1, 0:512], start=False, stop=True)
                    mm = tensor.matmul(psb[5][:, :], onesb[:], bg2r[0:1, 512:1024], start=False, stop=True)
                    mm.then_inc(hT, 1)
                    tensor.wait_ge(hV, G_v(b, 8))
                    for f in range(8):
                        mm = tensor.transpose(psb[6 + f // 4][:, (f % 4) * P:(f % 4 + 1) * P],
                                              gout[:, f * P:(f + 1) * P], id_f[:])
                    mm.then_inc(hT, 1)
                    tensor.wait_ge(hV, G_v(b, 9))
                    for f in range(8):
                        st_ = (f == 0)
                        o = f * 1024
                        tensor.matmul(psb[0][:, :], goutT[:, f * P:(f + 1) * P], woutw[:, o:o + 512], start=st_, stop=False)
                        tensor.matmul(psb[1][:, :], goutT[:, f * P:(f + 1) * P], woutw[:, o + 512:o + 1024], start=st_, stop=False)
                    tensor.matmul(psb[0][:, :], onesb[:], botr[0:1, 0:512], start=False, stop=True)
                    mm = tensor.matmul(psb[1][:, :], onesb[:], botr[0:1, 512:1024], start=False, stop=True)
                    mm.then_inc(hT, 1)

            BIN = b_in + KIN2 * NB
            BOUT = b_out + KOUT2 * NB
            BT = b_t + KT2 * NB
            BV = b_v + KV2 * NB
            BS = b_s + KS2 * NB
            cctx.close()

        if "C" in phases:
            emit_attnb(io["C"][0], io["C"][1])
        if "D" in phases:
            emit_ffn("fd", io["D"][0], io["D"][1], use_stash=("B" in phases))

        @block.sync
        def _(sync):
            sync.wait_ge(hOUT, BOUT)

    return nc



def _host_tables(n_blk):
    p = np.arange(P, dtype=np.float32)
    blk = np.arange(n_blk, dtype=np.float32)
    ipos1 = (blk[None, :] * P + p[:, None] + 1.0).astype(np.float32)
    posidx = (blk[None, :] * P + p[:, None]).astype(np.float32)
    jrev = (127.0 - p)[:, None].astype(np.float32)
    ip1 = (p + 1.0)[:, None].astype(np.float32)
    pairidx = np.repeat(np.arange(64, dtype=np.float32), 2)[None, :]
    i_idx = np.arange(P, dtype=np.float32)
    diffT = (i_idx[None, :] - i_idx[:, None]).astype(np.float32)
    causT = (i_idx[None, :] >= i_idx[:, None]).astype(np.float32)
    idf = np.eye(P, dtype=np.float32)
    return dict(ipos1=ipos1, posidx=posidx, jrev=jrev, ip1=ip1, pairidx=pairidx,
                diffT=diffT, causT=causT, idf=idf)


def make_inputs_for_core(core_inputs, n_blk=32):
    t = _host_tables(n_blk)
    c = lambda a: np.ascontiguousarray(a, dtype=np.float32)
    m = {
        "x": c(core_inputs["x"][: n_blk * P]),
        "wq": c(core_inputs["wq"]), "wk": c(core_inputs["wk"]),
        "wv": c(core_inputs["wv"]), "wg": c(core_inputs["wg"]), "wo": c(core_inputs["wo"]),
        "wgate": c(core_inputs["w_gate"].T), "wup": c(core_inputs["w_up"].T),
        "wdown": c(core_inputs["w_down"].T),
        "wqkv": c(core_inputs["w_qkv"].T), "bqkv": c(core_inputs["b_qkv"][None, :]),
        "wg1": c(core_inputs["w_g1"].T), "bg1": c(core_inputs["b_g1"][None, :]),
        "wg2": c(core_inputs["w_g2"].T), "bg2": c(core_inputs["b_g2"][None, :]),
        "wout": c(core_inputs["w_out"].T), "bout": c(core_inputs["b_out"][None, :]),
        "lng": c(core_inputs["ln_g"][None, :]), "lnb": c(core_inputs["ln_b"][None, :]),
    }
    m.update({k: np.ascontiguousarray(v) for k, v in t.items()})
    return m


def kernel(L0_attn_a_x, L0_attn_a_wq, L0_attn_a_wk, L0_attn_a_wv, L0_attn_a_wg,
           L0_attn_a_wo, L0_attn_b_w_qkv, L0_attn_b_b_qkv, L0_attn_b_w_g1,
           L0_attn_b_b_g1, L0_attn_b_w_g2, L0_attn_b_b_g2, L0_attn_b_w_out,
           L0_attn_b_b_out, L0_attn_b_ln_g, L0_attn_b_ln_b, L0_ffn_w_gate,
           L0_ffn_w_up, L0_ffn_w_down):
    full = dict(
        wq=L0_attn_a_wq, wk=L0_attn_a_wk, wv=L0_attn_a_wv, wg=L0_attn_a_wg,
        wo=L0_attn_a_wo, w_qkv=L0_attn_b_w_qkv, b_qkv=L0_attn_b_b_qkv,
        w_g1=L0_attn_b_w_g1, b_g1=L0_attn_b_b_g1, w_g2=L0_attn_b_w_g2,
        b_g2=L0_attn_b_b_g2, w_out=L0_attn_b_w_out, b_out=L0_attn_b_b_out,
        ln_g=L0_attn_b_ln_g, ln_b=L0_attn_b_ln_b, w_gate=L0_ffn_w_gate,
        w_up=L0_ffn_w_up, w_down=L0_ffn_w_down,
    )
    nblk = 32
    nc = build_nc(n_blk=nblk, phases=("A", "B", "C", "D"))
    in_maps = []
    for core in range(8):
        ci = dict(full)
        ci["x"] = np.ascontiguousarray(L0_attn_a_x[core])
        in_maps.append(make_inputs_for_core(ci, n_blk=nblk))
    res = run_bass_kernel_spmd(nc, in_maps, core_ids=list(range(8)))
    out = np.stack([np.asarray(r["y"]) for r in res.results], axis=0)
    return out.astype(np.float32)
